# version 19
# baseline (speedup 1.0000x reference)
import numpy as np
import ml_dtypes
import concourse.bass as bass
import concourse.mybir as mybir
from concourse.bass_utils import run_bass_kernel_spmd
from contextlib import ExitStack

F32 = mybir.dt.float32
BF16 = mybir.dt.bfloat16
AF = mybir.ActivationFunctionType
ALU = mybir.AluOpType
AX = mybir.AxisListType.X

MAGIC = 1.5 * 2.0 ** 23
LN16 = float(np.log(16.0))
LN7 = float(np.log(7.0))

SEQS = 2
N = 2048
D = 1024
C = 128
TPS = N // C
TILES = SEQS * TPS
HD = 4
DV = 512


def build_program(debug=False):
    nc = bass.Bass("TRN2", target_bir_lowering=False)

    x_d = nc.dram_tensor("x", [SEQS, N, D], F32, kind="ExternalInput")
    wq_d = nc.dram_tensor("wq", [D, D], F32, kind="ExternalInput")
    wk_d = nc.dram_tensor("wk", [D, D], F32, kind="ExternalInput")
    wv_d = nc.dram_tensor("wv", [D, 2048], F32, kind="ExternalInput")
    wg_d = nc.dram_tensor("wg", [D, 2048], F32, kind="ExternalInput")
    wgk1_d = nc.dram_tensor("wgk1", [D, 16], F32, kind="ExternalInput")
    wgk2_d = nc.dram_tensor("wgk2", [16, D], F32, kind="ExternalInput")
    wo_d = nc.dram_tensor("wo", [2048, D], F32, kind="ExternalInput")
    wffn_d = nc.dram_tensor("wffn", [D, D], F32, kind="ExternalInput")
    bgk2c_d = nc.dram_tensor("bgk2c", [128, 8], F32, kind="ExternalInput")
    gnwb_d = nc.dram_tensor("gnwb", [128, 2048], F32, kind="ExternalInput")
    hmat_d = nc.dram_tensor("hmat", [D, D], BF16, kind="ExternalInput")
    idn_d = nc.dram_tensor("idn", [128, 128], BF16, kind="ExternalInput")
    mask_d = nc.dram_tensor("maskT", [128, 128], BF16, kind="ExternalInput")
    out_d = nc.dram_tensor("out", [SEQS, N, D], F32, kind="ExternalOutput")
    xns_d = nc.dram_tensor("xnscr", [TILES, 128, D], BF16, kind="Internal")

    S_ = {"tensor": [], "vector": [], "scalar": [], "sync": [], "gpsimd": []}

    def E(eng, method, *args, inc=None, **kw):
        if inc is not None:
            sem, k = inc
            sem.n += k
            def f(e, m=method, a=args, kv=kw, sh=sem.h, kk=k):
                getattr(e, m)(*a, **kv).then_inc(sh, kk)
        else:
            def f(e, m=method, a=args, kv=kw):
                getattr(e, m)(*a, **kv)
        S_[eng].append(f)

    class Sem:
        def __init__(self, stack, name):
            self.h = stack.enter_context(nc.semaphore(name))
            self.n = 0

    def wait(eng, sem, val):
        if val and val > 0:
            S_[eng].append(lambda e, s=sem.h, v=val: e.wait_ge(s, v))

    def wait_t(eng, tok):
        if tok is not None:
            wait(eng, tok[0], tok[1])

    dbg_tensors = {}
    dbg_state = {"sem": None}

    def tap(name, ap, sem, val, blocker=None):
        if not debug:
            return
        sh = list(ap.shape) if hasattr(ap, "shape") else None
        t = nc.dram_tensor("dbg_" + name, sh, ap.dtype, kind="ExternalOutput")
        dbg_tensors[name] = t
        dsem = dbg_state["sem"]
        wait("gpsimd", sem, val)
        E("gpsimd", "dma_start", t[:, :] if len(sh) == 2 else t[:], ap, inc=(dsem, 16))
        if blocker:
            wait(blocker, dsem, dsem.n)

    with ExitStack() as top:
        xs = [Sem(top, "xs0"), Sem(top, "xs1")]
        ws2 = [Sem(top, "ws0"), Sem(top, "ws1")]
        csem = Sem(top, "csem")
        ss = Sem(top, "ss")
        osA = Sem(top, "osA"); osB = Sem(top, "osB")
        bxn = [Sem(top, "bxn0"), Sem(top, "bxn1")]
        bor = [Sem(top, "bor0"), Sem(top, "bor1")]
        wfp = [Sem(top, "wfp0"), Sem(top, "wfp1")]
        hmS = Sem(top, "hmS")
        ts = Sem(top, "ts"); vs = Sem(top, "vs"); as_ = Sem(top, "as")
        gs = Sem(top, "gs")
        dbg_state["sem"] = Sem(top, "dbg")

        p_tr0 = top.enter_context(nc.psum_tensor("p_tr0", [128, 1024], BF16))
        p_tr1 = top.enter_context(nc.psum_tensor("p_tr1", [128, 1024], BF16))
        p_pa = top.enter_context(nc.psum_tensor("p_pa", [128, 512], F32))
        p_pb = top.enter_context(nc.psum_tensor("p_pb", [128, 512], F32))
        p_sc = top.enter_context(nc.psum_tensor("p_sc", [128, 512], F32))
        p_o0 = top.enter_context(nc.psum_tensor("p_o0", [128, 512], F32))
        p_o1 = top.enter_context(nc.psum_tensor("p_o1", [128, 512], F32))
        p_ds = top.enter_context(nc.psum_tensor("p_ds", [128, 512], F32))
        p_tr = [p_tr0, p_tr1]
        p_o = [p_o0, p_o1]

        def sb(st, name, shape, dt):
            return st.enter_context(nc.sbuf_tensor(name, shape, dt))

        Ssb = sb(top, "S", [128, HD * 2 * DV], F32)
        gnwb = sb(top, "gnwb_s", [128, 2048], BF16)
        negb = sb(top, "negb", [128, 8], F32)
        idn = sb(top, "idn_s", [128, 128], BF16)
        maskT = sb(top, "mask_s", [128, 128], BF16)
        zbf = sb(top, "zbf", [128, 128], BF16)
        padb = sb(top, "padb", [128, 128], BF16)
        pada = sb(top, "pada", [128, 128], BF16)
        cst = sb(top, "cst", [128, 3], F32)
        xb = [sb(top, "xb0", [128, D], F32), sb(top, "xb1", [128, D], F32)]
        out_sb = sb(top, "out_sb", [128, D], F32)

        phA = ExitStack()
        wq = sb(phA, "wq_s", [128, 8192], BF16)
        wk = sb(phA, "wk_s", [128, 8192], BF16)
        wv = sb(phA, "wv_s", [128, 16384], BF16)
        wg = sb(phA, "wg_s", [128, 16384], BF16)
        wo = sb(phA, "wo_s", [128, 16384], BF16)
        wgk1 = sb(phA, "wgk1_s", [128, 128], BF16)
        wgk2 = sb(phA, "wgk2_s", [16, 1024], BF16)

        stg_ctx = ExitStack()
        stg = [sb(stg_ctx, "stg0", [128, 2048], F32), sb(stg_ctx, "stg1", [128, 2048], F32)]

        for dst, src in ((idn, idn_d), (maskT, mask_d), (negb, bgk2c_d)):
            E("sync", "dma_start", dst[:, :], src[:, :], inc=(csem, 16))
        const_tok = csem.n
        E("sync", "dma_start", xb[0][:, :], x_d[0, 0:C, :], inc=(xs[0], 16))
        xtok = {0: (xs[0], xs[0].n)}

        chunks = []
        for dc in range(8):
            chunks.append((wq_d[dc * 128:(dc + 1) * 128, :], wq[:, dc * 1024:(dc + 1) * 1024], 128, 1024))
        for dc in range(8):
            chunks.append((wk_d[dc * 128:(dc + 1) * 128, :], wk[:, dc * 1024:(dc + 1) * 1024], 128, 1024))
        for dc in range(8):
            chunks.append((wgk1_d[dc * 128:(dc + 1) * 128, :], wgk1[:, dc * 16:(dc + 1) * 16], 128, 16))
        chunks.append((wgk2_d[:, :], wgk2[:, :], 16, 1024))
        chunks.append((gnwb_d[:, :], gnwb[:, :], 128, 2048))
        for dc in range(8):
            chunks.append((wv_d[dc * 128:(dc + 1) * 128, :], wv[:, dc * 2048:(dc + 1) * 2048], 128, 2048))
        for dc in range(8):
            chunks.append((wg_d[dc * 128:(dc + 1) * 128, :], wg[:, dc * 2048:(dc + 1) * 2048], 128, 2048))
        for jc in range(16):
            chunks.append((wo_d[jc * 128:(jc + 1) * 128, :], wo[:, jc * 1024:(jc + 1) * 1024], 128, 1024))

        cast_tok = {}
        for i, (dap, sap, r, cc_) in enumerate(chunks):
            if i >= 2:
                wait("sync", vs, cast_tok[i - 2])
            E("sync", "dma_start", stg[i % 2][0:r, 0:cc_], dap, inc=(ws2[i % 2], 16))
            wait("vector", ws2[i % 2], ws2[i % 2].n)
            E("vector", "tensor_copy", sap, stg[i % 2][0:r, 0:cc_], inc=(vs, 1))
            cast_tok[i] = vs.n
        wcast = vs.n
        wait("vector", csem, const_tok)
        E("vector", "tensor_scalar_mul", negb[:, :], negb[:, :], -1.0)
        E("vector", "memset", zbf[:, :], 0.0)
        E("vector", "memset", cst[:, 0:1], 1e-5)
        E("vector", "memset", cst[:, 1:2], -LN16)
        E("vector", "memset", cst[:, 2:3], LN7)
        E("vector", "memset", Ssb[:, :], 0.0)

        stg_ctx.close()

        xn_tm = sb(phA, "xn_tm", [128, D], BF16)
        xn_fm = sb(phA, "xn_fm", [128, D], BF16)
        q_fm = sb(phA, "q_fm", [128, D], BF16)
        k_fm = sb(phA, "k_fm", [128, D], BF16)
        x1_sb = sb(phA, "x1_sb", [16, 128], BF16)
        D_bf = sb(phA, "D_bf", [128, D], BF16)
        EqS = sb(phA, "EqS", [128, D], BF16)
        EkS = sb(phA, "EkS", [128, D], BF16)
        v_tm = sb(phA, "v_tm", [128, 2048], BF16)
        g_tm = sb(phA, "g_tm", [128, 2048], BF16)
        sgA = sb(phA, "sgA", [128, 2048], BF16)
        sgB = sb(phA, "sgB", [128, 512], BF16)
        sbf = sb(phA, "sbf", [128, 2048], BF16)
        shat = sb(phA, "shat", [128, 256], BF16)
        khtm = sb(phA, "khtm", [128, 1024], BF16)
        ggb = sb(phA, "ggb", [128, 2048], BF16)
        otld = sb(phA, "otld", [128, 1024], BF16)
        o_fm = sb(phA, "o_fm", [128, 2048], BF16)
        xsum = sb(phA, "xsum", [128, 1], F32)
        xsq = sb(phA, "xsq", [128, 1], F32)
        meanv = sb(phA, "meanv", [128, 1], F32)
        varv = sb(phA, "varv", [128, 1], F32)
        rstdln = sb(phA, "rstdln", [128, 1], F32)
        tmp1 = sb(phA, "tmp1", [128, 1], F32)
        cendv = sb(phA, "cendv", [128, 8], F32)
        lamv = sb(phA, "lamv", [128, 8], F32)
        ssumv = sb(phA, "ssumv", [128, 4], F32)
        rstdv = sb(phA, "rstdv", [128, 4], F32)
        remA = nc.sbuf_bytes_remaining
        print("SBUF remA:", remA)

        tr_i = 0
        trg_ev = [None, None]
        VEND = AEND = TEND = 0
        stored_prev = 0
        outdma_prev = 0
        xtm_tok = 0
        zhi_prev = 0
        eqek_prev = 0
        outev_prev = 0
        shat_h3_prev = 0
        supd_h3_prev = 0
        gev_pb_prev = 0
        qt_prev = 0
        fold_half_prev = [0, 0]
        fold_toks = [0, 0]
        lnstats_tok = 0
        rstdln_tok = 0
        lnvar_tok = 0
        sc_tok = {}; shat_tok_d = {}; supd_tok_d = {}; ofm_tok_d = {}
        oh_tok = {}; ds_tok = {}; sq_tok = {}; og_tok = {}; oT_tok = {}
        ocp_tok = {}; sqr_tok = {}; scast_tok = {}

        def tr_group():
            nonlocal tr_i
            g = tr_i % 2
            tr_i += 1
            return g

        def emit_gate(n, h, fold_tok):
            if h < 0:
                return
            hp = h % 2
            kp2 = (n, h - 2) if h >= 2 else (n - 1, h + 2)
            wait("gpsimd", vs, ocp_tok[(n, h)])
            wait("gpsimd", ts, oT_tok.get(kp2, 0))
            E("gpsimd", "tensor_tensor", otld[:, hp * 512:(hp + 1) * 512],
              g_tm[:, hp * 512:(hp + 1) * 512], ggb[:, h * 512:(h + 1) * 512],
              ALU.mult)
            wait("gpsimd", as_, sq_tok[(n, h)])
            E("gpsimd", "tensor_scalar", otld[:, hp * 512:(hp + 1) * 512],
              otld[:, hp * 512:(hp + 1) * 512], rstdv[:, h:h + 1], None,
              ALU.mult, ALU.bypass, inc=(gs, 1))
            og_tok[(n, h)] = gs.n

        oT_grp = {}

        def emit_oT_pe(n, h):
            if h < 0:
                return
            hp = h % 2
            g2 = tr_group()
            oT_grp[(n, h)] = g2
            wait_t("tensor", trg_ev[g2])
            wait("tensor", gs, og_tok[(n, h)])
            for jj in range(4):
                icc = (ts, 1) if jj == 3 else None
                E("tensor", "transpose", p_tr[g2][:, jj * 128:(jj + 1) * 128],
                  otld[:, hp * 512 + jj * 128: hp * 512 + (jj + 1) * 128], idn[:, :], inc=icc)
            oT_tok[(n, h)] = ts.n

        def emit_ofm(n, h):
            if h < 0:
                return
            g2 = oT_grp[(n, h)]
            wait("scalar", ts, oT_tok[(n, h)])
            E("scalar", "copy", o_fm[:, h * 512:(h + 1) * 512], p_tr[g2][:, 0:512], inc=(as_, 1))
            ofm_tok_d[(n, h)] = as_.n
            trg_ev[g2] = (as_, as_.n)

        def emit_ln_stats(tile):
            nonlocal lnstats_tok
            par = tile % 2
            wait_t("vector", xtok[tile])
            wait("vector", osA, outdma_prev)
            E("vector", "tensor_scalar", out_sb[:, :], xb[par][:, :], 1.0, None,
              ALU.mult, ALU.add, accum_out=xsum[:, :])
            E("vector", "scalar_tensor_tensor", out_sb[:, :], xb[par][:, :], 1.0,
              xb[par][:, :], ALU.mult, ALU.mult, accum_out=xsq[:, :], inc=(vs, 1))
            lnstats_tok = vs.n

        def emit_ln_rstd():
            nonlocal rstdln_tok, lnvar_tok
            wait("scalar", vs, lnstats_tok)
            E("scalar", "activation", tmp1[:, :], xsum[:, :], AF.Square, bias=0.0,
              scale=1.0 / D, inc=(as_, 1))
            t_msq = as_.n
            wait("vector", as_, t_msq)
            E("vector", "tensor_scalar_mul", meanv[:, :], xsum[:, :], 1.0 / D)
            E("vector", "scalar_tensor_tensor", varv[:, :], xsq[:, :], 1.0 / D,
              tmp1[:, :], ALU.mult, ALU.subtract, inc=(vs, 1))
            lnvar_tok = vs.n
            wait("scalar", vs, lnvar_tok)
            E("scalar", "activation", rstdln[:, :], varv[:, :], AF.Ln, bias=cst[:, 0:1],
              scale=1.0, inc=(as_, 1))
            wait("scalar", as_, as_.n)
            E("scalar", "activation", rstdln[:, :], rstdln[:, :], AF.Exp, bias=0.0,
              scale=-0.5, inc=(as_, 1))
            rstdln_tok = as_.n

        def emit_ln_apply(tile):
            nonlocal xtm_tok
            par = tile % 2
            wait("vector", as_, rstdln_tok)
            E("vector", "tensor_scalar", xn_tm[:, :], xb[par][:, :], meanv[:, :],
              rstdln[:, :], ALU.subtract, ALU.mult, inc=(vs, 1))
            xtm_tok = vs.n

        emit_ln_stats(0)
        emit_ln_rstd()
        emit_ln_apply(0)

        for n in range(TILES):
            seq, r0 = n // TPS, (n % TPS) * C
            par = n % 2
            pn = n - 1

            if n == TPS:
                E("vector", "memset", Ssb[:, :], 0.0)

            if n + 1 < TILES:
                sq1, rr1 = (n + 1) // TPS, ((n + 1) % TPS) * C
                wait("sync", vs, outev_prev)
                xp = (n + 1) % 2
                E("sync", "dma_start", xb[xp][:, :], x_d[sq1, rr1:rr1 + C, :], inc=(xs[xp], 16))
                xtok[n + 1] = (xs[xp], xs[xp].n)

            g0 = tr_group()
            wait_t("tensor", trg_ev[g0])
            wait("tensor", vs, xtm_tok)
            for kk in range(8):
                icc = (ts, 1) if kk == 7 else None
                E("tensor", "transpose", p_tr[g0][:, kk * 128:(kk + 1) * 128],
                  xn_tm[:, kk * 128:(kk + 1) * 128], idn[:, :], inc=icc)
            xnT_tok = ts.n

            wait("scalar", ts, xnT_tok)
            wait("scalar", ss, stored_prev)
            E("scalar", "copy", xn_fm[:, :], p_tr[g0][:, :], inc=(as_, 1))
            xfm_tok = as_.n
            trg_ev[g0] = (as_, as_.n)

            wait("gpsimd", as_, xfm_tok)
            E("gpsimd", "dma_start", xns_d[n, :, :], xn_fm[:, :], inc=(ss, 16))
            stored_n = ss.n

            qk_banks = [p_sc, p_ds, p_pa, p_pb]
            qk_free = [shat_h3_prev, supd_h3_prev, supd_h3_prev, gev_pb_prev]
            evtoks = {}
            x1ev_tok = 0
            wait("tensor", as_, xfm_tok)
            if n == 0:
                wait("tensor", vs, wcast)
                tap("wq", wq[:, :], vs, wcast)
                tap("wk", wk[:, :], vs, wcast)
                tap("wv", wv[:, :], vs, wcast)
                tap("wg", wg[:, :], vs, wcast)
                tap("wo", wo[:, :], vs, wcast)
                tap("gnwbs", gnwb[:, :], vs, wcast)
            for gi, (name, wsb, dst, off, fcs) in enumerate((
                    ("qA", wq, q_fm, 0, range(0, 4)),
                    ("qB", wq, q_fm, 512, range(4, 8)),
                    ("kA", wk, k_fm, 0, range(0, 4)),
                    ("kB", wk, k_fm, 512, range(4, 8)))):
                bank = qk_banks[gi]
                wait("tensor", vs, qk_free[gi])
                for fi, fc in enumerate(fcs):
                    for dc in range(8):
                        icc = (ts, 1) if (fi == 3 and dc == 7) else None
                        E("tensor", "matmul", bank[:, (fc % 4) * 128:(fc % 4 + 1) * 128],
                          wsb[:, dc * 1024 + fc * 128: dc * 1024 + (fc + 1) * 128],
                          xn_fm[:, dc * 128:(dc + 1) * 128],
                          start=(dc == 0), stop=(dc == 7), inc=icc)
                wait("scalar", ts, ts.n)
                if gi == 0:
                    wait("scalar", ts, oh_tok.get((pn, 3), 0))
                E("scalar", "copy", dst[:, off:off + 512], bank[:, :], inc=(as_, 1))
                evtoks[name] = as_.n
                if gi == 0:
                    wait("tensor", vs, outev_prev)
                    for dc in range(8):
                        icc = (ts, 1) if dc == 7 else None
                        E("tensor", "matmul", p_o0[0:16, 0:128], wgk1[:, dc * 16:(dc + 1) * 16],
                          xn_fm[:, dc * 128:(dc + 1) * 128], start=(dc == 0), stop=(dc == 7), inc=icc)
                    x1_tok = ts.n
                    wait("scalar", ts, x1_tok)
                    wait("scalar", ts, zhi_prev)
                    E("scalar", "copy", x1_sb[:, :], p_o0[0:16, 0:128], inc=(as_, 1))
                    x1ev_tok = as_.n
                if gi == 1:
                    wait("tensor", as_, x1ev_tok)
                    for fc in range(8):
                        bank2 = p_o0 if fc < 4 else p_o1
                        icc = (ts, 1) if fc == 7 else None
                        E("tensor", "matmul", bank2[:, (fc % 4) * 128:(fc % 4 + 1) * 128],
                          wgk2[:, fc * 128:(fc + 1) * 128], x1_sb[:, :], start=True, stop=True, inc=icc)
                    z_tok = ts.n
                    zhi_prev = z_tok

            wait("scalar", ts, z_tok)
            wait("scalar", vs, qt_prev)
            for ch in range(8):
                bank = p_o0 if ch < 4 else p_o1
                E("scalar", "activation", EkS[:, ch * 128:(ch + 1) * 128],
                  bank[:, (ch % 4) * 128:(ch % 4 + 1) * 128], AF.Exp,
                  bias=negb[:, ch:ch + 1], scale=-1.0)
            E("scalar", "activation", D_bf[:, :], EkS[:, :], AF.Ln, bias=1.0, scale=1.0, inc=(as_, 1))
            gates_tok = as_.n

            vev_toks = [0] * 4
            gev_toks = [0] * 4
            for gi in range(8):
                wsb = wv if gi < 4 else wg
                grp = gi % 4
                bank = qk_banks[grp]
                if gi < 4:
                    wait("tensor", as_, evtoks[("qA", "qB", "kA", "kB")[gi]])
                else:
                    wait("tensor", vs, vev_toks[grp])
                for dc in range(8):
                    icc = (ts, 1) if dc == 7 else None
                    E("tensor", "matmul", bank[:, :], xn_fm[:, dc * 128:(dc + 1) * 128],
                      wsb[:, dc * 2048 + grp * 512: dc * 2048 + (grp + 1) * 512],
                      start=(dc == 0), stop=(dc == 7), inc=icc)
                vg_w = ts.n
                if gi < 4:
                    wait("vector", ts, vg_w)
                    if gi == 0:
                        wait("vector", ts, ds_tok.get((pn, 3), 0))
                    E("vector", "tensor_copy", v_tm[:, grp * 512:(grp + 1) * 512], bank[:, :], inc=(vs, 1))
                    vev_toks[grp] = vs.n
                else:
                    wait("vector", ts, vg_w)
                    if gi == 4:
                        wait("vector", as_, sq_tok.get((pn, 3), 0))
                        wait("vector", gs, fold_half_prev[0])
                    if gi == 6:
                        wait("vector", gs, fold_half_prev[1])
                    E("vector", "tensor_copy", g_tm[:, grp * 512:(grp + 1) * 512], bank[:, :], inc=(vs, 1))
                    gev_toks[grp] = vs.n
                    if gi in (5, 7):
                        hh = 0 if gi == 5 else 1
                        lo = hh * 1024
                        wait("scalar", vs, gev_toks[grp])
                        wait("scalar", gs, fold_half_prev[hh])
                        E("scalar", "activation", sgA[:, lo:lo + 1024],
                          g_tm[:, lo:lo + 1024], AF.Exp, bias=0.0, scale=-1.0)
                        E("scalar", "activation", sgA[:, lo:lo + 1024],
                          sgA[:, lo:lo + 1024], AF.Ln, bias=1.0, scale=1.0)
                        E("scalar", "activation", sgA[:, lo:lo + 1024],
                          sgA[:, lo:lo + 1024], AF.Exp, bias=0.0, scale=-1.0, inc=(as_, 1))
                        sig_half = as_.n
                        wait("gpsimd", vs, gev_toks[grp])
                        E("gpsimd", "tensor_tensor", ggb[:, lo:lo + 1024],
                          g_tm[:, lo:lo + 1024], gnwb[:, lo:lo + 1024], ALU.mult)
                        wait("gpsimd", as_, sig_half)
                        E("gpsimd", "tensor_tensor", ggb[:, lo:lo + 1024],
                          ggb[:, lo:lo + 1024], sgA[:, lo:lo + 1024], ALU.mult, inc=(gs, 1))
                        fold_toks[hh] = gs.n
                        fold_half_prev[hh] = gs.n
                if gi == 1:
                    wait("vector", as_, gates_tok)
                    wait("vector", as_, eqek_prev)
                    for ch in range(8):
                        icc = (vs, 1) if ch == 7 else None
                        E("vector", "tensor_tensor_scan", EqS[:, ch * 128:(ch + 1) * 128],
                          D_bf[:, ch * 128:(ch + 1) * 128], zbf[:, :], 0.0, ALU.add, ALU.add, inc=icc)
                    for ch in range(8):
                        icc = (vs, 1) if ch == 7 else None
                        E("vector", "tensor_copy", cendv[:, ch:ch + 1],
                          EqS[:, ch * 128 + 127: ch * 128 + 128], inc=icc)
                    scan_tok = vs.n
                    wait("scalar", vs, scan_tok)
                    E("scalar", "activation", lamv[:, :], cendv[:, :], AF.Exp, bias=0.0, scale=-1.0 / 16.0)
                    E("scalar", "activation", D_bf[:, :], EqS[:, :], AF.Exp, bias=cst[:, 1:2], scale=-1.0 / 16.0)
                    E("scalar", "activation", EkS[:, :], EqS[:, :], AF.Exp, bias=0.0, scale=1.0 / 16.0, inc=(as_, 1))
                    eqek_tok = as_.n
                    eqek_prev = eqek_tok
                if gi == 3:
                    wait("vector", ts, oh_tok.get((pn, 3), 0))
                    E("vector", "tensor_copy", sbf[:, 0:1024], Ssb[:, 0:1024])
                    E("vector", "tensor_copy", sbf[:, 1024:2048], Ssb[:, 1024:2048], inc=(vs, 1))
                    sbf_toks = {0: (vs, vs.n), 1: (vs, vs.n)}
                    wait("vector", as_, eqek_tok)
                    E("vector", "tensor_mul", q_fm[:, :], q_fm[:, :], D_bf[:, :])
                    E("vector", "tensor_mul", k_fm[:, :], k_fm[:, :], EkS[:, :])
                    for ch in range(8):
                        icc = (vs, 1) if ch == 7 else None
                        E("vector", "tensor_scalar_mul", D_bf[:, ch * 128:(ch + 1) * 128],
                          k_fm[:, ch * 128:(ch + 1) * 128], lamv[:, ch:ch + 1], inc=icc)
                    qt_tok = vs.n
                    if n == 0:
                        tap("lamv0", lamv[:, :], vs, qt_tok, blocker="scalar")
                        tap("cendv0", cendv[:, :], vs, qt_tok, blocker="scalar")
                        tap("kt0", k_fm[:, :], vs, qt_tok, blocker="tensor")
                    if n == 1:
                        tap("Ssb1", Ssb[:, :], vs, sbf_toks[0][1], blocker="vector")
                        tap("sbf1", sbf[:, :], vs, sbf_toks[0][1], blocker="vector")
                        tap("qt1", q_fm[:, :], vs, qt_tok, blocker="tensor")
                        tap("kt1", k_fm[:, :], vs, qt_tok, blocker="tensor")
                        tap("lamv1", lamv[:, :], vs, qt_tok, blocker="scalar")
                        tap("cum1", EqS[:, :], vs, qt_tok, blocker="scalar")
            sc_free = gev_toks[0]
            supd_last = supd_h3_prev
            khtm_tok = 0
            for h in range(HD):
                hp = h % 2
                c0, c1 = 2 * h, 2 * h + 1
                kprev = (n, h - 2) if h >= 2 else (pn, h + 2)

                wait("tensor", vs, qt_tok)
                wait("tensor", vs, sc_free)
                for ic in range(2):
                    cc_ = (c0, c1)[ic]
                    icc = (ts, 1) if ic == 1 else None
                    E("tensor", "matmul", p_sc[:, 0:128], k_fm[:, cc_ * 128:(cc_ + 1) * 128],
                      q_fm[:, cc_ * 128:(cc_ + 1) * 128], start=(ic == 0), stop=(ic == 1), inc=icc)
                sc_tok[(n, h)] = ts.n

                if h == 0:
                    g1 = tr_group()
                    wait_t("tensor", trg_ev[g1])
                    for cc_ in range(8):
                        icc = (ts, 1) if cc_ == 7 else None
                        E("tensor", "transpose", p_tr[g1][:, cc_ * 128:(cc_ + 1) * 128],
                          D_bf[:, cc_ * 128:(cc_ + 1) * 128], idn[:, :], inc=icc)
                    khT_tok = ts.n

                wait("vector", ts, sc_tok[(n, h)])
                wait("vector", ts, oh_tok.get(kprev, 0))
                E("vector", "tensor_tensor", shat[:, hp * 128:(hp + 1) * 128],
                  p_sc[:, 0:128], maskT[:, :], ALU.mult, inc=(vs, 1))
                shat_tok_d[(n, h)] = vs.n
                sc_free = vs.n
                if n == 1 and h == 0:
                    tap("shat1", shat[:, 0:128], vs, vs.n, blocker="tensor")
                if h == 0:
                    wait("vector", ts, khT_tok)
                    E("vector", "tensor_copy", khtm[:, :], p_tr[g1][:, :], inc=(vs, 1))
                    khtm_tok = vs.n
                    trg_ev[g1] = (vs, vs.n)
                    if n == 0:
                        tap("khtm0", khtm[:, :], vs, khtm_tok, blocker="tensor")

                wait_t("tensor", sbf_toks[hp])
                if h < 2:
                    wait("tensor", as_, gates_tok)
                wait("tensor", vs, ocp_tok.get(kprev, 0))
                for ic in range(2):
                    cc_ = (c0, c1)[ic]
                    E("tensor", "matmul", p_o[hp][:, :], q_fm[:, cc_ * 128:(cc_ + 1) * 128],
                      sbf[:, hp * 1024 + ic * 512: hp * 1024 + (ic + 1) * 512],
                      start=(ic == 0), stop=False)
                wait("tensor", vs, shat_tok_d[(n, h)])
                E("tensor", "matmul", p_o[hp][:, :], shat[:, hp * 128:(hp + 1) * 128],
                  v_tm[:, h * 512:(h + 1) * 512], start=False, stop=True, inc=(ts, 1))
                oh_tok[(n, h)] = ts.n

                wait("vector", ts, oh_tok[(n, h)])
                if h == 0:
                    wait("vector", gs, fold_toks[0])
                if h >= 2:
                    wait("vector", as_, sqr_tok[(n, h - 2)])
                    wait("vector", gs, og_tok[(n, h - 2)])
                E("vector", "tensor_copy", g_tm[:, hp * 512:(hp + 1) * 512],
                  p_o[hp][:, :], inc=(vs, 1))
                ocp_tok[(n, h)] = vs.n

                if h < 2:
                    wait("gpsimd", ts, oh_tok[(n, h)])
                    E("gpsimd", "tensor_copy", sbf[:, hp * 1024:(hp + 1) * 1024],
                      Ssb[:, (h + 2) * 1024:(h + 3) * 1024], inc=(gs, 1))
                    sbf_toks[hp] = (gs, gs.n)
                    scast_tok[(n, h + 2)] = gs.n

                wait("scalar", vs, ocp_tok[(n, h)])
                E("scalar", "activation", sgB[:, :], g_tm[:, hp * 512:(hp + 1) * 512],
                  AF.Square, accum_out=ssumv[:, h:h + 1], inc=(as_, 1))
                sqr_tok[(n, h)] = as_.n
                wait("scalar", as_, as_.n)
                E("scalar", "activation", rstdv[:, h:h + 1], ssumv[:, h:h + 1],
                  AF.Ln, bias=cst[:, 0:1], scale=1.0 / DV, inc=(as_, 1))
                wait("scalar", as_, as_.n)
                E("scalar", "activation", rstdv[:, h:h + 1], rstdv[:, h:h + 1],
                  AF.Exp, bias=0.0, scale=-0.5, inc=(as_, 1))
                sq_tok[(n, h)] = as_.n

                wait("tensor", vs, khtm_tok)
                wait("tensor", vs, supd_last)
                if h == 0:
                    wait("tensor", vs, gev_toks[1])
                    wait("tensor", vs, gev_toks[2])
                E("tensor", "matmul", p_ds[:, :], khtm[:, h * 256: h * 256 + 128],
                  v_tm[:, h * 512:(h + 1) * 512], start=True, stop=True)
                E("tensor", "matmul", p_pa[:, :], khtm[:, h * 256 + 128: h * 256 + 256],
                  v_tm[:, h * 512:(h + 1) * 512], start=True, stop=True, inc=(ts, 1))
                ds_tok[(n, h)] = ts.n

                wait("vector", ts, ds_tok[(n, h)])
                if h >= 2:
                    wait("vector", gs, scast_tok[(n, h)])
                E("vector", "scalar_tensor_tensor", Ssb[:, c0 * 512:(c0 + 1) * 512],
                  Ssb[:, c0 * 512:(c0 + 1) * 512], lamv[:, c0:c0 + 1], p_ds[:, :],
                  ALU.mult, ALU.add)
                E("vector", "scalar_tensor_tensor", Ssb[:, c1 * 512:(c1 + 1) * 512],
                  Ssb[:, c1 * 512:(c1 + 1) * 512], lamv[:, c1:c1 + 1], p_pa[:, :],
                  ALU.mult, ALU.add, inc=(vs, 1))
                supd_tok_d[(n, h)] = vs.n
                supd_last = vs.n

                emit_gate(n, h - 1, fold_toks[0 if h - 1 <= 1 else 1])
                emit_oT_pe(n, h - 2)
                emit_ofm(n, h - 2)

            emit_gate(n, 3, fold_toks[1])
            emit_oT_pe(n, 2)
            emit_ofm(n, 2)
            emit_oT_pe(n, 3)
            emit_ofm(n, 3)
            if n + 1 < TILES:
                emit_ln_stats(n + 1)
                emit_ln_rstd()

            if n + 1 < TILES:
                wait("vector", ts, xnT_tok)
                emit_ln_apply(n + 1)

            wait("tensor", as_, ofm_tok_d[(n, 3)])
            wait("tensor", vs, ocp_tok[(n, 2)])
            for jc in range(16):
                icc = (ts, 1) if jc == 15 else None
                E("tensor", "matmul", p_o0[:, :], o_fm[:, jc * 128:(jc + 1) * 128],
                  wo[:, jc * 1024 + 0: jc * 1024 + 512],
                  start=(jc == 0), stop=(jc == 15), inc=icc)
            wait("tensor", vs, ocp_tok[(n, 3)])
            for jc in range(16):
                icc = (ts, 1) if jc == 15 else None
                E("tensor", "matmul", p_o1[:, :], o_fm[:, jc * 128:(jc + 1) * 128],
                  wo[:, jc * 1024 + 512: jc * 1024 + 1024],
                  start=(jc == 0), stop=(jc == 15), inc=icc)
            ao_tok = ts.n

            wait("vector", ts, ao_tok)
            wait("vector", osA, outdma_prev)
            E("vector", "tensor_tensor", out_sb[:, 0:512], p_o0[:, :], xb[par][:, 0:512], ALU.add)
            E("vector", "tensor_tensor", out_sb[:, 512:1024], p_o1[:, :], xb[par][:, 512:1024],
              ALU.add, inc=(vs, 1))
            outev_tok = vs.n
            if n in (0, 1):
                tap("outA" if n == 0 else "outA1", out_sb[:, :], vs, outev_tok, blocker="gpsimd")

            wait("gpsimd", vs, outev_tok)
            E("gpsimd", "dma_start", out_d[seq, r0:r0 + C, :], out_sb[:, :], inc=(osA, 16))
            outdma_prev = osA.n

            stored_prev = stored_n
            outev_prev = outev_tok
            shat_h3_prev = shat_tok_d[(n, 3)]
            supd_h3_prev = supd_tok_d[(n, 3)]
            gev_pb_prev = gev_toks[3]
            qt_prev = qt_tok
            VEND, AEND, TEND = vs.n, as_.n, ts.n

        phA.close()
        phB = ExitStack()
        hm = sb(phB, "hm_s", [128, 8192], BF16)
        wfT = sb(phB, "wfT", [128, 8192], BF16)
        stgB = [sb(phB, "stgB0", [128, 1024], F32), sb(phB, "stgB1", [128, 1024], F32)]
        xnb = [sb(phB, "xnb0", [128, D], BF16), sb(phB, "xnb1", [128, D], BF16)]
        outrd = [sb(phB, "outrd0", [128, D], F32), sb(phB, "outrd1", [128, D], F32)]
        xr_f = [sb(phB, "xr_f0", [128, D], F32), sb(phB, "xr_f1", [128, D], F32)]
        xq_tm = [sb(phB, "xq_tm0", [128, D], BF16), sb(phB, "xq_tm1", [128, D], BF16)]
        xq_fm = [sb(phB, "xq_fm0", [128, D], BF16), sb(phB, "xq_fm1", [128, D], BF16)]
        out_sb2 = sb(phB, "out_sb2", [128, D], F32)
        wrow = sb(phB, "wrow", [128, D], BF16)
        asum = sb(phB, "asum", [128, 1], F32)
        sv = sb(phB, "sv", [128, 1], F32)
        sinv = sb(phB, "sinv", [128, 1], F32)
        mx0 = sb(phB, "mx0", [128, 2], F32)
        mx1 = sb(phB, "mx1", [128, 2], F32)
        mxc = sb(phB, "mxc", [128, 2], F32)
        r7 = sb(phB, "r7", [128, 2], F32)
        s7 = sb(phB, "s7", [128, 2], F32)
        remB = nc.sbuf_bytes_remaining
        print("SBUF remB:", remB)

        wait("sync", ts, TEND); wait("sync", vs, VEND); wait("sync", as_, AEND)
        wait("sync", osA, osA.n); wait("sync", ss, ss.n)

        wfT_r = wfT[:, :].rearrange("p (a b) -> p a b", b=1024)
        ptr_r = p_tr0[:, :].rearrange("p (a b) -> p a b", b=128)
        wrow_toks = {}
        wfT_ev = 0
        for oc in range(8):
            if oc >= 2:
                wait("sync", vs, wrow_toks[oc - 2])
            E("sync", "dma_start", stgB[oc % 2][:, :], wffn_d[oc * 128:(oc + 1) * 128, :], inc=(wfp[oc % 2], 16))
            wait("vector", wfp[oc % 2], wfp[oc % 2].n)
            if oc == 0:
                wait("vector", ts, TEND)
                wait("vector", as_, AEND)
            E("vector", "tensor_reduce", asum[:, :], stgB[oc % 2][:, :], AX, ALU.add,
              apply_absolute_value=True, inc=(vs, 1))
            wait("vector", vs, vs.n)
            E("vector", "tensor_scalar", sv[:, :], asum[:, :], 1.0 / D, 1e-5,
              ALU.mult, ALU.max, inc=(vs, 1))
            wait("scalar", vs, vs.n)
            if oc == 0:
                wait("scalar", ts, TEND)
            E("scalar", "activation", sinv[:, :], sv[:, :], AF.Ln, bias=0.0, scale=1.0, inc=(as_, 1))
            wait("scalar", as_, as_.n)
            E("scalar", "activation", sinv[:, :], sinv[:, :], AF.Exp, bias=0.0, scale=-1.0, inc=(as_, 1))
            wait("vector", as_, as_.n)
            E("vector", "tensor_scalar", xr_f[0][:, :], stgB[oc % 2][:, :], sinv[:, :], MAGIC,
              ALU.mult, ALU.add)
            E("vector", "tensor_scalar", xr_f[0][:, :], xr_f[0][:, :], MAGIC, -1.0,
              ALU.subtract, ALU.max)
            E("vector", "tensor_scalar", wrow[:, :], xr_f[0][:, :], 1.0, sv[:, :],
              ALU.min, ALU.mult, inc=(vs, 1))
            wrow_toks[oc] = vs.n
            wait("tensor", vs, vs.n)
            if oc == 0:
                wait_t("tensor", trg_ev[0])
            else:
                wait("tensor", vs, wfT_ev)
            for ic in range(8):
                icc = (ts, 1) if ic == 7 else None
                E("tensor", "transpose", p_tr0[:, ic * 128:(ic + 1) * 128],
                  wrow[:, ic * 128:(ic + 1) * 128], idn[:, :], inc=icc)
            wait("vector", ts, ts.n)
            E("vector", "tensor_copy", wfT_r[:, :, oc * 128:(oc + 1) * 128], ptr_r[:, :, :], inc=(vs, 1))
            wfT_ev = vs.n
        for dc in range(8):
            E("sync", "dma_start", hm[:, dc * 1024:(dc + 1) * 1024],
              hmat_d[dc * 128:(dc + 1) * 128, :], inc=(hmS, 16))
        wprep_bs = hmS.n
        wprep_vs = vs.n
        tap("wfT", wfT[:, :], vs, wfT_ev, blocker="tensor")
        tap("hm", hm[:, :], hmS, wprep_bs, blocker="tensor")

        xh_tok = {}; ff_tok = {}; qz_tok = {}; outevB = {}
        xtokB = {}; otokB = {}
        outdmaB_par = [0, 0]
        xqfm_ev_par = [0, 0]
        out_sbB = [out_sb, out_sb2]

        def emit_b_tail(t):
            if t < 0:
                return
            par = t % 2
            seq, r0 = t // TPS, (t % TPS) * C
            trb = p_tr[par]
            wait("tensor", vs, qz_tok[t])
            if t >= 2:
                wait("tensor", vs, xqfm_ev_par[par])
            elif t == 0:
                wait("tensor", vs, wfT_ev)
            for ic in range(8):
                icc = (ts, 1) if ic == 7 else None
                E("tensor", "transpose", trb[:, ic * 128:(ic + 1) * 128],
                  xq_tm[par][:, ic * 128:(ic + 1) * 128], idn[:, :], inc=icc)
            xqT = ts.n
            wait("vector", ts, xqT)
            E("vector", "tensor_copy", xq_fm[par][:, :], trb[:, :], inc=(vs, 1))
            xqfm_ev_par[par] = vs.n
            wait("tensor", vs, xqfm_ev_par[par])
            if t >= 1:
                wait("tensor", vs, outevB[t - 1])
            for half, bank in ((0, p_o0), (1, p_o1)):
                for ic in range(8):
                    icc = (ts, 1) if (half == 1 and ic == 7) else None
                    E("tensor", "matmul", bank[:, :], xq_fm[par][:, ic * 128:(ic + 1) * 128],
                      wfT[:, ic * 1024 + half * 512: ic * 1024 + (half + 1) * 512],
                      start=(ic == 0), stop=(ic == 7), inc=icc)
            ff_tok[t] = ts.n
            wait("vector", ts, ff_tok[t])
            wait_t("vector", otokB[t])
            wait("vector", osB, outdmaB_par[par])
            if t == 0:
                wait("vector", osA, osA.n)
            E("vector", "tensor_tensor", out_sbB[par][:, 0:512], p_o0[:, :], outrd[par][:, 0:512], ALU.add)
            E("vector", "tensor_tensor", out_sbB[par][:, 512:1024], p_o1[:, :], outrd[par][:, 512:1024],
              ALU.add, inc=(vs, 1))
            outevB[t] = vs.n
            if t == 0:
                tap("outB", out_sbB[par][:, :], vs, outevB[t], blocker="gpsimd")
            wait("gpsimd", vs, outevB[t])
            E("gpsimd", "dma_start", out_d[seq, r0:r0 + C, :], out_sbB[par][:, :], inc=(osB, 16))
            outdmaB_par[par] = osB.n

        for t in range(TILES):
            seq, r0 = t // TPS, (t % TPS) * C
            par = t % 2
            banksA = [(p_pa, p_pb), (p_sc, p_ds)][par]
            if t >= 2:
                wait("sync", ts, xh_tok[t - 2])
                wait("sync", vs, outevB.get(t - 2, 0))
            E("sync", "dma_start", xnb[par][:, :], xns_d[t, :, :], inc=(bxn[par], 16))
            xtokB[t] = (bxn[par], bxn[par].n)
            E("sync", "dma_start", outrd[par][:, :], out_d[seq, r0:r0 + C, :], inc=(bor[par], 16))
            otokB[t] = (bor[par], bor[par].n)

            wait_t("tensor", xtokB[t])
            if t == 0:
                wait("tensor", hmS, wprep_bs)
                wait("tensor", vs, wprep_vs)
                wait("tensor", as_, AEND)
            elif t == 1:
                wait("tensor", vs, wprep_vs)
            else:
                wait("tensor", vs, qz_tok[t - 2])
            for half in (0, 1):
                bank = banksA[half]
                for dc in range(8):
                    icc = (ts, 1) if (half == 1 and dc == 7) else None
                    E("tensor", "matmul", bank[:, :], xnb[par][:, dc * 128:(dc + 1) * 128],
                      hm[:, dc * 1024 + half * 512: dc * 1024 + (half + 1) * 512],
                      start=(dc == 0), stop=(dc == 7), inc=icc)
            xh_tok[t] = ts.n

            emit_b_tail(t - 1)

            wait("vector", ts, xh_tok[t])
            E("vector", "reduce_max", mx0[:, par:par + 1], banksA[0][:, :], AX, apply_absolute_value=True)
            E("vector", "reduce_max", mx1[:, par:par + 1], banksA[1][:, :], AX, apply_absolute_value=True, inc=(vs, 1))
            wait("vector", vs, vs.n)
            E("vector", "scalar_tensor_tensor", mxc[:, par:par + 1], mx0[:, par:par + 1], 1e-5,
              mx1[:, par:par + 1], ALU.max, ALU.max, inc=(vs, 1))
            wait("vector", vs, vs.n)
            E("vector", "reciprocal", r7[:, par:par + 1], mxc[:, par:par + 1], inc=(vs, 1))
            wait("vector", vs, vs.n)
            E("vector", "tensor_scalar_mul", s7[:, par:par + 1], mxc[:, par:par + 1], 1.0 / 7.0)
            E("vector", "tensor_scalar_mul", r7[:, par:par + 1], r7[:, par:par + 1], 7.0, inc=(vs, 1))
            wait("vector", vs, vs.n)
            E("vector", "tensor_scalar", xr_f[par][:, 0:512], banksA[0][:, :], r7[:, par:par + 1], MAGIC,
              ALU.mult, ALU.add)
            E("vector", "tensor_scalar", xr_f[par][:, 512:1024], banksA[1][:, :], r7[:, par:par + 1], MAGIC,
              ALU.mult, ALU.add)
            E("vector", "tensor_scalar", xq_tm[par][:, :], xr_f[par][:, :], MAGIC, s7[:, par:par + 1],
              ALU.subtract, ALU.mult, inc=(vs, 1))
            qz_tok[t] = vs.n

        emit_b_tail(TILES - 1)

        wait("gpsimd", osB, osB.n)
        phB.close()

        for s in (xs[0], xs[1], ws2[0], ws2[1], csem, ss, osA, osB,
                  bxn[0], bxn[1], bor[0], bor[1], wfp[0], wfp[1], hmS,
                  ts, vs, as_, gs, dbg_state["sem"]):
            nc.gpsimd.sem_clear(s.h)
        nc.all_engine_barrier()
        with nc.Block() as block:
            @block.sync
            def _(eng):
                for f in S_["sync"]:
                    f(eng)

            @block.tensor
            def _(eng):
                for f in S_["tensor"]:
                    f(eng)

            @block.vector
            def _(eng):
                for f in S_["vector"]:
                    f(eng)

            @block.scalar
            def _(eng):
                for f in S_["scalar"]:
                    f(eng)

            @block.gpsimd
            def _(eng):
                for f in S_["gpsimd"]:
                    f(eng)

    nc.reset()
    return nc


_CACHE = {}


def _hadamard_bf16():
    h = np.array([[1.0]], dtype=np.float64)
    while h.shape[0] < 1024:
        h = np.block([[h, h], [h, -h]])
    return np.ascontiguousarray((h / 32.0).astype(ml_dtypes.bfloat16))


def kernel(L0_attn_x, L0_attn_Wq, L0_attn_Wk, L0_attn_Wv, L0_attn_Wg,
           L0_attn_Wgk1, L0_attn_Wgk2, L0_attn_bgk2, L0_attn_Wo,
           L0_attn_g_norm_weight, L0_ffn_weight):
    x = np.ascontiguousarray(np.asarray(L0_attn_x, dtype=np.float32))
    common = {
        "wq": np.ascontiguousarray(np.asarray(L0_attn_Wq, np.float32)),
        "wk": np.ascontiguousarray(np.asarray(L0_attn_Wk, np.float32)),
        "wv": np.ascontiguousarray(np.asarray(L0_attn_Wv, np.float32)),
        "wg": np.ascontiguousarray(np.asarray(L0_attn_Wg, np.float32)),
        "wgk1": np.ascontiguousarray(np.asarray(L0_attn_Wgk1, np.float32)),
        "wgk2": np.ascontiguousarray(np.asarray(L0_attn_Wgk2, np.float32)),
        "wo": np.ascontiguousarray(np.asarray(L0_attn_Wo, np.float32)),
        "wffn": np.ascontiguousarray(np.asarray(L0_ffn_weight, np.float32)),
        "bgk2c": np.ascontiguousarray(np.asarray(L0_attn_bgk2, np.float32).reshape(8, 128).T),
        "gnwb": np.ascontiguousarray(np.broadcast_to(
            np.tile(np.asarray(L0_attn_g_norm_weight, np.float32), 4)[None, :], (128, 2048))),
        "hmat": _hadamard_bf16(),
        "idn": np.ascontiguousarray(np.eye(128, dtype=ml_dtypes.bfloat16)),
        "maskT": np.ascontiguousarray(
            np.triu(np.ones((128, 128), np.float32)).astype(ml_dtypes.bfloat16)),
    }
    if "nc" not in _CACHE:
        _CACHE["nc"] = build_program()
    nc = _CACHE["nc"]
    in_maps = []
    for c in range(8):
        m = dict(common)
        m["x"] = np.ascontiguousarray(x[2 * c:2 * c + 2])
        in_maps.append(m)
    res = run_bass_kernel_spmd(nc, in_maps, core_ids=list(range(8)))
    return np.concatenate([np.asarray(r["out"], dtype=np.float32) for r in res.results], axis=0)


# version 20
# speedup vs baseline: 1.3655x; 1.3655x over previous
import numpy as np
import ml_dtypes
import concourse.bass as bass
import concourse.mybir as mybir
from concourse.bass_utils import run_bass_kernel_spmd
from contextlib import ExitStack

F32 = mybir.dt.float32
BF16 = mybir.dt.bfloat16
AF = mybir.ActivationFunctionType
ALU = mybir.AluOpType
AX = mybir.AxisListType.X

MAGIC = 1.5 * 2.0 ** 23
LN16 = float(np.log(16.0))
LN7 = float(np.log(7.0))

SEQS = 2
N = 2048
D = 1024
C = 128
TPS = N // C
TILES = SEQS * TPS
HD = 4
DV = 512


def build_program(debug=False):
    nc = bass.Bass("TRN2", target_bir_lowering=False)

    x_d = nc.dram_tensor("x", [SEQS, N, D], F32, kind="ExternalInput")
    wq_d = nc.dram_tensor("wq", [D, D], F32, kind="ExternalInput")
    wk_d = nc.dram_tensor("wk", [D, D], F32, kind="ExternalInput")
    wv_d = nc.dram_tensor("wv", [D, 2048], F32, kind="ExternalInput")
    wg_d = nc.dram_tensor("wg", [D, 2048], F32, kind="ExternalInput")
    wgk1_d = nc.dram_tensor("wgk1", [D, 16], F32, kind="ExternalInput")
    wgk2_d = nc.dram_tensor("wgk2", [16, D], F32, kind="ExternalInput")
    wo_d = nc.dram_tensor("wo", [2048, D], F32, kind="ExternalInput")
    wffn_d = nc.dram_tensor("wffn", [D, D], F32, kind="ExternalInput")
    bgk2c_d = nc.dram_tensor("bgk2c", [128, 8], F32, kind="ExternalInput")
    gnwb_d = nc.dram_tensor("gnwb", [128, 2048], F32, kind="ExternalInput")
    hmat_d = nc.dram_tensor("hmat", [D, D], BF16, kind="ExternalInput")
    idn_d = nc.dram_tensor("idn", [128, 128], BF16, kind="ExternalInput")
    mask_d = nc.dram_tensor("maskT", [128, 128], BF16, kind="ExternalInput")
    out_d = nc.dram_tensor("out", [SEQS, N, D], F32, kind="ExternalOutput")
    xns_d = nc.dram_tensor("xnscr", [TILES, 128, D], BF16, kind="Internal")

    S_ = {"tensor": [], "vector": [], "scalar": [], "sync": [], "gpsimd": []}

    def E(eng, method, *args, inc=None, **kw):
        if inc is not None:
            sem, k = inc
            sem.n += k
            def f(e, m=method, a=args, kv=kw, sh=sem.h, kk=k):
                getattr(e, m)(*a, **kv).then_inc(sh, kk)
        else:
            def f(e, m=method, a=args, kv=kw):
                getattr(e, m)(*a, **kv)
        S_[eng].append(f)

    class Sem:
        def __init__(self, stack, name):
            self.h = stack.enter_context(nc.semaphore(name))
            self.n = 0

    def wait(eng, sem, val):
        if val and val > 0:
            S_[eng].append(lambda e, s=sem.h, v=val: e.wait_ge(s, v))

    def wait_t(eng, tok):
        if tok is not None:
            wait(eng, tok[0], tok[1])

    dbg_tensors = {}
    dbg_state = {"sem": None}

    def tap(name, ap, sem, val, blocker=None):
        if not debug:
            return
        sh = list(ap.shape) if hasattr(ap, "shape") else None
        t = nc.dram_tensor("dbg_" + name, sh, ap.dtype, kind="ExternalOutput")
        dbg_tensors[name] = t
        dsem = dbg_state["sem"]
        wait("gpsimd", sem, val)
        E("gpsimd", "dma_start", t[:, :] if len(sh) == 2 else t[:], ap, inc=(dsem, 16))
        if blocker:
            wait(blocker, dsem, dsem.n)

    with ExitStack() as top:
        xs = [Sem(top, "xs0"), Sem(top, "xs1")]
        ws2 = [Sem(top, "ws0"), Sem(top, "ws1")]
        csem = Sem(top, "csem")
        ss = Sem(top, "ss")
        osA = Sem(top, "osA"); osB = Sem(top, "osB")
        bxn = [Sem(top, "bxn0"), Sem(top, "bxn1")]
        bor = [Sem(top, "bor0"), Sem(top, "bor1")]
        wfp = [Sem(top, "wfp0"), Sem(top, "wfp1")]
        hmS = Sem(top, "hmS")
        ts = Sem(top, "ts"); vs = Sem(top, "vs"); as_ = Sem(top, "as")
        gs = Sem(top, "gs")
        dbg_state["sem"] = Sem(top, "dbg")

        p_tr0 = top.enter_context(nc.psum_tensor("p_tr0", [128, 1024], BF16))
        p_tr1 = top.enter_context(nc.psum_tensor("p_tr1", [128, 1024], BF16))
        p_pa = top.enter_context(nc.psum_tensor("p_pa", [128, 512], F32))
        p_pb = top.enter_context(nc.psum_tensor("p_pb", [128, 512], F32))
        p_sc = top.enter_context(nc.psum_tensor("p_sc", [128, 512], F32))
        p_o0 = top.enter_context(nc.psum_tensor("p_o0", [128, 512], F32))
        p_o1 = top.enter_context(nc.psum_tensor("p_o1", [128, 512], F32))
        p_ds = top.enter_context(nc.psum_tensor("p_ds", [128, 512], F32))
        p_tr = [p_tr0, p_tr1]
        p_o = [p_o0, p_o1]

        def sb(st, name, shape, dt):
            return st.enter_context(nc.sbuf_tensor(name, shape, dt))

        Ssb = sb(top, "S", [128, HD * 2 * DV], F32)
        gnwb = sb(top, "gnwb_s", [128, 2048], BF16)
        negb = sb(top, "negb", [128, 8], F32)
        idn = sb(top, "idn_s", [128, 128], BF16)
        maskT = sb(top, "mask_s", [128, 128], BF16)
        zbf = sb(top, "zbf", [128, 128], BF16)
        padb = sb(top, "padb", [128, 128], BF16)
        pada = sb(top, "pada", [128, 128], BF16)
        cst = sb(top, "cst", [128, 3], F32)
        xb = [sb(top, "xb0", [128, D], F32), sb(top, "xb1", [128, D], F32)]
        out_sb = sb(top, "out_sb", [128, D], F32)

        phA = ExitStack()
        wq = sb(phA, "wq_s", [128, 8192], BF16)
        wk = sb(phA, "wk_s", [128, 8192], BF16)
        wv = sb(phA, "wv_s", [128, 16384], BF16)
        wg = sb(phA, "wg_s", [128, 16384], BF16)
        wo = sb(phA, "wo_s", [128, 16384], BF16)
        wgk1 = sb(phA, "wgk1_s", [128, 128], BF16)
        wgk2 = sb(phA, "wgk2_s", [16, 1024], BF16)

        stg_ctx = ExitStack()
        stg = [sb(stg_ctx, "stg0", [128, 2048], F32), sb(stg_ctx, "stg1", [128, 2048], F32)]

        for dst, src in ((idn, idn_d), (maskT, mask_d), (negb, bgk2c_d)):
            E("sync", "dma_start", dst[:, :], src[:, :], inc=(csem, 16))
        const_tok = csem.n
        E("sync", "dma_start", xb[0][:, :], x_d[0, 0:C, :], inc=(xs[0], 16))
        xtok = {0: (xs[0], xs[0].n)}

        chunks = []
        for dc in range(8):
            chunks.append((wq_d[dc * 128:(dc + 1) * 128, :], wq[:, dc * 1024:(dc + 1) * 1024], 128, 1024))
        for dc in range(8):
            chunks.append((wk_d[dc * 128:(dc + 1) * 128, :], wk[:, dc * 1024:(dc + 1) * 1024], 128, 1024))
        for dc in range(8):
            chunks.append((wgk1_d[dc * 128:(dc + 1) * 128, :], wgk1[:, dc * 16:(dc + 1) * 16], 128, 16))
        chunks.append((wgk2_d[:, :], wgk2[:, :], 16, 1024))
        chunks.append((gnwb_d[:, :], gnwb[:, :], 128, 2048))
        for dc in range(8):
            chunks.append((wv_d[dc * 128:(dc + 1) * 128, :], wv[:, dc * 2048:(dc + 1) * 2048], 128, 2048))
        for dc in range(8):
            chunks.append((wg_d[dc * 128:(dc + 1) * 128, :], wg[:, dc * 2048:(dc + 1) * 2048], 128, 2048))
        for jc in range(16):
            chunks.append((wo_d[jc * 128:(jc + 1) * 128, :], wo[:, jc * 1024:(jc + 1) * 1024], 128, 1024))

        cast_tok = {}
        for i, (dap, sap, r, cc_) in enumerate(chunks):
            if i >= 2:
                wait("sync", vs, cast_tok[i - 2])
            E("sync", "dma_start", stg[i % 2][0:r, 0:cc_], dap, inc=(ws2[i % 2], 16))
            wait("vector", ws2[i % 2], ws2[i % 2].n)
            E("vector", "tensor_copy", sap, stg[i % 2][0:r, 0:cc_], inc=(vs, 1))
            cast_tok[i] = vs.n
        wcast = vs.n
        wait("vector", csem, const_tok)
        E("vector", "tensor_scalar_mul", negb[:, :], negb[:, :], -1.0)
        E("vector", "memset", zbf[:, :], 0.0)
        E("vector", "memset", cst[:, 0:1], 1e-5)
        E("vector", "memset", cst[:, 1:2], -LN16)
        E("vector", "memset", cst[:, 2:3], LN7)
        E("vector", "memset", Ssb[:, :], 0.0)

        stg_ctx.close()

        xn_tm = sb(phA, "xn_tm", [128, D], BF16)
        xn_fm = sb(phA, "xn_fm", [128, D], BF16)
        q_fm = sb(phA, "q_fm", [128, D], BF16)
        k_fm = sb(phA, "k_fm", [128, D], BF16)
        x1_sb = sb(phA, "x1_sb", [16, 128], BF16)
        D_bf = sb(phA, "D_bf", [128, D], BF16)
        EqS = sb(phA, "EqS", [128, D], BF16)
        EkS = sb(phA, "EkS", [128, D], BF16)
        v_tm = sb(phA, "v_tm", [128, 2048], BF16)
        g_tm = sb(phA, "g_tm", [128, 2048], BF16)
        sgA = sb(phA, "sgA", [128, 2048], BF16)
        sgB = sb(phA, "sgB", [128, 512], BF16)
        sbf = sb(phA, "sbf", [128, 2048], BF16)
        shat = sb(phA, "shat", [128, 256], BF16)
        khtm = sb(phA, "khtm", [128, 1024], BF16)
        ggb = sb(phA, "ggb", [128, 2048], BF16)
        otld = sb(phA, "otld", [128, 1024], BF16)
        o_fm = sb(phA, "o_fm", [128, 2048], BF16)
        xsum = sb(phA, "xsum", [128, 1], F32)
        xsq = sb(phA, "xsq", [128, 1], F32)
        meanv = sb(phA, "meanv", [128, 1], F32)
        varv = sb(phA, "varv", [128, 1], F32)
        rstdln = sb(phA, "rstdln", [128, 1], F32)
        tmp1 = sb(phA, "tmp1", [128, 1], F32)
        cendv = sb(phA, "cendv", [128, 8], F32)
        lamv = sb(phA, "lamv", [128, 8], F32)
        ssumv = sb(phA, "ssumv", [128, 4], F32)
        rstdv = sb(phA, "rstdv", [128, 4], F32)
        remA = nc.sbuf_bytes_remaining
        print("SBUF remA:", remA)

        tr_i = 0
        trg_ev = [None, None]
        VEND = AEND = TEND = 0
        stored_prev = 0
        outdma_prev = 0
        xtm_tok = 0
        zhi_prev = 0
        eqek_prev = 0
        outev_prev = 0
        shat_h3_prev = 0
        supd_h3_prev = 0
        gev_pb_prev = 0
        qt_prev = 0
        fold_half_prev = [0, 0]
        fold_toks = [0, 0]
        lnstats_tok = 0
        rstdln_tok = 0
        lnvar_tok = 0
        sc_tok = {}; shat_tok_d = {}; supd_tok_d = {}; ofm_tok_d = {}
        oh_tok = {}; ds_tok = {}; sq_tok = {}; og_tok = {}; oT_tok = {}
        ocp_tok = {}; sqr_tok = {}; scast_tok = {}

        def tr_group():
            nonlocal tr_i
            g = tr_i % 2
            tr_i += 1
            return g

        def emit_gate(n, h, fold_tok):
            if h < 0:
                return
            hp = h % 2
            kp2 = (n, h - 2) if h >= 2 else (n - 1, h + 2)
            wait("vector", as_, sq_tok[(n, h)])
            wait("vector", gs, fold_tok)
            wait("vector", ts, oT_tok.get(kp2, 0))
            E("vector", "scalar_tensor_tensor", otld[:, hp * 512:(hp + 1) * 512],
              g_tm[:, hp * 512:(hp + 1) * 512], rstdv[:, h:h + 1],
              ggb[:, h * 512:(h + 1) * 512], ALU.mult, ALU.mult, inc=(vs, 1))
            og_tok[(n, h)] = vs.n

        oT_grp = {}

        def emit_oT_pe(n, h):
            if h < 0:
                return
            hp = h % 2
            g2 = tr_group()
            oT_grp[(n, h)] = g2
            wait_t("tensor", trg_ev[g2])
            wait("tensor", vs, og_tok[(n, h)])
            for jj in range(4):
                icc = (ts, 1) if jj == 3 else None
                E("tensor", "transpose", p_tr[g2][:, jj * 128:(jj + 1) * 128],
                  otld[:, hp * 512 + jj * 128: hp * 512 + (jj + 1) * 128], idn[:, :], inc=icc)
            oT_tok[(n, h)] = ts.n

        def emit_ofm(n, h):
            if h < 0:
                return
            g2 = oT_grp[(n, h)]
            wait("scalar", ts, oT_tok[(n, h)])
            E("scalar", "copy", o_fm[:, h * 512:(h + 1) * 512], p_tr[g2][:, 0:512], inc=(as_, 1))
            ofm_tok_d[(n, h)] = as_.n
            trg_ev[g2] = (as_, as_.n)

        def emit_ln_stats(tile):
            nonlocal lnstats_tok
            par = tile % 2
            wait_t("vector", xtok[tile])
            wait("vector", osA, outdma_prev)
            E("vector", "tensor_scalar", out_sb[:, :], xb[par][:, :], 1.0, None,
              ALU.mult, ALU.add, accum_out=xsum[:, :])
            E("vector", "scalar_tensor_tensor", out_sb[:, :], xb[par][:, :], 1.0,
              xb[par][:, :], ALU.mult, ALU.mult, accum_out=xsq[:, :], inc=(vs, 1))
            lnstats_tok = vs.n

        def emit_ln_rstd():
            nonlocal rstdln_tok, lnvar_tok
            wait("scalar", vs, lnstats_tok)
            E("scalar", "activation", tmp1[:, :], xsum[:, :], AF.Square, bias=0.0,
              scale=1.0 / D, inc=(as_, 1))
            t_msq = as_.n
            wait("vector", as_, t_msq)
            E("vector", "tensor_scalar_mul", meanv[:, :], xsum[:, :], 1.0 / D)
            E("vector", "scalar_tensor_tensor", varv[:, :], xsq[:, :], 1.0 / D,
              tmp1[:, :], ALU.mult, ALU.subtract, inc=(vs, 1))
            lnvar_tok = vs.n
            wait("scalar", vs, lnvar_tok)
            E("scalar", "activation", rstdln[:, :], varv[:, :], AF.Ln, bias=cst[:, 0:1],
              scale=1.0, inc=(as_, 1))
            wait("scalar", as_, as_.n)
            E("scalar", "activation", rstdln[:, :], rstdln[:, :], AF.Exp, bias=0.0,
              scale=-0.5, inc=(as_, 1))
            rstdln_tok = as_.n

        def emit_ln_apply(tile):
            nonlocal xtm_tok
            par = tile % 2
            wait("vector", as_, rstdln_tok)
            E("vector", "tensor_scalar", xn_tm[:, :], xb[par][:, :], meanv[:, :],
              rstdln[:, :], ALU.subtract, ALU.mult, inc=(vs, 1))
            xtm_tok = vs.n

        emit_ln_stats(0)
        emit_ln_rstd()
        emit_ln_apply(0)

        for n in range(TILES):
            seq, r0 = n // TPS, (n % TPS) * C
            par = n % 2
            pn = n - 1

            if n == TPS:
                E("vector", "memset", Ssb[:, :], 0.0)

            if n + 1 < TILES:
                sq1, rr1 = (n + 1) // TPS, ((n + 1) % TPS) * C
                wait("sync", vs, outev_prev)
                xp = (n + 1) % 2
                E("sync", "dma_start", xb[xp][:, :], x_d[sq1, rr1:rr1 + C, :], inc=(xs[xp], 16))
                xtok[n + 1] = (xs[xp], xs[xp].n)

            g0 = tr_group()
            wait_t("tensor", trg_ev[g0])
            wait("tensor", vs, xtm_tok)
            for kk in range(8):
                icc = (ts, 1) if kk == 7 else None
                E("tensor", "transpose", p_tr[g0][:, kk * 128:(kk + 1) * 128],
                  xn_tm[:, kk * 128:(kk + 1) * 128], idn[:, :], inc=icc)
            xnT_tok = ts.n

            wait("scalar", ts, xnT_tok)
            wait("scalar", ss, stored_prev)
            E("scalar", "copy", xn_fm[:, :], p_tr[g0][:, :], inc=(as_, 1))
            xfm_tok = as_.n
            trg_ev[g0] = (as_, as_.n)

            wait("gpsimd", as_, xfm_tok)
            E("gpsimd", "dma_start", xns_d[n, :, :], xn_fm[:, :], inc=(ss, 16))
            stored_n = ss.n

            qk_banks = [p_sc, p_ds, p_pa, p_pb]
            qk_free = [shat_h3_prev, supd_h3_prev, supd_h3_prev, gev_pb_prev]
            evtoks = {}
            x1ev_tok = 0
            wait("tensor", as_, xfm_tok)
            if n == 0:
                wait("tensor", vs, wcast)
                tap("wq", wq[:, :], vs, wcast)
                tap("wk", wk[:, :], vs, wcast)
                tap("wv", wv[:, :], vs, wcast)
                tap("wg", wg[:, :], vs, wcast)
                tap("wo", wo[:, :], vs, wcast)
                tap("gnwbs", gnwb[:, :], vs, wcast)
            for gi, (name, wsb, dst, off, fcs) in enumerate((
                    ("qA", wq, q_fm, 0, range(0, 4)),
                    ("qB", wq, q_fm, 512, range(4, 8)),
                    ("kA", wk, k_fm, 0, range(0, 4)),
                    ("kB", wk, k_fm, 512, range(4, 8)))):
                bank = qk_banks[gi]
                wait("tensor", vs, qk_free[gi])
                for fi, fc in enumerate(fcs):
                    for dc in range(8):
                        icc = (ts, 1) if (fi == 3 and dc == 7) else None
                        E("tensor", "matmul", bank[:, (fc % 4) * 128:(fc % 4 + 1) * 128],
                          wsb[:, dc * 1024 + fc * 128: dc * 1024 + (fc + 1) * 128],
                          xn_fm[:, dc * 128:(dc + 1) * 128],
                          start=(dc == 0), stop=(dc == 7), inc=icc)
                wait("scalar", ts, ts.n)
                if gi == 0:
                    wait("scalar", ts, oh_tok.get((pn, 3), 0))
                E("scalar", "copy", dst[:, off:off + 512], bank[:, :], inc=(as_, 1))
                evtoks[name] = as_.n
                if gi == 0:
                    wait("tensor", vs, outev_prev)
                    for dc in range(8):
                        icc = (ts, 1) if dc == 7 else None
                        E("tensor", "matmul", p_o0[0:16, 0:128], wgk1[:, dc * 16:(dc + 1) * 16],
                          xn_fm[:, dc * 128:(dc + 1) * 128], start=(dc == 0), stop=(dc == 7), inc=icc)
                    x1_tok = ts.n
                    wait("scalar", ts, x1_tok)
                    wait("scalar", ts, zhi_prev)
                    E("scalar", "copy", x1_sb[:, :], p_o0[0:16, 0:128], inc=(as_, 1))
                    x1ev_tok = as_.n
                if gi == 1:
                    wait("tensor", as_, x1ev_tok)
                    for fc in range(8):
                        bank2 = p_o0 if fc < 4 else p_o1
                        icc = (ts, 1) if fc == 7 else None
                        E("tensor", "matmul", bank2[:, (fc % 4) * 128:(fc % 4 + 1) * 128],
                          wgk2[:, fc * 128:(fc + 1) * 128], x1_sb[:, :], start=True, stop=True, inc=icc)
                    z_tok = ts.n
                    zhi_prev = z_tok

            wait("scalar", ts, z_tok)
            wait("scalar", vs, qt_prev)
            for ch in range(8):
                bank = p_o0 if ch < 4 else p_o1
                E("scalar", "activation", EkS[:, ch * 128:(ch + 1) * 128],
                  bank[:, (ch % 4) * 128:(ch % 4 + 1) * 128], AF.Exp,
                  bias=negb[:, ch:ch + 1], scale=-1.0)
            E("scalar", "activation", D_bf[:, :], EkS[:, :], AF.Ln, bias=1.0, scale=1.0, inc=(as_, 1))
            gates_tok = as_.n

            vev_toks = [0] * 4
            gev_toks = [0] * 4
            for gi in range(8):
                wsb = wv if gi < 4 else wg
                grp = gi % 4
                bank = qk_banks[grp]
                if gi < 4:
                    wait("tensor", as_, evtoks[("qA", "qB", "kA", "kB")[gi]])
                else:
                    wait("tensor", vs, vev_toks[grp])
                for dc in range(8):
                    icc = (ts, 1) if dc == 7 else None
                    E("tensor", "matmul", bank[:, :], xn_fm[:, dc * 128:(dc + 1) * 128],
                      wsb[:, dc * 2048 + grp * 512: dc * 2048 + (grp + 1) * 512],
                      start=(dc == 0), stop=(dc == 7), inc=icc)
                vg_w = ts.n
                if gi < 4:
                    wait("vector", ts, vg_w)
                    if gi == 0:
                        wait("vector", ts, ds_tok.get((pn, 3), 0))
                    E("vector", "tensor_copy", v_tm[:, grp * 512:(grp + 1) * 512], bank[:, :], inc=(vs, 1))
                    vev_toks[grp] = vs.n
                else:
                    wait("vector", ts, vg_w)
                    if gi == 4:
                        wait("vector", as_, sq_tok.get((pn, 3), 0))
                        wait("vector", gs, fold_half_prev[0])
                    if gi == 6:
                        wait("vector", gs, fold_half_prev[1])
                    E("vector", "tensor_copy", g_tm[:, grp * 512:(grp + 1) * 512], bank[:, :], inc=(vs, 1))
                    gev_toks[grp] = vs.n
                    if gi in (5, 7):
                        hh = 0 if gi == 5 else 1
                        lo = hh * 1024
                        wait("scalar", vs, gev_toks[grp])
                        wait("scalar", gs, fold_half_prev[hh])
                        E("scalar", "activation", sgA[:, lo:lo + 1024],
                          g_tm[:, lo:lo + 1024], AF.Exp, bias=0.0, scale=-1.0)
                        E("scalar", "activation", sgA[:, lo:lo + 1024],
                          sgA[:, lo:lo + 1024], AF.Ln, bias=1.0, scale=1.0)
                        E("scalar", "activation", sgA[:, lo:lo + 1024],
                          sgA[:, lo:lo + 1024], AF.Exp, bias=0.0, scale=-1.0, inc=(as_, 1))
                        sig_half = as_.n
                        wait("gpsimd", vs, gev_toks[grp])
                        wait("gpsimd", vs, og_tok.get((pn, 2 * hh + 1), 0))
                        E("gpsimd", "tensor_tensor", ggb[:, lo:lo + 1024],
                          g_tm[:, lo:lo + 1024], gnwb[:, lo:lo + 1024], ALU.mult)
                        wait("gpsimd", as_, sig_half)
                        E("gpsimd", "tensor_tensor", ggb[:, lo:lo + 1024],
                          ggb[:, lo:lo + 1024], sgA[:, lo:lo + 1024], ALU.mult, inc=(gs, 1))
                        fold_toks[hh] = gs.n
                        fold_half_prev[hh] = gs.n
                if gi == 1:
                    wait("vector", as_, gates_tok)
                    wait("vector", as_, eqek_prev)
                    for ch in range(8):
                        icc = (vs, 1) if ch == 7 else None
                        E("vector", "tensor_tensor_scan", EqS[:, ch * 128:(ch + 1) * 128],
                          D_bf[:, ch * 128:(ch + 1) * 128], zbf[:, :], 0.0, ALU.add, ALU.add, inc=icc)
                    for ch in range(8):
                        icc = (vs, 1) if ch == 7 else None
                        E("vector", "tensor_copy", cendv[:, ch:ch + 1],
                          EqS[:, ch * 128 + 127: ch * 128 + 128], inc=icc)
                    scan_tok = vs.n
                    wait("scalar", vs, scan_tok)
                    E("scalar", "activation", lamv[:, :], cendv[:, :], AF.Exp, bias=0.0, scale=-1.0 / 16.0)
                    E("scalar", "activation", D_bf[:, :], EqS[:, :], AF.Exp, bias=cst[:, 1:2], scale=-1.0 / 16.0)
                    E("scalar", "activation", EkS[:, :], EqS[:, :], AF.Exp, bias=0.0, scale=1.0 / 16.0, inc=(as_, 1))
                    eqek_tok = as_.n
                    eqek_prev = eqek_tok
                if gi == 3:
                    wait("vector", ts, oh_tok.get((pn, 3), 0))
                    E("vector", "tensor_copy", sbf[:, 0:1024], Ssb[:, 0:1024])
                    E("vector", "tensor_copy", sbf[:, 1024:2048], Ssb[:, 1024:2048], inc=(vs, 1))
                    sbf_toks = {0: (vs, vs.n), 1: (vs, vs.n)}
                    wait("vector", as_, eqek_tok)
                    E("vector", "tensor_mul", q_fm[:, :], q_fm[:, :], D_bf[:, :])
                    E("vector", "tensor_mul", k_fm[:, :], k_fm[:, :], EkS[:, :])
                    for ch in range(8):
                        icc = (vs, 1) if ch == 7 else None
                        E("vector", "tensor_scalar_mul", D_bf[:, ch * 128:(ch + 1) * 128],
                          k_fm[:, ch * 128:(ch + 1) * 128], lamv[:, ch:ch + 1], inc=icc)
                    qt_tok = vs.n
                    if n == 0:
                        tap("lamv0", lamv[:, :], vs, qt_tok, blocker="scalar")
                        tap("cendv0", cendv[:, :], vs, qt_tok, blocker="scalar")
                        tap("kt0", k_fm[:, :], vs, qt_tok, blocker="tensor")
                    if n == 1:
                        tap("Ssb1", Ssb[:, :], vs, sbf_toks[0][1], blocker="vector")
                        tap("sbf1", sbf[:, :], vs, sbf_toks[0][1], blocker="vector")
                        tap("qt1", q_fm[:, :], vs, qt_tok, blocker="tensor")
                        tap("kt1", k_fm[:, :], vs, qt_tok, blocker="tensor")
                        tap("lamv1", lamv[:, :], vs, qt_tok, blocker="scalar")
                        tap("cum1", EqS[:, :], vs, qt_tok, blocker="scalar")
            sc_free = gev_toks[0]
            supd_last = supd_h3_prev
            khtm_tok = 0
            for h in range(HD):
                hp = h % 2
                c0, c1 = 2 * h, 2 * h + 1
                kprev = (n, h - 2) if h >= 2 else (pn, h + 2)

                wait("tensor", vs, qt_tok)
                wait("tensor", vs, sc_free)
                for ic in range(2):
                    cc_ = (c0, c1)[ic]
                    icc = (ts, 1) if ic == 1 else None
                    E("tensor", "matmul", p_sc[:, 0:128], k_fm[:, cc_ * 128:(cc_ + 1) * 128],
                      q_fm[:, cc_ * 128:(cc_ + 1) * 128], start=(ic == 0), stop=(ic == 1), inc=icc)
                sc_tok[(n, h)] = ts.n

                if h == 0:
                    g1 = tr_group()
                    wait_t("tensor", trg_ev[g1])
                    for cc_ in range(8):
                        icc = (ts, 1) if cc_ == 7 else None
                        E("tensor", "transpose", p_tr[g1][:, cc_ * 128:(cc_ + 1) * 128],
                          D_bf[:, cc_ * 128:(cc_ + 1) * 128], idn[:, :], inc=icc)
                    khT_tok = ts.n

                wait("vector", ts, sc_tok[(n, h)])
                wait("vector", ts, oh_tok.get(kprev, 0))
                E("vector", "tensor_tensor", shat[:, hp * 128:(hp + 1) * 128],
                  p_sc[:, 0:128], maskT[:, :], ALU.mult, inc=(vs, 1))
                shat_tok_d[(n, h)] = vs.n
                sc_free = vs.n
                if n == 1 and h == 0:
                    tap("shat1", shat[:, 0:128], vs, vs.n, blocker="tensor")
                if h == 0:
                    wait("vector", ts, khT_tok)
                    E("vector", "tensor_copy", khtm[:, :], p_tr[g1][:, :], inc=(vs, 1))
                    khtm_tok = vs.n
                    trg_ev[g1] = (vs, vs.n)
                    if n == 0:
                        tap("khtm0", khtm[:, :], vs, khtm_tok, blocker="tensor")

                wait_t("tensor", sbf_toks[hp])
                if h < 2:
                    wait("tensor", as_, gates_tok)
                wait("tensor", vs, ocp_tok.get(kprev, 0))
                for ic in range(2):
                    cc_ = (c0, c1)[ic]
                    E("tensor", "matmul", p_o[hp][:, :], q_fm[:, cc_ * 128:(cc_ + 1) * 128],
                      sbf[:, hp * 1024 + ic * 512: hp * 1024 + (ic + 1) * 512],
                      start=(ic == 0), stop=False)
                wait("tensor", vs, shat_tok_d[(n, h)])
                E("tensor", "matmul", p_o[hp][:, :], shat[:, hp * 128:(hp + 1) * 128],
                  v_tm[:, h * 512:(h + 1) * 512], start=False, stop=True, inc=(ts, 1))
                oh_tok[(n, h)] = ts.n

                wait("vector", ts, oh_tok[(n, h)])
                if h == 0:
                    wait("vector", gs, fold_toks[0])
                if h >= 2:
                    wait("vector", as_, sqr_tok[(n, h - 2)])
                E("vector", "tensor_copy", g_tm[:, hp * 512:(hp + 1) * 512],
                  p_o[hp][:, :], inc=(vs, 1))
                ocp_tok[(n, h)] = vs.n


                wait("scalar", vs, ocp_tok[(n, h)])
                E("scalar", "activation", sgB[:, :], g_tm[:, hp * 512:(hp + 1) * 512],
                  AF.Square, accum_out=ssumv[:, h:h + 1], inc=(as_, 1))
                sqr_tok[(n, h)] = as_.n
                wait("scalar", as_, as_.n)
                E("scalar", "activation", rstdv[:, h:h + 1], ssumv[:, h:h + 1],
                  AF.Ln, bias=cst[:, 0:1], scale=1.0 / DV, inc=(as_, 1))
                wait("scalar", as_, as_.n)
                E("scalar", "activation", rstdv[:, h:h + 1], rstdv[:, h:h + 1],
                  AF.Exp, bias=0.0, scale=-0.5, inc=(as_, 1))
                sq_tok[(n, h)] = as_.n

                if h < 2:
                    wait("scalar", ts, oh_tok[(n, h)])
                    E("scalar", "copy", sbf[:, hp * 1024:(hp + 1) * 1024],
                      Ssb[:, (h + 2) * 1024:(h + 3) * 1024], inc=(as_, 1))
                    sbf_toks[hp] = (as_, as_.n)
                    scast_tok[(n, h + 2)] = as_.n

                wait("tensor", vs, khtm_tok)
                wait("tensor", vs, supd_last)
                if h == 0:
                    wait("tensor", vs, gev_toks[1])
                    wait("tensor", vs, gev_toks[2])
                E("tensor", "matmul", p_ds[:, :], khtm[:, h * 256: h * 256 + 128],
                  v_tm[:, h * 512:(h + 1) * 512], start=True, stop=True)
                E("tensor", "matmul", p_pa[:, :], khtm[:, h * 256 + 128: h * 256 + 256],
                  v_tm[:, h * 512:(h + 1) * 512], start=True, stop=True, inc=(ts, 1))
                ds_tok[(n, h)] = ts.n

                wait("vector", ts, ds_tok[(n, h)])
                if h >= 2:
                    wait("vector", as_, scast_tok[(n, h)])
                E("vector", "scalar_tensor_tensor", Ssb[:, c0 * 512:(c0 + 1) * 512],
                  Ssb[:, c0 * 512:(c0 + 1) * 512], lamv[:, c0:c0 + 1], p_ds[:, :],
                  ALU.mult, ALU.add)
                E("vector", "scalar_tensor_tensor", Ssb[:, c1 * 512:(c1 + 1) * 512],
                  Ssb[:, c1 * 512:(c1 + 1) * 512], lamv[:, c1:c1 + 1], p_pa[:, :],
                  ALU.mult, ALU.add, inc=(vs, 1))
                supd_tok_d[(n, h)] = vs.n
                supd_last = vs.n

                emit_gate(n, h - 1, fold_toks[0 if h - 1 <= 1 else 1])
                emit_oT_pe(n, h - 2)
                emit_ofm(n, h - 2)

            emit_gate(n, 3, fold_toks[1])
            emit_oT_pe(n, 2)
            emit_ofm(n, 2)
            emit_oT_pe(n, 3)
            emit_ofm(n, 3)
            if n + 1 < TILES:
                emit_ln_stats(n + 1)
                emit_ln_rstd()

            if n + 1 < TILES:
                wait("vector", ts, xnT_tok)
                emit_ln_apply(n + 1)

            wait("tensor", as_, ofm_tok_d[(n, 3)])
            wait("tensor", vs, ocp_tok[(n, 2)])
            for jc in range(16):
                icc = (ts, 1) if jc == 15 else None
                E("tensor", "matmul", p_o0[:, :], o_fm[:, jc * 128:(jc + 1) * 128],
                  wo[:, jc * 1024 + 0: jc * 1024 + 512],
                  start=(jc == 0), stop=(jc == 15), inc=icc)
            wait("tensor", vs, ocp_tok[(n, 3)])
            for jc in range(16):
                icc = (ts, 1) if jc == 15 else None
                E("tensor", "matmul", p_o1[:, :], o_fm[:, jc * 128:(jc + 1) * 128],
                  wo[:, jc * 1024 + 512: jc * 1024 + 1024],
                  start=(jc == 0), stop=(jc == 15), inc=icc)
            ao_tok = ts.n

            wait("vector", ts, ao_tok)
            wait("vector", osA, outdma_prev)
            E("vector", "tensor_tensor", out_sb[:, 0:512], p_o0[:, :], xb[par][:, 0:512], ALU.add)
            E("vector", "tensor_tensor", out_sb[:, 512:1024], p_o1[:, :], xb[par][:, 512:1024],
              ALU.add, inc=(vs, 1))
            outev_tok = vs.n
            if n in (0, 1):
                tap("outA" if n == 0 else "outA1", out_sb[:, :], vs, outev_tok, blocker="gpsimd")

            wait("gpsimd", vs, outev_tok)
            E("gpsimd", "dma_start", out_d[seq, r0:r0 + C, :], out_sb[:, :], inc=(osA, 16))
            outdma_prev = osA.n

            stored_prev = stored_n
            outev_prev = outev_tok
            shat_h3_prev = shat_tok_d[(n, 3)]
            supd_h3_prev = supd_tok_d[(n, 3)]
            gev_pb_prev = gev_toks[3]
            qt_prev = qt_tok
            VEND, AEND, TEND = vs.n, as_.n, ts.n

        phA.close()
        phB = ExitStack()
        hm = sb(phB, "hm_s", [128, 8192], BF16)
        wfT = sb(phB, "wfT", [128, 8192], BF16)
        stgB = [sb(phB, "stgB0", [128, 1024], F32), sb(phB, "stgB1", [128, 1024], F32)]
        xnb = [sb(phB, "xnb0", [128, D], BF16), sb(phB, "xnb1", [128, D], BF16)]
        outrd = [sb(phB, "outrd0", [128, D], F32), sb(phB, "outrd1", [128, D], F32)]
        xr_f = [sb(phB, "xr_f0", [128, D], F32), sb(phB, "xr_f1", [128, D], F32)]
        xq_tm = [sb(phB, "xq_tm0", [128, D], BF16), sb(phB, "xq_tm1", [128, D], BF16)]
        xq_fm = [sb(phB, "xq_fm0", [128, D], BF16), sb(phB, "xq_fm1", [128, D], BF16)]
        out_sb2 = sb(phB, "out_sb2", [128, D], F32)
        wrow = sb(phB, "wrow", [128, D], BF16)
        asum = sb(phB, "asum", [128, 1], F32)
        sv = sb(phB, "sv", [128, 1], F32)
        sinv = sb(phB, "sinv", [128, 1], F32)
        mx0 = sb(phB, "mx0", [128, 2], F32)
        mx1 = sb(phB, "mx1", [128, 2], F32)
        mxc = sb(phB, "mxc", [128, 2], F32)
        r7 = sb(phB, "r7", [128, 2], F32)
        s7 = sb(phB, "s7", [128, 2], F32)
        remB = nc.sbuf_bytes_remaining
        print("SBUF remB:", remB)

        wait("sync", ts, TEND); wait("sync", vs, VEND); wait("sync", as_, AEND)
        wait("sync", osA, osA.n); wait("sync", ss, ss.n)

        wfT_r = wfT[:, :].rearrange("p (a b) -> p a b", b=1024)
        ptr_r = p_tr0[:, :].rearrange("p (a b) -> p a b", b=128)
        wrow_toks = {}
        wfT_ev = 0
        for oc in range(8):
            if oc >= 2:
                wait("sync", vs, wrow_toks[oc - 2])
            E("sync", "dma_start", stgB[oc % 2][:, :], wffn_d[oc * 128:(oc + 1) * 128, :], inc=(wfp[oc % 2], 16))
            wait("vector", wfp[oc % 2], wfp[oc % 2].n)
            if oc == 0:
                wait("vector", ts, TEND)
                wait("vector", as_, AEND)
            E("vector", "tensor_reduce", asum[:, :], stgB[oc % 2][:, :], AX, ALU.add,
              apply_absolute_value=True, inc=(vs, 1))
            wait("vector", vs, vs.n)
            E("vector", "tensor_scalar", sv[:, :], asum[:, :], 1.0 / D, 1e-5,
              ALU.mult, ALU.max, inc=(vs, 1))
            wait("scalar", vs, vs.n)
            if oc == 0:
                wait("scalar", ts, TEND)
            E("scalar", "activation", sinv[:, :], sv[:, :], AF.Ln, bias=0.0, scale=1.0, inc=(as_, 1))
            wait("scalar", as_, as_.n)
            E("scalar", "activation", sinv[:, :], sinv[:, :], AF.Exp, bias=0.0, scale=-1.0, inc=(as_, 1))
            wait("vector", as_, as_.n)
            E("vector", "tensor_scalar", xr_f[0][:, :], stgB[oc % 2][:, :], sinv[:, :], MAGIC,
              ALU.mult, ALU.add)
            E("vector", "tensor_scalar", xr_f[0][:, :], xr_f[0][:, :], MAGIC, -1.0,
              ALU.subtract, ALU.max)
            E("vector", "tensor_scalar", wrow[:, :], xr_f[0][:, :], 1.0, sv[:, :],
              ALU.min, ALU.mult, inc=(vs, 1))
            wrow_toks[oc] = vs.n
            wait("tensor", vs, vs.n)
            if oc == 0:
                wait_t("tensor", trg_ev[0])
            else:
                wait("tensor", vs, wfT_ev)
            for ic in range(8):
                icc = (ts, 1) if ic == 7 else None
                E("tensor", "transpose", p_tr0[:, ic * 128:(ic + 1) * 128],
                  wrow[:, ic * 128:(ic + 1) * 128], idn[:, :], inc=icc)
            wait("vector", ts, ts.n)
            E("vector", "tensor_copy", wfT_r[:, :, oc * 128:(oc + 1) * 128], ptr_r[:, :, :], inc=(vs, 1))
            wfT_ev = vs.n
        for dc in range(8):
            E("sync", "dma_start", hm[:, dc * 1024:(dc + 1) * 1024],
              hmat_d[dc * 128:(dc + 1) * 128, :], inc=(hmS, 16))
        wprep_bs = hmS.n
        wprep_vs = vs.n
        tap("wfT", wfT[:, :], vs, wfT_ev, blocker="tensor")
        tap("hm", hm[:, :], hmS, wprep_bs, blocker="tensor")

        xh_tok = {}; ff_tok = {}; qz_tok = {}; outevB = {}
        xtokB = {}; otokB = {}
        outdmaB_par = [0, 0]
        xqfm_ev_par = [0, 0]
        out_sbB = [out_sb, out_sb2]

        def emit_b_tail(t):
            if t < 0:
                return
            par = t % 2
            seq, r0 = t // TPS, (t % TPS) * C
            trb = p_tr[par]
            wait("tensor", vs, qz_tok[t])
            if t >= 2:
                wait("tensor", vs, xqfm_ev_par[par])
            elif t == 0:
                wait("tensor", vs, wfT_ev)
            for ic in range(8):
                icc = (ts, 1) if ic == 7 else None
                E("tensor", "transpose", trb[:, ic * 128:(ic + 1) * 128],
                  xq_tm[par][:, ic * 128:(ic + 1) * 128], idn[:, :], inc=icc)
            xqT = ts.n
            wait("vector", ts, xqT)
            E("vector", "tensor_copy", xq_fm[par][:, :], trb[:, :], inc=(vs, 1))
            xqfm_ev_par[par] = vs.n
            wait("tensor", vs, xqfm_ev_par[par])
            if t >= 1:
                wait("tensor", vs, outevB[t - 1])
            for half, bank in ((0, p_o0), (1, p_o1)):
                for ic in range(8):
                    icc = (ts, 1) if (half == 1 and ic == 7) else None
                    E("tensor", "matmul", bank[:, :], xq_fm[par][:, ic * 128:(ic + 1) * 128],
                      wfT[:, ic * 1024 + half * 512: ic * 1024 + (half + 1) * 512],
                      start=(ic == 0), stop=(ic == 7), inc=icc)
            ff_tok[t] = ts.n
            wait("vector", ts, ff_tok[t])
            wait_t("vector", otokB[t])
            wait("vector", osB, outdmaB_par[par])
            if t == 0:
                wait("vector", osA, osA.n)
            E("vector", "tensor_tensor", out_sbB[par][:, 0:512], p_o0[:, :], outrd[par][:, 0:512], ALU.add)
            E("vector", "tensor_tensor", out_sbB[par][:, 512:1024], p_o1[:, :], outrd[par][:, 512:1024],
              ALU.add, inc=(vs, 1))
            outevB[t] = vs.n
            if t == 0:
                tap("outB", out_sbB[par][:, :], vs, outevB[t], blocker="gpsimd")
            wait("gpsimd", vs, outevB[t])
            E("gpsimd", "dma_start", out_d[seq, r0:r0 + C, :], out_sbB[par][:, :], inc=(osB, 16))
            outdmaB_par[par] = osB.n

        for t in range(TILES):
            seq, r0 = t // TPS, (t % TPS) * C
            par = t % 2
            banksA = [(p_pa, p_pb), (p_sc, p_ds)][par]
            if t >= 2:
                wait("sync", ts, xh_tok[t - 2])
                wait("sync", vs, outevB.get(t - 2, 0))
            E("sync", "dma_start", xnb[par][:, :], xns_d[t, :, :], inc=(bxn[par], 16))
            xtokB[t] = (bxn[par], bxn[par].n)
            E("sync", "dma_start", outrd[par][:, :], out_d[seq, r0:r0 + C, :], inc=(bor[par], 16))
            otokB[t] = (bor[par], bor[par].n)

            wait_t("tensor", xtokB[t])
            if t == 0:
                wait("tensor", hmS, wprep_bs)
                wait("tensor", vs, wprep_vs)
                wait("tensor", as_, AEND)
            elif t == 1:
                wait("tensor", vs, wprep_vs)
            else:
                wait("tensor", vs, qz_tok[t - 2])
            for half in (0, 1):
                bank = banksA[half]
                for dc in range(8):
                    icc = (ts, 1) if (half == 1 and dc == 7) else None
                    E("tensor", "matmul", bank[:, :], xnb[par][:, dc * 128:(dc + 1) * 128],
                      hm[:, dc * 1024 + half * 512: dc * 1024 + (half + 1) * 512],
                      start=(dc == 0), stop=(dc == 7), inc=icc)
            xh_tok[t] = ts.n

            emit_b_tail(t - 1)

            wait("vector", ts, xh_tok[t])
            E("vector", "reduce_max", mx0[:, par:par + 1], banksA[0][:, :], AX, apply_absolute_value=True)
            E("vector", "reduce_max", mx1[:, par:par + 1], banksA[1][:, :], AX, apply_absolute_value=True, inc=(vs, 1))
            wait("vector", vs, vs.n)
            E("vector", "scalar_tensor_tensor", mxc[:, par:par + 1], mx0[:, par:par + 1], 1e-5,
              mx1[:, par:par + 1], ALU.max, ALU.max, inc=(vs, 1))
            wait("vector", vs, vs.n)
            E("vector", "reciprocal", r7[:, par:par + 1], mxc[:, par:par + 1], inc=(vs, 1))
            wait("vector", vs, vs.n)
            E("vector", "tensor_scalar_mul", s7[:, par:par + 1], mxc[:, par:par + 1], 1.0 / 7.0)
            E("vector", "tensor_scalar_mul", r7[:, par:par + 1], r7[:, par:par + 1], 7.0, inc=(vs, 1))
            wait("vector", vs, vs.n)
            E("vector", "tensor_scalar", xr_f[par][:, 0:512], banksA[0][:, :], r7[:, par:par + 1], MAGIC,
              ALU.mult, ALU.add)
            E("vector", "tensor_scalar", xr_f[par][:, 512:1024], banksA[1][:, :], r7[:, par:par + 1], MAGIC,
              ALU.mult, ALU.add)
            E("vector", "tensor_scalar", xq_tm[par][:, :], xr_f[par][:, :], MAGIC, s7[:, par:par + 1],
              ALU.subtract, ALU.mult, inc=(vs, 1))
            qz_tok[t] = vs.n

        emit_b_tail(TILES - 1)

        wait("gpsimd", osB, osB.n)
        phB.close()

        for s in (xs[0], xs[1], ws2[0], ws2[1], csem, ss, osA, osB,
                  bxn[0], bxn[1], bor[0], bor[1], wfp[0], wfp[1], hmS,
                  ts, vs, as_, gs, dbg_state["sem"]):
            nc.gpsimd.sem_clear(s.h)
        nc.all_engine_barrier()
        with nc.Block() as block:
            @block.sync
            def _(eng):
                for f in S_["sync"]:
                    f(eng)

            @block.tensor
            def _(eng):
                for f in S_["tensor"]:
                    f(eng)

            @block.vector
            def _(eng):
                for f in S_["vector"]:
                    f(eng)

            @block.scalar
            def _(eng):
                for f in S_["scalar"]:
                    f(eng)

            @block.gpsimd
            def _(eng):
                for f in S_["gpsimd"]:
                    f(eng)

    nc.reset()
    return nc


_CACHE = {}


def _hadamard_bf16():
    h = np.array([[1.0]], dtype=np.float64)
    while h.shape[0] < 1024:
        h = np.block([[h, h], [h, -h]])
    return np.ascontiguousarray((h / 32.0).astype(ml_dtypes.bfloat16))


def kernel(L0_attn_x, L0_attn_Wq, L0_attn_Wk, L0_attn_Wv, L0_attn_Wg,
           L0_attn_Wgk1, L0_attn_Wgk2, L0_attn_bgk2, L0_attn_Wo,
           L0_attn_g_norm_weight, L0_ffn_weight):
    x = np.ascontiguousarray(np.asarray(L0_attn_x, dtype=np.float32))
    common = {
        "wq": np.ascontiguousarray(np.asarray(L0_attn_Wq, np.float32)),
        "wk": np.ascontiguousarray(np.asarray(L0_attn_Wk, np.float32)),
        "wv": np.ascontiguousarray(np.asarray(L0_attn_Wv, np.float32)),
        "wg": np.ascontiguousarray(np.asarray(L0_attn_Wg, np.float32)),
        "wgk1": np.ascontiguousarray(np.asarray(L0_attn_Wgk1, np.float32)),
        "wgk2": np.ascontiguousarray(np.asarray(L0_attn_Wgk2, np.float32)),
        "wo": np.ascontiguousarray(np.asarray(L0_attn_Wo, np.float32)),
        "wffn": np.ascontiguousarray(np.asarray(L0_ffn_weight, np.float32)),
        "bgk2c": np.ascontiguousarray(np.asarray(L0_attn_bgk2, np.float32).reshape(8, 128).T),
        "gnwb": np.ascontiguousarray(np.broadcast_to(
            np.tile(np.asarray(L0_attn_g_norm_weight, np.float32), 4)[None, :], (128, 2048))),
        "hmat": _hadamard_bf16(),
        "idn": np.ascontiguousarray(np.eye(128, dtype=ml_dtypes.bfloat16)),
        "maskT": np.ascontiguousarray(
            np.triu(np.ones((128, 128), np.float32)).astype(ml_dtypes.bfloat16)),
    }
    if "nc" not in _CACHE:
        _CACHE["nc"] = build_program()
    nc = _CACHE["nc"]
    in_maps = []
    for c in range(8):
        m = dict(common)
        m["x"] = np.ascontiguousarray(x[2 * c:2 * c + 2])
        in_maps.append(m)
    res = run_bass_kernel_spmd(nc, in_maps, core_ids=list(range(8)))
    return np.concatenate([np.asarray(r["out"], dtype=np.float32) for r in res.results], axis=0)


# version 21
# speedup vs baseline: 1.3894x; 1.0174x over previous
import numpy as np
import ml_dtypes
import concourse.bass as bass
import concourse.mybir as mybir
from concourse.bass_utils import run_bass_kernel_spmd
from contextlib import ExitStack

F32 = mybir.dt.float32
BF16 = mybir.dt.bfloat16
AF = mybir.ActivationFunctionType
ALU = mybir.AluOpType
AX = mybir.AxisListType.X

MAGIC = 1.5 * 2.0 ** 23
LN16 = float(np.log(16.0))
LN7 = float(np.log(7.0))

SEQS = 2
N = 2048
D = 1024
C = 128
TPS = N // C
TILES = SEQS * TPS
HD = 4
DV = 512


def build_program(debug=False):
    nc = bass.Bass("TRN2", target_bir_lowering=False)

    x_d = nc.dram_tensor("x", [SEQS, N, D], F32, kind="ExternalInput")
    wq_d = nc.dram_tensor("wq", [D, D], F32, kind="ExternalInput")
    wk_d = nc.dram_tensor("wk", [D, D], F32, kind="ExternalInput")
    wv_d = nc.dram_tensor("wv", [D, 2048], F32, kind="ExternalInput")
    wg_d = nc.dram_tensor("wg", [D, 2048], F32, kind="ExternalInput")
    wgk1_d = nc.dram_tensor("wgk1", [D, 16], F32, kind="ExternalInput")
    wgk2_d = nc.dram_tensor("wgk2", [16, D], F32, kind="ExternalInput")
    wo_d = nc.dram_tensor("wo", [2048, D], F32, kind="ExternalInput")
    wffn_d = nc.dram_tensor("wffn", [D, D], F32, kind="ExternalInput")
    bgk2c_d = nc.dram_tensor("bgk2c", [128, 8], F32, kind="ExternalInput")
    gnwb_d = nc.dram_tensor("gnwb", [128, 2048], F32, kind="ExternalInput")
    hmat_d = nc.dram_tensor("hmat", [D, D], BF16, kind="ExternalInput")
    idn_d = nc.dram_tensor("idn", [128, 128], BF16, kind="ExternalInput")
    mask_d = nc.dram_tensor("maskT", [128, 128], BF16, kind="ExternalInput")
    out_d = nc.dram_tensor("out", [SEQS, N, D], F32, kind="ExternalOutput")
    xns_d = nc.dram_tensor("xnscr", [TILES, 128, D], BF16, kind="Internal")

    S_ = {"tensor": [], "vector": [], "scalar": [], "sync": [], "gpsimd": []}

    def E(eng, method, *args, inc=None, **kw):
        if inc is not None:
            sem, k = inc
            sem.n += k
            def f(e, m=method, a=args, kv=kw, sh=sem.h, kk=k):
                getattr(e, m)(*a, **kv).then_inc(sh, kk)
        else:
            def f(e, m=method, a=args, kv=kw):
                getattr(e, m)(*a, **kv)
        S_[eng].append(f)

    class Sem:
        def __init__(self, stack, name):
            self.h = stack.enter_context(nc.semaphore(name))
            self.n = 0

    def wait(eng, sem, val):
        if val and val > 0:
            S_[eng].append(lambda e, s=sem.h, v=val: e.wait_ge(s, v))

    def wait_t(eng, tok):
        if tok is not None:
            wait(eng, tok[0], tok[1])

    dbg_tensors = {}
    dbg_state = {"sem": None}

    def tap(name, ap, sem, val, blocker=None):
        if not debug:
            return
        sh = list(ap.shape) if hasattr(ap, "shape") else None
        t = nc.dram_tensor("dbg_" + name, sh, ap.dtype, kind="ExternalOutput")
        dbg_tensors[name] = t
        dsem = dbg_state["sem"]
        wait("gpsimd", sem, val)
        E("gpsimd", "dma_start", t[:, :] if len(sh) == 2 else t[:], ap, inc=(dsem, 16))
        if blocker:
            wait(blocker, dsem, dsem.n)

    with ExitStack() as top:
        xs = [Sem(top, "xs0"), Sem(top, "xs1")]
        ws2 = [Sem(top, "ws0"), Sem(top, "ws1")]
        csem = Sem(top, "csem")
        ss = Sem(top, "ss")
        osA = Sem(top, "osA"); osB = Sem(top, "osB")
        bxn = [Sem(top, "bxn0"), Sem(top, "bxn1")]
        bor = [Sem(top, "bor0"), Sem(top, "bor1")]
        wfp = [Sem(top, "wfp0"), Sem(top, "wfp1")]
        hmS = Sem(top, "hmS")
        ts = Sem(top, "ts"); vs = Sem(top, "vs"); as_ = Sem(top, "as")
        gs = Sem(top, "gs")
        dbg_state["sem"] = Sem(top, "dbg")

        p_tr0 = top.enter_context(nc.psum_tensor("p_tr0", [128, 1024], BF16))
        p_tr1 = top.enter_context(nc.psum_tensor("p_tr1", [128, 1024], BF16))
        p_pa = top.enter_context(nc.psum_tensor("p_pa", [128, 512], F32))
        p_pb = top.enter_context(nc.psum_tensor("p_pb", [128, 512], F32))
        p_sc = top.enter_context(nc.psum_tensor("p_sc", [128, 512], F32))
        p_o0 = top.enter_context(nc.psum_tensor("p_o0", [128, 512], F32))
        p_o1 = top.enter_context(nc.psum_tensor("p_o1", [128, 512], F32))
        p_ds = top.enter_context(nc.psum_tensor("p_ds", [128, 512], F32))
        p_tr = [p_tr0, p_tr1]
        p_o = [p_o0, p_o1]

        def sb(st, name, shape, dt):
            return st.enter_context(nc.sbuf_tensor(name, shape, dt))

        Ssb = sb(top, "S", [128, HD * 2 * DV], F32)
        gnwb = sb(top, "gnwb_s", [128, 2048], BF16)
        negb = sb(top, "negb", [128, 8], F32)
        idn = sb(top, "idn_s", [128, 128], BF16)
        maskT = sb(top, "mask_s", [128, 128], BF16)
        zbf = sb(top, "zbf", [128, 128], BF16)
        padb = sb(top, "padb", [128, 128], BF16)
        pada = sb(top, "pada", [128, 128], BF16)
        cst = sb(top, "cst", [128, 3], F32)
        xb = [sb(top, "xb0", [128, D], F32), sb(top, "xb1", [128, D], F32)]
        out_sb = sb(top, "out_sb", [128, D], F32)

        phA = ExitStack()
        wq = sb(phA, "wq_s", [128, 8192], BF16)
        wk = sb(phA, "wk_s", [128, 8192], BF16)
        wv = sb(phA, "wv_s", [128, 16384], BF16)
        wg = sb(phA, "wg_s", [128, 16384], BF16)
        wo = sb(phA, "wo_s", [128, 16384], BF16)
        wgk1 = sb(phA, "wgk1_s", [128, 128], BF16)
        wgk2 = sb(phA, "wgk2_s", [16, 1024], BF16)

        stg_ctx = ExitStack()
        stg = [sb(stg_ctx, "stg0", [128, 2048], F32), sb(stg_ctx, "stg1", [128, 2048], F32)]

        for dst, src in ((idn, idn_d), (maskT, mask_d), (negb, bgk2c_d)):
            E("sync", "dma_start", dst[:, :], src[:, :], inc=(csem, 16))
        const_tok = csem.n
        E("sync", "dma_start", xb[0][:, :], x_d[0, 0:C, :], inc=(xs[0], 16))
        xtok = {0: (xs[0], xs[0].n)}

        chunks = []
        for dc in range(8):
            chunks.append((wq_d[dc * 128:(dc + 1) * 128, :], wq[:, dc * 1024:(dc + 1) * 1024], 128, 1024))
        for dc in range(8):
            chunks.append((wk_d[dc * 128:(dc + 1) * 128, :], wk[:, dc * 1024:(dc + 1) * 1024], 128, 1024))
        for dc in range(8):
            chunks.append((wgk1_d[dc * 128:(dc + 1) * 128, :], wgk1[:, dc * 16:(dc + 1) * 16], 128, 16))
        chunks.append((wgk2_d[:, :], wgk2[:, :], 16, 1024))
        chunks.append((gnwb_d[:, :], gnwb[:, :], 128, 2048))
        for dc in range(8):
            chunks.append((wv_d[dc * 128:(dc + 1) * 128, :], wv[:, dc * 2048:(dc + 1) * 2048], 128, 2048))
        for dc in range(8):
            chunks.append((wg_d[dc * 128:(dc + 1) * 128, :], wg[:, dc * 2048:(dc + 1) * 2048], 128, 2048))
        for jc in range(16):
            chunks.append((wo_d[jc * 128:(jc + 1) * 128, :], wo[:, jc * 1024:(jc + 1) * 1024], 128, 1024))

        cast_tok = {}
        for i, (dap, sap, r, cc_) in enumerate(chunks):
            if i >= 2:
                wait("sync", vs, cast_tok[i - 2])
            E("sync", "dma_start", stg[i % 2][0:r, 0:cc_], dap, inc=(ws2[i % 2], 16))
            wait("vector", ws2[i % 2], ws2[i % 2].n)
            E("vector", "tensor_copy", sap, stg[i % 2][0:r, 0:cc_], inc=(vs, 1))
            cast_tok[i] = vs.n
        wcast = vs.n
        wait("vector", csem, const_tok)
        E("vector", "tensor_scalar_mul", negb[:, :], negb[:, :], -1.0)
        E("vector", "memset", zbf[:, :], 0.0)
        E("vector", "memset", cst[:, 0:1], 1e-5)
        E("vector", "memset", cst[:, 1:2], -LN16)
        E("vector", "memset", cst[:, 2:3], LN7)
        E("vector", "memset", Ssb[:, :], 0.0)

        stg_ctx.close()

        xn_tm = sb(phA, "xn_tm", [128, D], BF16)
        xn_fm = sb(phA, "xn_fm", [128, D], BF16)
        q_fm = sb(phA, "q_fm", [128, D], BF16)
        k_fm = sb(phA, "k_fm", [128, D], BF16)
        x1_sb = sb(phA, "x1_sb", [16, 128], BF16)
        D_bf = sb(phA, "D_bf", [128, D], BF16)
        EqS = sb(phA, "EqS", [128, D], BF16)
        EkS = sb(phA, "EkS", [128, D], BF16)
        v_tm = sb(phA, "v_tm", [128, 2048], BF16)
        g_tm = sb(phA, "g_tm", [128, 2048], BF16)
        sgA = sb(phA, "sgA", [128, 2048], BF16)
        sgB = sb(phA, "sgB", [128, 512], BF16)
        sbf = sb(phA, "sbf", [128, 2048], BF16)
        shat = sb(phA, "shat", [128, 256], BF16)
        khtm = sb(phA, "khtm", [128, 1024], BF16)
        ggb = sb(phA, "ggb", [128, 2048], BF16)
        otld = sb(phA, "otld", [128, 1024], BF16)
        o_fm = sb(phA, "o_fm", [128, 2048], BF16)
        xsum = sb(phA, "xsum", [128, 1], F32)
        xsq = sb(phA, "xsq", [128, 1], F32)
        meanv = sb(phA, "meanv", [128, 1], F32)
        varv = sb(phA, "varv", [128, 1], F32)
        rstdln = sb(phA, "rstdln", [128, 1], F32)
        tmp1 = sb(phA, "tmp1", [128, 1], F32)
        cendv = sb(phA, "cendv", [128, 8], F32)
        lamv = sb(phA, "lamv", [128, 8], F32)
        ssumv = sb(phA, "ssumv", [128, 4], F32)
        rstdv = sb(phA, "rstdv", [128, 4], F32)
        remA = nc.sbuf_bytes_remaining
        print("SBUF remA:", remA)

        tr_i = 0
        trg_ev = [None, None]
        VEND = AEND = TEND = 0
        stored_prev = 0
        outdma_prev = 0
        xtm_tok = 0
        zhi_prev = 0
        eqek_prev = 0
        outev_prev = 0
        shat_h3_prev = 0
        supd_h3_prev = 0
        gev_pb_prev = 0
        qt_prev = 0
        fold_half_prev = [0, 0]
        fold_toks = [0, 0]
        lnstats_tok = 0
        rstdln_tok = 0
        lnvar_tok = 0
        sc_tok = {}; shat_tok_d = {}; supd_tok_d = {}; ofm_tok_d = {}
        oh_tok = {}; ds_tok = {}; sq_tok = {}; og_tok = {}; oT_tok = {}
        ocp_tok = {}; sqr_tok = {}; scast_tok = {}

        def tr_group():
            nonlocal tr_i
            g = tr_i % 2
            tr_i += 1
            return g

        def emit_gate(n, h, fold_tok):
            if h < 0:
                return
            hp = h % 2
            kp2 = (n, h - 2) if h >= 2 else (n - 1, h + 2)
            wait("vector", as_, sq_tok[(n, h)])
            wait("vector", gs, fold_tok)
            wait("vector", ts, oT_tok.get(kp2, 0))
            E("vector", "scalar_tensor_tensor", otld[:, hp * 512:(hp + 1) * 512],
              g_tm[:, hp * 512:(hp + 1) * 512], rstdv[:, h:h + 1],
              ggb[:, h * 512:(h + 1) * 512], ALU.mult, ALU.mult, inc=(vs, 1))
            og_tok[(n, h)] = vs.n

        oT_grp = {}

        def emit_oT_pe(n, h):
            if h < 0:
                return
            hp = h % 2
            g2 = tr_group()
            oT_grp[(n, h)] = g2
            wait_t("tensor", trg_ev[g2])
            wait("tensor", vs, og_tok[(n, h)])
            for jj in range(4):
                icc = (ts, 1) if jj == 3 else None
                E("tensor", "transpose", p_tr[g2][:, jj * 128:(jj + 1) * 128],
                  otld[:, hp * 512 + jj * 128: hp * 512 + (jj + 1) * 128], idn[:, :], inc=icc)
            oT_tok[(n, h)] = ts.n

        def emit_ofm(n, h):
            if h < 0:
                return
            g2 = oT_grp[(n, h)]
            wait("scalar", ts, oT_tok[(n, h)])
            E("scalar", "copy", o_fm[:, h * 512:(h + 1) * 512], p_tr[g2][:, 0:512], inc=(as_, 1))
            ofm_tok_d[(n, h)] = as_.n
            trg_ev[g2] = (as_, as_.n)

        def emit_ln_stats(tile):
            nonlocal lnstats_tok
            par = tile % 2
            wait_t("vector", xtok[tile])
            wait("vector", osA, outdma_prev)
            E("vector", "tensor_scalar", out_sb[:, :], xb[par][:, :], 1.0, None,
              ALU.mult, ALU.add, accum_out=xsum[:, :])
            E("vector", "scalar_tensor_tensor", out_sb[:, :], xb[par][:, :], 1.0,
              xb[par][:, :], ALU.mult, ALU.mult, accum_out=xsq[:, :], inc=(vs, 1))
            lnstats_tok = vs.n

        def emit_ln_rstd():
            nonlocal rstdln_tok, lnvar_tok
            wait("scalar", vs, lnstats_tok)
            E("scalar", "activation", tmp1[:, :], xsum[:, :], AF.Square, bias=0.0,
              scale=1.0 / D, inc=(as_, 1))
            t_msq = as_.n
            wait("vector", as_, t_msq)
            E("vector", "tensor_scalar_mul", meanv[:, :], xsum[:, :], 1.0 / D)
            E("vector", "scalar_tensor_tensor", varv[:, :], xsq[:, :], 1.0 / D,
              tmp1[:, :], ALU.mult, ALU.subtract, inc=(vs, 1))
            lnvar_tok = vs.n
            wait("scalar", vs, lnvar_tok)
            E("scalar", "activation", rstdln[:, :], varv[:, :], AF.Ln, bias=cst[:, 0:1],
              scale=1.0, inc=(as_, 1))
            wait("scalar", as_, as_.n)
            E("scalar", "activation", rstdln[:, :], rstdln[:, :], AF.Exp, bias=0.0,
              scale=-0.5, inc=(as_, 1))
            rstdln_tok = as_.n

        def emit_ln_apply(tile):
            nonlocal xtm_tok
            par = tile % 2
            wait("vector", as_, rstdln_tok)
            E("vector", "tensor_scalar", xn_tm[:, :], xb[par][:, :], meanv[:, :],
              rstdln[:, :], ALU.subtract, ALU.mult, inc=(vs, 1))
            xtm_tok = vs.n

        emit_ln_stats(0)
        emit_ln_rstd()
        emit_ln_apply(0)

        for n in range(TILES):
            seq, r0 = n // TPS, (n % TPS) * C
            par = n % 2
            pn = n - 1

            if n == TPS:
                E("vector", "memset", Ssb[:, :], 0.0)

            if n + 1 < TILES:
                sq1, rr1 = (n + 1) // TPS, ((n + 1) % TPS) * C
                wait("sync", vs, outev_prev)
                xp = (n + 1) % 2
                E("sync", "dma_start", xb[xp][:, :], x_d[sq1, rr1:rr1 + C, :], inc=(xs[xp], 16))
                xtok[n + 1] = (xs[xp], xs[xp].n)

            g0 = tr_group()
            wait_t("tensor", trg_ev[g0])
            wait("tensor", vs, xtm_tok)
            for kk in range(8):
                icc = (ts, 1) if kk == 7 else None
                E("tensor", "transpose", p_tr[g0][:, kk * 128:(kk + 1) * 128],
                  xn_tm[:, kk * 128:(kk + 1) * 128], idn[:, :], inc=icc)
            xnT_tok = ts.n

            wait("scalar", ts, xnT_tok)
            wait("scalar", ss, stored_prev)
            E("scalar", "copy", xn_fm[:, :], p_tr[g0][:, :], inc=(as_, 1))
            xfm_tok = as_.n
            trg_ev[g0] = (as_, as_.n)

            wait("gpsimd", as_, xfm_tok)
            E("gpsimd", "dma_start", xns_d[n, :, :], xn_fm[:, :], inc=(ss, 16))
            stored_n = ss.n

            qk_banks = [p_sc, p_ds, p_pa, p_pb]
            qk_free = [shat_h3_prev, supd_h3_prev, supd_h3_prev, gev_pb_prev]
            evtoks = {}
            x1ev_tok = 0
            wait("tensor", as_, xfm_tok)
            if n == 0:
                wait("tensor", vs, wcast)
                tap("wq", wq[:, :], vs, wcast)
                tap("wk", wk[:, :], vs, wcast)
                tap("wv", wv[:, :], vs, wcast)
                tap("wg", wg[:, :], vs, wcast)
                tap("wo", wo[:, :], vs, wcast)
                tap("gnwbs", gnwb[:, :], vs, wcast)
            for gi, (name, wsb, dst, off, fcs) in enumerate((
                    ("qA", wq, q_fm, 0, range(0, 4)),
                    ("qB", wq, q_fm, 512, range(4, 8)),
                    ("kA", wk, k_fm, 0, range(0, 4)),
                    ("kB", wk, k_fm, 512, range(4, 8)))):
                bank = qk_banks[gi]
                wait("tensor", vs, qk_free[gi])
                for fi, fc in enumerate(fcs):
                    for dc in range(8):
                        icc = (ts, 1) if (fi == 3 and dc == 7) else None
                        E("tensor", "matmul", bank[:, (fc % 4) * 128:(fc % 4 + 1) * 128],
                          wsb[:, dc * 1024 + fc * 128: dc * 1024 + (fc + 1) * 128],
                          xn_fm[:, dc * 128:(dc + 1) * 128],
                          start=(dc == 0), stop=(dc == 7), inc=icc)
                wait("scalar", ts, ts.n)
                if gi == 0:
                    wait("scalar", ts, oh_tok.get((pn, 3), 0))
                E("scalar", "copy", dst[:, off:off + 512], bank[:, :], inc=(as_, 1))
                evtoks[name] = as_.n
                if gi == 0:
                    wait("tensor", vs, outev_prev)
                    for dc in range(8):
                        icc = (ts, 1) if dc == 7 else None
                        E("tensor", "matmul", p_o0[0:16, 0:128], wgk1[:, dc * 16:(dc + 1) * 16],
                          xn_fm[:, dc * 128:(dc + 1) * 128], start=(dc == 0), stop=(dc == 7), inc=icc)
                    x1_tok = ts.n
                    wait("scalar", ts, x1_tok)
                    wait("scalar", ts, zhi_prev)
                    E("scalar", "copy", x1_sb[:, :], p_o0[0:16, 0:128], inc=(as_, 1))
                    x1ev_tok = as_.n
                if gi == 1:
                    wait("tensor", as_, x1ev_tok)
                    for fc in range(8):
                        bank2 = p_o0 if fc < 4 else p_o1
                        icc = (ts, 1) if fc == 7 else None
                        E("tensor", "matmul", bank2[:, (fc % 4) * 128:(fc % 4 + 1) * 128],
                          wgk2[:, fc * 128:(fc + 1) * 128], x1_sb[:, :], start=True, stop=True, inc=icc)
                    z_tok = ts.n
                    zhi_prev = z_tok

            wait("scalar", ts, z_tok)
            wait("scalar", vs, qt_prev)
            for ch in range(8):
                bank = p_o0 if ch < 4 else p_o1
                E("scalar", "activation", EkS[:, ch * 128:(ch + 1) * 128],
                  bank[:, (ch % 4) * 128:(ch % 4 + 1) * 128], AF.Exp,
                  bias=negb[:, ch:ch + 1], scale=-1.0)
            E("scalar", "activation", D_bf[:, :], EkS[:, :], AF.Ln, bias=1.0, scale=1.0, inc=(as_, 1))
            gates_tok = as_.n

            vev_toks = [0] * 4
            gev_toks = [0] * 4
            for gi in range(8):
                wsb = wg if gi < 4 else wv
                grp = gi % 4
                bank = qk_banks[grp]
                if gi < 4:
                    wait("tensor", as_, evtoks[("qA", "qB", "kA", "kB")[gi]])
                else:
                    wait("tensor", vs, gev_toks[grp])
                for dc in range(8):
                    icc = (ts, 1) if dc == 7 else None
                    E("tensor", "matmul", bank[:, :], xn_fm[:, dc * 128:(dc + 1) * 128],
                      wsb[:, dc * 2048 + grp * 512: dc * 2048 + (grp + 1) * 512],
                      start=(dc == 0), stop=(dc == 7), inc=icc)
                vg_w = ts.n
                if gi >= 4:
                    wait("vector", ts, vg_w)
                    if gi == 4:
                        wait("vector", ts, ds_tok.get((pn, 3), 0))
                    E("vector", "tensor_copy", v_tm[:, grp * 512:(grp + 1) * 512], bank[:, :], inc=(vs, 1))
                    vev_toks[grp] = vs.n
                else:
                    wait("vector", ts, vg_w)
                    if gi == 0:
                        wait("vector", as_, sq_tok.get((pn, 3), 0))
                        wait("vector", gs, fold_half_prev[0])
                    if gi == 2:
                        wait("vector", gs, fold_half_prev[1])
                    E("vector", "tensor_copy", g_tm[:, grp * 512:(grp + 1) * 512], bank[:, :], inc=(vs, 1))
                    gev_toks[grp] = vs.n
                if gi in (2, 4):
                    hh = 0 if gi == 2 else 1
                    lo = hh * 1024
                    wait("scalar", vs, gev_toks[2 * hh + 1])
                    wait("scalar", gs, fold_half_prev[hh])
                    E("scalar", "activation", sgA[:, lo:lo + 1024],
                      g_tm[:, lo:lo + 1024], AF.Exp, bias=0.0, scale=-1.0)
                    E("scalar", "activation", sgA[:, lo:lo + 1024],
                      sgA[:, lo:lo + 1024], AF.Ln, bias=1.0, scale=1.0)
                    E("scalar", "activation", sgA[:, lo:lo + 1024],
                      sgA[:, lo:lo + 1024], AF.Exp, bias=0.0, scale=-1.0, inc=(as_, 1))
                    sig_half = as_.n
                    wait("gpsimd", vs, gev_toks[2 * hh + 1])
                    wait("gpsimd", vs, og_tok.get((pn, 2 * hh + 1), 0))
                    E("gpsimd", "tensor_tensor", ggb[:, lo:lo + 1024],
                      g_tm[:, lo:lo + 1024], gnwb[:, lo:lo + 1024], ALU.mult)
                    wait("gpsimd", as_, sig_half)
                    E("gpsimd", "tensor_tensor", ggb[:, lo:lo + 1024],
                      ggb[:, lo:lo + 1024], sgA[:, lo:lo + 1024], ALU.mult, inc=(gs, 1))
                    fold_toks[hh] = gs.n
                    fold_half_prev[hh] = gs.n
                if gi == 1:
                    wait("vector", as_, gates_tok)
                    wait("vector", as_, eqek_prev)
                    for ch in range(8):
                        icc = (vs, 1) if ch == 7 else None
                        E("vector", "tensor_tensor_scan", EqS[:, ch * 128:(ch + 1) * 128],
                          D_bf[:, ch * 128:(ch + 1) * 128], zbf[:, :], 0.0, ALU.add, ALU.add, inc=icc)
                    for ch in range(8):
                        icc = (vs, 1) if ch == 7 else None
                        E("vector", "tensor_copy", cendv[:, ch:ch + 1],
                          EqS[:, ch * 128 + 127: ch * 128 + 128], inc=icc)
                    scan_tok = vs.n
                    wait("scalar", vs, scan_tok)
                    E("scalar", "activation", lamv[:, :], cendv[:, :], AF.Exp, bias=0.0, scale=-1.0 / 16.0)
                    E("scalar", "activation", D_bf[:, :], EqS[:, :], AF.Exp, bias=cst[:, 1:2], scale=-1.0 / 16.0)
                    E("scalar", "activation", EkS[:, :], EqS[:, :], AF.Exp, bias=0.0, scale=1.0 / 16.0, inc=(as_, 1))
                    eqek_tok = as_.n
                    eqek_prev = eqek_tok
                if gi == 3:
                    wait("vector", ts, oh_tok.get((pn, 3), 0))
                    E("vector", "tensor_copy", sbf[:, 0:1024], Ssb[:, 0:1024])
                    E("vector", "tensor_copy", sbf[:, 1024:2048], Ssb[:, 1024:2048], inc=(vs, 1))
                    sbf_toks = {0: (vs, vs.n), 1: (vs, vs.n)}
                    wait("vector", as_, eqek_tok)
                    E("vector", "tensor_mul", q_fm[:, :], q_fm[:, :], D_bf[:, :])
                    E("vector", "tensor_mul", k_fm[:, :], k_fm[:, :], EkS[:, :])
                    for ch in range(8):
                        icc = (vs, 1) if ch == 7 else None
                        E("vector", "tensor_scalar_mul", D_bf[:, ch * 128:(ch + 1) * 128],
                          k_fm[:, ch * 128:(ch + 1) * 128], lamv[:, ch:ch + 1], inc=icc)
                    qt_tok = vs.n
                    if n == 0:
                        tap("lamv0", lamv[:, :], vs, qt_tok, blocker="scalar")
                        tap("cendv0", cendv[:, :], vs, qt_tok, blocker="scalar")
                        tap("kt0", k_fm[:, :], vs, qt_tok, blocker="tensor")
                    if n == 1:
                        tap("Ssb1", Ssb[:, :], vs, sbf_toks[0][1], blocker="vector")
                        tap("sbf1", sbf[:, :], vs, sbf_toks[0][1], blocker="vector")
                        tap("qt1", q_fm[:, :], vs, qt_tok, blocker="tensor")
                        tap("kt1", k_fm[:, :], vs, qt_tok, blocker="tensor")
                        tap("lamv1", lamv[:, :], vs, qt_tok, blocker="scalar")
                        tap("cum1", EqS[:, :], vs, qt_tok, blocker="scalar")
            sc_free = vev_toks[0]
            supd_last = supd_h3_prev
            khtm_tok = 0
            for h in range(HD):
                hp = h % 2
                c0, c1 = 2 * h, 2 * h + 1
                kprev = (n, h - 2) if h >= 2 else (pn, h + 2)

                wait("tensor", vs, qt_tok)
                wait("tensor", vs, sc_free)
                for ic in range(2):
                    cc_ = (c0, c1)[ic]
                    icc = (ts, 1) if ic == 1 else None
                    E("tensor", "matmul", p_sc[:, 0:128], k_fm[:, cc_ * 128:(cc_ + 1) * 128],
                      q_fm[:, cc_ * 128:(cc_ + 1) * 128], start=(ic == 0), stop=(ic == 1), inc=icc)
                sc_tok[(n, h)] = ts.n

                if h == 0:
                    g1 = tr_group()
                    wait_t("tensor", trg_ev[g1])
                    for cc_ in range(8):
                        icc = (ts, 1) if cc_ == 7 else None
                        E("tensor", "transpose", p_tr[g1][:, cc_ * 128:(cc_ + 1) * 128],
                          D_bf[:, cc_ * 128:(cc_ + 1) * 128], idn[:, :], inc=icc)
                    khT_tok = ts.n

                wait("vector", ts, sc_tok[(n, h)])
                wait("vector", ts, oh_tok.get(kprev, 0))
                E("vector", "tensor_tensor", shat[:, hp * 128:(hp + 1) * 128],
                  p_sc[:, 0:128], maskT[:, :], ALU.mult, inc=(vs, 1))
                shat_tok_d[(n, h)] = vs.n
                sc_free = vs.n
                if n == 1 and h == 0:
                    tap("shat1", shat[:, 0:128], vs, vs.n, blocker="tensor")
                if h == 0:
                    wait("vector", ts, khT_tok)
                    E("vector", "tensor_copy", khtm[:, :], p_tr[g1][:, :], inc=(vs, 1))
                    khtm_tok = vs.n
                    trg_ev[g1] = (vs, vs.n)
                    if n == 0:
                        tap("khtm0", khtm[:, :], vs, khtm_tok, blocker="tensor")

                wait_t("tensor", sbf_toks[hp])
                if h < 2:
                    wait("tensor", as_, gates_tok)
                wait("tensor", vs, ocp_tok.get(kprev, 0))
                for ic in range(2):
                    cc_ = (c0, c1)[ic]
                    E("tensor", "matmul", p_o[hp][:, :], q_fm[:, cc_ * 128:(cc_ + 1) * 128],
                      sbf[:, hp * 1024 + ic * 512: hp * 1024 + (ic + 1) * 512],
                      start=(ic == 0), stop=False)
                wait("tensor", vs, shat_tok_d[(n, h)])
                E("tensor", "matmul", p_o[hp][:, :], shat[:, hp * 128:(hp + 1) * 128],
                  v_tm[:, h * 512:(h + 1) * 512], start=False, stop=True, inc=(ts, 1))
                oh_tok[(n, h)] = ts.n

                wait("vector", ts, oh_tok[(n, h)])
                if h == 0:
                    wait("vector", gs, fold_toks[0])
                if h >= 2:
                    wait("vector", as_, sqr_tok[(n, h - 2)])
                E("vector", "tensor_copy", g_tm[:, hp * 512:(hp + 1) * 512],
                  p_o[hp][:, :], inc=(vs, 1))
                ocp_tok[(n, h)] = vs.n


                wait("scalar", vs, ocp_tok[(n, h)])
                E("scalar", "activation", sgB[:, :], g_tm[:, hp * 512:(hp + 1) * 512],
                  AF.Square, accum_out=ssumv[:, h:h + 1], inc=(as_, 1))
                sqr_tok[(n, h)] = as_.n
                wait("scalar", as_, as_.n)
                E("scalar", "activation", rstdv[:, h:h + 1], ssumv[:, h:h + 1],
                  AF.Ln, bias=cst[:, 0:1], scale=1.0 / DV, inc=(as_, 1))
                wait("scalar", as_, as_.n)
                E("scalar", "activation", rstdv[:, h:h + 1], rstdv[:, h:h + 1],
                  AF.Exp, bias=0.0, scale=-0.5, inc=(as_, 1))
                sq_tok[(n, h)] = as_.n

                if h < 2:
                    wait("scalar", ts, oh_tok[(n, h)])
                    E("scalar", "copy", sbf[:, hp * 1024:(hp + 1) * 1024],
                      Ssb[:, (h + 2) * 1024:(h + 3) * 1024], inc=(as_, 1))
                    sbf_toks[hp] = (as_, as_.n)
                    scast_tok[(n, h + 2)] = as_.n

                wait("tensor", vs, khtm_tok)
                wait("tensor", vs, supd_last)
                if h == 0:
                    wait("tensor", vs, vev_toks[1])
                    wait("tensor", vs, vev_toks[2])
                E("tensor", "matmul", p_ds[:, :], khtm[:, h * 256: h * 256 + 128],
                  v_tm[:, h * 512:(h + 1) * 512], start=True, stop=True)
                E("tensor", "matmul", p_pa[:, :], khtm[:, h * 256 + 128: h * 256 + 256],
                  v_tm[:, h * 512:(h + 1) * 512], start=True, stop=True, inc=(ts, 1))
                ds_tok[(n, h)] = ts.n

                wait("vector", ts, ds_tok[(n, h)])
                if h >= 2:
                    wait("vector", as_, scast_tok[(n, h)])
                E("vector", "scalar_tensor_tensor", Ssb[:, c0 * 512:(c0 + 1) * 512],
                  Ssb[:, c0 * 512:(c0 + 1) * 512], lamv[:, c0:c0 + 1], p_ds[:, :],
                  ALU.mult, ALU.add)
                E("vector", "scalar_tensor_tensor", Ssb[:, c1 * 512:(c1 + 1) * 512],
                  Ssb[:, c1 * 512:(c1 + 1) * 512], lamv[:, c1:c1 + 1], p_pa[:, :],
                  ALU.mult, ALU.add, inc=(vs, 1))
                supd_tok_d[(n, h)] = vs.n
                supd_last = vs.n

                emit_gate(n, h - 1, fold_toks[0 if h - 1 <= 1 else 1])
                emit_oT_pe(n, h - 2)
                emit_ofm(n, h - 2)

            emit_gate(n, 3, fold_toks[1])
            emit_oT_pe(n, 2)
            emit_ofm(n, 2)
            emit_oT_pe(n, 3)
            emit_ofm(n, 3)
            if n + 1 < TILES:
                emit_ln_stats(n + 1)
                emit_ln_rstd()

            if n + 1 < TILES:
                wait("vector", ts, xnT_tok)
                emit_ln_apply(n + 1)

            wait("tensor", as_, ofm_tok_d[(n, 3)])
            wait("tensor", vs, ocp_tok[(n, 2)])
            for jc in range(16):
                icc = (ts, 1) if jc == 15 else None
                E("tensor", "matmul", p_o0[:, :], o_fm[:, jc * 128:(jc + 1) * 128],
                  wo[:, jc * 1024 + 0: jc * 1024 + 512],
                  start=(jc == 0), stop=(jc == 15), inc=icc)
            wait("tensor", vs, ocp_tok[(n, 3)])
            for jc in range(16):
                icc = (ts, 1) if jc == 15 else None
                E("tensor", "matmul", p_o1[:, :], o_fm[:, jc * 128:(jc + 1) * 128],
                  wo[:, jc * 1024 + 512: jc * 1024 + 1024],
                  start=(jc == 0), stop=(jc == 15), inc=icc)
            ao_tok = ts.n

            wait("vector", ts, ao_tok)
            wait("vector", osA, outdma_prev)
            E("vector", "tensor_tensor", out_sb[:, 0:512], p_o0[:, :], xb[par][:, 0:512], ALU.add)
            E("vector", "tensor_tensor", out_sb[:, 512:1024], p_o1[:, :], xb[par][:, 512:1024],
              ALU.add, inc=(vs, 1))
            outev_tok = vs.n
            if n in (0, 1):
                tap("outA" if n == 0 else "outA1", out_sb[:, :], vs, outev_tok, blocker="gpsimd")

            wait("gpsimd", vs, outev_tok)
            E("gpsimd", "dma_start", out_d[seq, r0:r0 + C, :], out_sb[:, :], inc=(osA, 16))
            outdma_prev = osA.n

            stored_prev = stored_n
            outev_prev = outev_tok
            shat_h3_prev = shat_tok_d[(n, 3)]
            supd_h3_prev = supd_tok_d[(n, 3)]
            gev_pb_prev = vev_toks[3]
            qt_prev = qt_tok
            VEND, AEND, TEND = vs.n, as_.n, ts.n

        phA.close()
        phB = ExitStack()
        hm = sb(phB, "hm_s", [128, 8192], BF16)
        wfT = sb(phB, "wfT", [128, 8192], BF16)
        stgB = [sb(phB, "stgB0", [128, 1024], F32), sb(phB, "stgB1", [128, 1024], F32)]
        xnb = [sb(phB, "xnb0", [128, D], BF16), sb(phB, "xnb1", [128, D], BF16)]
        outrd = [sb(phB, "outrd0", [128, D], F32), sb(phB, "outrd1", [128, D], F32)]
        xr_f = [sb(phB, "xr_f0", [128, D], F32), sb(phB, "xr_f1", [128, D], F32)]
        xq_tm = [sb(phB, "xq_tm0", [128, D], BF16), sb(phB, "xq_tm1", [128, D], BF16)]
        xq_fm = [sb(phB, "xq_fm0", [128, D], BF16), sb(phB, "xq_fm1", [128, D], BF16)]
        out_sb2 = sb(phB, "out_sb2", [128, D], F32)
        wrow = sb(phB, "wrow", [128, D], BF16)
        asum = sb(phB, "asum", [128, 1], F32)
        sv = sb(phB, "sv", [128, 1], F32)
        sinv = sb(phB, "sinv", [128, 1], F32)
        mx0 = sb(phB, "mx0", [128, 2], F32)
        mx1 = sb(phB, "mx1", [128, 2], F32)
        mxc = sb(phB, "mxc", [128, 2], F32)
        r7 = sb(phB, "r7", [128, 2], F32)
        s7 = sb(phB, "s7", [128, 2], F32)
        remB = nc.sbuf_bytes_remaining
        print("SBUF remB:", remB)

        wait("sync", ts, TEND); wait("sync", vs, VEND); wait("sync", as_, AEND)
        wait("sync", osA, osA.n); wait("sync", ss, ss.n)

        wfT_r = wfT[:, :].rearrange("p (a b) -> p a b", b=1024)
        ptr_r = p_tr0[:, :].rearrange("p (a b) -> p a b", b=128)
        wrow_toks = {}
        wfT_ev = 0
        for oc in range(8):
            if oc >= 2:
                wait("sync", vs, wrow_toks[oc - 2])
            E("sync", "dma_start", stgB[oc % 2][:, :], wffn_d[oc * 128:(oc + 1) * 128, :], inc=(wfp[oc % 2], 16))
            wait("vector", wfp[oc % 2], wfp[oc % 2].n)
            if oc == 0:
                wait("vector", ts, TEND)
                wait("vector", as_, AEND)
            E("vector", "tensor_reduce", asum[:, :], stgB[oc % 2][:, :], AX, ALU.add,
              apply_absolute_value=True, inc=(vs, 1))
            wait("vector", vs, vs.n)
            E("vector", "tensor_scalar", sv[:, :], asum[:, :], 1.0 / D, 1e-5,
              ALU.mult, ALU.max, inc=(vs, 1))
            wait("scalar", vs, vs.n)
            if oc == 0:
                wait("scalar", ts, TEND)
            E("scalar", "activation", sinv[:, :], sv[:, :], AF.Ln, bias=0.0, scale=1.0, inc=(as_, 1))
            wait("scalar", as_, as_.n)
            E("scalar", "activation", sinv[:, :], sinv[:, :], AF.Exp, bias=0.0, scale=-1.0, inc=(as_, 1))
            wait("vector", as_, as_.n)
            E("vector", "tensor_scalar", xr_f[0][:, :], stgB[oc % 2][:, :], sinv[:, :], MAGIC,
              ALU.mult, ALU.add)
            E("vector", "tensor_scalar", xr_f[0][:, :], xr_f[0][:, :], MAGIC, -1.0,
              ALU.subtract, ALU.max)
            E("vector", "tensor_scalar", wrow[:, :], xr_f[0][:, :], 1.0, sv[:, :],
              ALU.min, ALU.mult, inc=(vs, 1))
            wrow_toks[oc] = vs.n
            wait("tensor", vs, vs.n)
            if oc == 0:
                wait_t("tensor", trg_ev[0])
            else:
                wait("tensor", vs, wfT_ev)
            for ic in range(8):
                icc = (ts, 1) if ic == 7 else None
                E("tensor", "transpose", p_tr0[:, ic * 128:(ic + 1) * 128],
                  wrow[:, ic * 128:(ic + 1) * 128], idn[:, :], inc=icc)
            wait("vector", ts, ts.n)
            E("vector", "tensor_copy", wfT_r[:, :, oc * 128:(oc + 1) * 128], ptr_r[:, :, :], inc=(vs, 1))
            wfT_ev = vs.n
        for dc in range(8):
            E("sync", "dma_start", hm[:, dc * 1024:(dc + 1) * 1024],
              hmat_d[dc * 128:(dc + 1) * 128, :], inc=(hmS, 16))
        wprep_bs = hmS.n
        wprep_vs = vs.n
        tap("wfT", wfT[:, :], vs, wfT_ev, blocker="tensor")
        tap("hm", hm[:, :], hmS, wprep_bs, blocker="tensor")

        xh_tok = {}; ff_tok = {}; qz_tok = {}; outevB = {}
        xtokB = {}; otokB = {}
        outdmaB_par = [0, 0]
        xqfm_ev_par = [0, 0]
        out_sbB = [out_sb, out_sb2]

        def emit_b_tail(t):
            if t < 0:
                return
            par = t % 2
            seq, r0 = t // TPS, (t % TPS) * C
            trb = p_tr[par]
            wait("tensor", vs, qz_tok[t])
            if t >= 2:
                wait("tensor", vs, xqfm_ev_par[par])
            elif t == 0:
                wait("tensor", vs, wfT_ev)
            for ic in range(8):
                icc = (ts, 1) if ic == 7 else None
                E("tensor", "transpose", trb[:, ic * 128:(ic + 1) * 128],
                  xq_tm[par][:, ic * 128:(ic + 1) * 128], idn[:, :], inc=icc)
            xqT = ts.n
            wait("vector", ts, xqT)
            E("vector", "tensor_copy", xq_fm[par][:, :], trb[:, :], inc=(vs, 1))
            xqfm_ev_par[par] = vs.n
            wait("tensor", vs, xqfm_ev_par[par])
            if t >= 1:
                wait("tensor", vs, outevB[t - 1])
            for half, bank in ((0, p_o0), (1, p_o1)):
                for ic in range(8):
                    icc = (ts, 1) if (half == 1 and ic == 7) else None
                    E("tensor", "matmul", bank[:, :], xq_fm[par][:, ic * 128:(ic + 1) * 128],
                      wfT[:, ic * 1024 + half * 512: ic * 1024 + (half + 1) * 512],
                      start=(ic == 0), stop=(ic == 7), inc=icc)
            ff_tok[t] = ts.n
            wait("vector", ts, ff_tok[t])
            wait_t("vector", otokB[t])
            wait("vector", osB, outdmaB_par[par])
            if t == 0:
                wait("vector", osA, osA.n)
            E("vector", "tensor_tensor", out_sbB[par][:, 0:512], p_o0[:, :], outrd[par][:, 0:512], ALU.add)
            E("vector", "tensor_tensor", out_sbB[par][:, 512:1024], p_o1[:, :], outrd[par][:, 512:1024],
              ALU.add, inc=(vs, 1))
            outevB[t] = vs.n
            if t == 0:
                tap("outB", out_sbB[par][:, :], vs, outevB[t], blocker="gpsimd")
            wait("gpsimd", vs, outevB[t])
            E("gpsimd", "dma_start", out_d[seq, r0:r0 + C, :], out_sbB[par][:, :], inc=(osB, 16))
            outdmaB_par[par] = osB.n

        for t in range(TILES):
            seq, r0 = t // TPS, (t % TPS) * C
            par = t % 2
            banksA = [(p_pa, p_pb), (p_sc, p_ds)][par]
            if t >= 2:
                wait("sync", ts, xh_tok[t - 2])
                wait("sync", vs, outevB.get(t - 2, 0))
            E("sync", "dma_start", xnb[par][:, :], xns_d[t, :, :], inc=(bxn[par], 16))
            xtokB[t] = (bxn[par], bxn[par].n)
            E("sync", "dma_start", outrd[par][:, :], out_d[seq, r0:r0 + C, :], inc=(bor[par], 16))
            otokB[t] = (bor[par], bor[par].n)

            wait_t("tensor", xtokB[t])
            if t == 0:
                wait("tensor", hmS, wprep_bs)
                wait("tensor", vs, wprep_vs)
                wait("tensor", as_, AEND)
            elif t == 1:
                wait("tensor", vs, wprep_vs)
            else:
                wait("tensor", vs, qz_tok[t - 2])
            for half in (0, 1):
                bank = banksA[half]
                for dc in range(8):
                    icc = (ts, 1) if (half == 1 and dc == 7) else None
                    E("tensor", "matmul", bank[:, :], xnb[par][:, dc * 128:(dc + 1) * 128],
                      hm[:, dc * 1024 + half * 512: dc * 1024 + (half + 1) * 512],
                      start=(dc == 0), stop=(dc == 7), inc=icc)
            xh_tok[t] = ts.n

            emit_b_tail(t - 1)

            wait("vector", ts, xh_tok[t])
            E("vector", "reduce_max", mx0[:, par:par + 1], banksA[0][:, :], AX, apply_absolute_value=True)
            E("vector", "reduce_max", mx1[:, par:par + 1], banksA[1][:, :], AX, apply_absolute_value=True, inc=(vs, 1))
            wait("vector", vs, vs.n)
            E("vector", "scalar_tensor_tensor", mxc[:, par:par + 1], mx0[:, par:par + 1], 1e-5,
              mx1[:, par:par + 1], ALU.max, ALU.max, inc=(vs, 1))
            wait("vector", vs, vs.n)
            E("vector", "reciprocal", r7[:, par:par + 1], mxc[:, par:par + 1], inc=(vs, 1))
            wait("vector", vs, vs.n)
            E("vector", "tensor_scalar_mul", s7[:, par:par + 1], mxc[:, par:par + 1], 1.0 / 7.0)
            E("vector", "tensor_scalar_mul", r7[:, par:par + 1], r7[:, par:par + 1], 7.0, inc=(vs, 1))
            wait("vector", vs, vs.n)
            E("vector", "tensor_scalar", xr_f[par][:, 0:512], banksA[0][:, :], r7[:, par:par + 1], MAGIC,
              ALU.mult, ALU.add)
            E("vector", "tensor_scalar", xr_f[par][:, 512:1024], banksA[1][:, :], r7[:, par:par + 1], MAGIC,
              ALU.mult, ALU.add)
            E("vector", "tensor_scalar", xq_tm[par][:, :], xr_f[par][:, :], MAGIC, s7[:, par:par + 1],
              ALU.subtract, ALU.mult, inc=(vs, 1))
            qz_tok[t] = vs.n

        emit_b_tail(TILES - 1)

        wait("gpsimd", osB, osB.n)
        phB.close()

        for s in (xs[0], xs[1], ws2[0], ws2[1], csem, ss, osA, osB,
                  bxn[0], bxn[1], bor[0], bor[1], wfp[0], wfp[1], hmS,
                  ts, vs, as_, gs, dbg_state["sem"]):
            nc.gpsimd.sem_clear(s.h)
        nc.all_engine_barrier()
        with nc.Block() as block:
            @block.sync
            def _(eng):
                for f in S_["sync"]:
                    f(eng)

            @block.tensor
            def _(eng):
                for f in S_["tensor"]:
                    f(eng)

            @block.vector
            def _(eng):
                for f in S_["vector"]:
                    f(eng)

            @block.scalar
            def _(eng):
                for f in S_["scalar"]:
                    f(eng)

            @block.gpsimd
            def _(eng):
                for f in S_["gpsimd"]:
                    f(eng)

    nc.reset()
    return nc


_CACHE = {}


def _hadamard_bf16():
    h = np.array([[1.0]], dtype=np.float64)
    while h.shape[0] < 1024:
        h = np.block([[h, h], [h, -h]])
    return np.ascontiguousarray((h / 32.0).astype(ml_dtypes.bfloat16))


def kernel(L0_attn_x, L0_attn_Wq, L0_attn_Wk, L0_attn_Wv, L0_attn_Wg,
           L0_attn_Wgk1, L0_attn_Wgk2, L0_attn_bgk2, L0_attn_Wo,
           L0_attn_g_norm_weight, L0_ffn_weight):
    x = np.ascontiguousarray(np.asarray(L0_attn_x, dtype=np.float32))
    common = {
        "wq": np.ascontiguousarray(np.asarray(L0_attn_Wq, np.float32)),
        "wk": np.ascontiguousarray(np.asarray(L0_attn_Wk, np.float32)),
        "wv": np.ascontiguousarray(np.asarray(L0_attn_Wv, np.float32)),
        "wg": np.ascontiguousarray(np.asarray(L0_attn_Wg, np.float32)),
        "wgk1": np.ascontiguousarray(np.asarray(L0_attn_Wgk1, np.float32)),
        "wgk2": np.ascontiguousarray(np.asarray(L0_attn_Wgk2, np.float32)),
        "wo": np.ascontiguousarray(np.asarray(L0_attn_Wo, np.float32)),
        "wffn": np.ascontiguousarray(np.asarray(L0_ffn_weight, np.float32)),
        "bgk2c": np.ascontiguousarray(np.asarray(L0_attn_bgk2, np.float32).reshape(8, 128).T),
        "gnwb": np.ascontiguousarray(np.broadcast_to(
            np.tile(np.asarray(L0_attn_g_norm_weight, np.float32), 4)[None, :], (128, 2048))),
        "hmat": _hadamard_bf16(),
        "idn": np.ascontiguousarray(np.eye(128, dtype=ml_dtypes.bfloat16)),
        "maskT": np.ascontiguousarray(
            np.triu(np.ones((128, 128), np.float32)).astype(ml_dtypes.bfloat16)),
    }
    if "nc" not in _CACHE:
        _CACHE["nc"] = build_program()
    nc = _CACHE["nc"]
    in_maps = []
    for c in range(8):
        m = dict(common)
        m["x"] = np.ascontiguousarray(x[2 * c:2 * c + 2])
        in_maps.append(m)
    res = run_bass_kernel_spmd(nc, in_maps, core_ids=list(range(8)))
    return np.concatenate([np.asarray(r["out"], dtype=np.float32) for r in res.results], axis=0)


# version 23
# speedup vs baseline: 1.4954x; 1.0763x over previous
import numpy as np
import ml_dtypes
import concourse.bass as bass
import concourse.mybir as mybir
from concourse.bass_utils import run_bass_kernel_spmd
from contextlib import ExitStack

F32 = mybir.dt.float32
BF16 = mybir.dt.bfloat16
AF = mybir.ActivationFunctionType
ALU = mybir.AluOpType
AX = mybir.AxisListType.X

MAGIC = 1.5 * 2.0 ** 23
LN16 = float(np.log(16.0))
LN7 = float(np.log(7.0))

SEQS = 2
N = 2048
D = 1024
C = 128
TPS = N // C
TILES = SEQS * TPS
HD = 4
DV = 512


def build_program(debug=False):
    nc = bass.Bass("TRN2", target_bir_lowering=False)

    x_d = nc.dram_tensor("x", [SEQS, N, D], F32, kind="ExternalInput")
    wq_d = nc.dram_tensor("wq", [D, D], F32, kind="ExternalInput")
    wk_d = nc.dram_tensor("wk", [D, D], F32, kind="ExternalInput")
    wv_d = nc.dram_tensor("wv", [D, 2048], F32, kind="ExternalInput")
    wg_d = nc.dram_tensor("wg", [D, 2048], F32, kind="ExternalInput")
    wgk1_d = nc.dram_tensor("wgk1", [D, 16], F32, kind="ExternalInput")
    wgk2_d = nc.dram_tensor("wgk2", [16, D], F32, kind="ExternalInput")
    wo_d = nc.dram_tensor("wo", [2048, D], F32, kind="ExternalInput")
    wffn_d = nc.dram_tensor("wffn", [D, D], F32, kind="ExternalInput")
    bgk2c_d = nc.dram_tensor("bgk2c", [128, 8], F32, kind="ExternalInput")
    gnwb_d = nc.dram_tensor("gnwb", [128, 2048], F32, kind="ExternalInput")
    hmat_d = nc.dram_tensor("hmat", [D, D], BF16, kind="ExternalInput")
    idn_d = nc.dram_tensor("idn", [128, 128], BF16, kind="ExternalInput")
    mask_d = nc.dram_tensor("maskT", [128, 128], BF16, kind="ExternalInput")
    out_d = nc.dram_tensor("out", [SEQS, N, D], F32, kind="ExternalOutput")
    xns_d = nc.dram_tensor("xnscr", [TILES, 128, D], BF16, kind="Internal")

    S_ = {"tensor": [], "vector": [], "scalar": [], "sync": [], "gpsimd": []}

    def E(eng, method, *args, inc=None, **kw):
        if inc is not None:
            sem, k = inc
            sem.n += k
            def f(e, m=method, a=args, kv=kw, sh=sem.h, kk=k):
                getattr(e, m)(*a, **kv).then_inc(sh, kk)
        else:
            def f(e, m=method, a=args, kv=kw):
                getattr(e, m)(*a, **kv)
        S_[eng].append(f)

    class Sem:
        def __init__(self, stack, name):
            self.h = stack.enter_context(nc.semaphore(name))
            self.n = 0

    def wait(eng, sem, val):
        if val and val > 0:
            S_[eng].append(lambda e, s=sem.h, v=val: e.wait_ge(s, v))

    def wait_t(eng, tok):
        if tok is not None:
            wait(eng, tok[0], tok[1])

    dbg_tensors = {}
    dbg_state = {"sem": None}

    def tap(name, ap, sem, val, blocker=None):
        if not debug:
            return
        sh = list(ap.shape) if hasattr(ap, "shape") else None
        t = nc.dram_tensor("dbg_" + name, sh, ap.dtype, kind="ExternalOutput")
        dbg_tensors[name] = t
        dsem = dbg_state["sem"]
        wait("gpsimd", sem, val)
        E("gpsimd", "dma_start", t[:, :] if len(sh) == 2 else t[:], ap, inc=(dsem, 16))
        if blocker:
            wait(blocker, dsem, dsem.n)

    with ExitStack() as top:
        xs = [Sem(top, "xs0"), Sem(top, "xs1")]
        ws2 = [Sem(top, "ws0"), Sem(top, "ws1")]
        csem = Sem(top, "csem")
        ss = Sem(top, "ss")
        osA = Sem(top, "osA"); osB = Sem(top, "osB")
        bxn = [Sem(top, "bxn0"), Sem(top, "bxn1")]
        bor = [Sem(top, "bor0"), Sem(top, "bor1")]
        wfp = [Sem(top, "wfp0"), Sem(top, "wfp1")]
        hmS = Sem(top, "hmS")
        ts = Sem(top, "ts"); vs = Sem(top, "vs"); as_ = Sem(top, "as")
        gs = Sem(top, "gs")
        dbg_state["sem"] = Sem(top, "dbg")

        p_tr0 = top.enter_context(nc.psum_tensor("p_tr0", [128, 1024], BF16))
        p_tr1 = top.enter_context(nc.psum_tensor("p_tr1", [128, 1024], BF16))
        p_pa = top.enter_context(nc.psum_tensor("p_pa", [128, 512], F32))
        p_pb = top.enter_context(nc.psum_tensor("p_pb", [128, 512], F32))
        p_sc = top.enter_context(nc.psum_tensor("p_sc", [128, 512], F32))
        p_o0 = top.enter_context(nc.psum_tensor("p_o0", [128, 512], F32))
        p_o1 = top.enter_context(nc.psum_tensor("p_o1", [128, 512], F32))
        p_ds = top.enter_context(nc.psum_tensor("p_ds", [128, 512], F32))
        p_tr = [p_tr0, p_tr1]
        p_o = [p_o0, p_o1]

        def sb(st, name, shape, dt):
            return st.enter_context(nc.sbuf_tensor(name, shape, dt))

        Ssb = sb(top, "S", [128, HD * 2 * DV], F32)
        gnwb = sb(top, "gnwb_s", [128, 2048], BF16)
        negb = sb(top, "negb", [128, 8], F32)
        idn = sb(top, "idn_s", [128, 128], BF16)
        maskT = sb(top, "mask_s", [128, 128], BF16)
        zbf = sb(top, "zbf", [128, 128], BF16)
        padb = sb(top, "padb", [128, 128], BF16)
        pada = sb(top, "pada", [128, 128], BF16)
        cst = sb(top, "cst", [128, 3], F32)
        xb = [sb(top, "xb0", [128, D], F32), sb(top, "xb1", [128, D], F32)]
        out_sb = sb(top, "out_sb", [128, D], F32)

        phA = ExitStack()
        wq = sb(phA, "wq_s", [128, 8192], BF16)
        wk = sb(phA, "wk_s", [128, 8192], BF16)
        wv = sb(phA, "wv_s", [128, 16384], BF16)
        wg = sb(phA, "wg_s", [128, 16384], BF16)
        wo = sb(phA, "wo_s", [128, 16384], BF16)
        wgk1 = sb(phA, "wgk1_s", [128, 128], BF16)
        wgk2 = sb(phA, "wgk2_s", [16, 1024], BF16)

        stg_ctx = ExitStack()
        stg = [sb(stg_ctx, "stg0", [128, 2048], F32), sb(stg_ctx, "stg1", [128, 2048], F32)]

        for dst, src in ((idn, idn_d), (maskT, mask_d), (negb, bgk2c_d)):
            E("sync", "dma_start", dst[:, :], src[:, :], inc=(csem, 16))
        const_tok = csem.n
        E("sync", "dma_start", xb[0][:, :], x_d[0, 0:C, :], inc=(xs[0], 16))
        xtok = {0: (xs[0], xs[0].n)}

        chunks = []
        for dc in range(8):
            chunks.append((wq_d[dc * 128:(dc + 1) * 128, :], wq[:, dc * 1024:(dc + 1) * 1024], 128, 1024))
        for dc in range(8):
            chunks.append((wk_d[dc * 128:(dc + 1) * 128, :], wk[:, dc * 1024:(dc + 1) * 1024], 128, 1024))
        for dc in range(8):
            chunks.append((wgk1_d[dc * 128:(dc + 1) * 128, :], wgk1[:, dc * 16:(dc + 1) * 16], 128, 16))
        chunks.append((wgk2_d[:, :], wgk2[:, :], 16, 1024))
        chunks.append((gnwb_d[:, :], gnwb[:, :], 128, 2048))
        for dc in range(8):
            chunks.append((wv_d[dc * 128:(dc + 1) * 128, :], wv[:, dc * 2048:(dc + 1) * 2048], 128, 2048))
        for dc in range(8):
            chunks.append((wg_d[dc * 128:(dc + 1) * 128, :], wg[:, dc * 2048:(dc + 1) * 2048], 128, 2048))
        for jc in range(16):
            chunks.append((wo_d[jc * 128:(jc + 1) * 128, :], wo[:, jc * 1024:(jc + 1) * 1024], 128, 1024))

        cast_tok = {}
        for i, (dap, sap, r, cc_) in enumerate(chunks):
            if i >= 2:
                wait("sync", vs, cast_tok[i - 2])
            E("sync", "dma_start", stg[i % 2][0:r, 0:cc_], dap, inc=(ws2[i % 2], 16))
            wait("vector", ws2[i % 2], ws2[i % 2].n)
            E("vector", "tensor_copy", sap, stg[i % 2][0:r, 0:cc_], inc=(vs, 1))
            cast_tok[i] = vs.n
        wcast = vs.n
        wait("vector", csem, const_tok)
        E("vector", "tensor_scalar_mul", negb[:, :], negb[:, :], -1.0)
        E("vector", "memset", zbf[:, :], 0.0)
        E("vector", "memset", cst[:, 0:1], 1e-5)
        E("vector", "memset", cst[:, 1:2], -LN16)
        E("vector", "memset", cst[:, 2:3], LN7)
        E("vector", "memset", Ssb[:, :], 0.0)

        stg_ctx.close()

        xn_tm = sb(phA, "xn_tm", [128, D], BF16)
        xn_fm = sb(phA, "xn_fm", [128, D], BF16)
        q_fm = sb(phA, "q_fm", [128, D], BF16)
        k_fm = sb(phA, "k_fm", [128, D], BF16)
        x1_sb = sb(phA, "x1_sb", [16, 128], BF16)
        D_bf = sb(phA, "D_bf", [128, D], BF16)
        EqS = sb(phA, "EqS", [128, D], BF16)
        EkS = sb(phA, "EkS", [128, D], BF16)
        v_tm = sb(phA, "v_tm", [128, 2048], BF16)
        g_tm = sb(phA, "g_tm", [128, 2048], BF16)
        sgA = sb(phA, "sgA", [128, 2048], BF16)
        sgB = sb(phA, "sgB", [128, 512], BF16)
        sbf = sb(phA, "sbf", [128, 2048], BF16)
        shat = sb(phA, "shat", [128, 256], BF16)
        khtm = sb(phA, "khtm", [128, 1024], BF16)
        ggb = sb(phA, "ggb", [128, 2048], BF16)
        otld = sb(phA, "otld", [128, 1024], BF16)
        o_fm = sb(phA, "o_fm", [128, 2048], BF16)
        xsum = sb(phA, "xsum", [128, 1], F32)
        xsq = sb(phA, "xsq", [128, 1], F32)
        meanv = sb(phA, "meanv", [128, 1], F32)
        varv = sb(phA, "varv", [128, 1], F32)
        rstdln = sb(phA, "rstdln", [128, 1], F32)
        tmp1 = sb(phA, "tmp1", [128, 1], F32)
        cendv = sb(phA, "cendv", [128, 8], F32)
        lamv = sb(phA, "lamv", [128, 8], F32)
        ssumv = sb(phA, "ssumv", [128, 4], F32)
        rstdv = sb(phA, "rstdv", [128, 4], F32)
        remA = nc.sbuf_bytes_remaining
        print("SBUF remA:", remA)

        tr_i = 0
        trg_ev = [None, None]
        VEND = AEND = TEND = 0
        stored_prev = 0
        outdma_prev = 0
        xtm_tok = 0
        zhi_prev = 0
        eqek_prev = 0
        outev_prev = 0
        shat_h3_prev = 0
        supd_h3_prev = 0
        gev_pb_prev = 0
        qt_prev = 0
        fold_half_prev = [0, 0]
        fold_toks = [0, 0]
        lnstats_tok = 0
        rstdln_tok = 0
        lnvar_tok = 0
        sc_tok = {}; shat_tok_d = {}; supd_tok_d = {}; ofm_tok_d = {}
        oh_tok = {}; ds_tok = {}; sq_tok = {}; og_tok = {}; oT_tok = {}
        ocp_tok = {}; sqr_tok = {}; scast_tok = {}

        def tr_group():
            nonlocal tr_i
            g = tr_i % 2
            tr_i += 1
            return g

        def emit_gate(n, h, fold_tok):
            if h < 0:
                return
            hp = h % 2
            kp2 = (n, h - 2) if h >= 2 else (n - 1, h + 2)
            wait("vector", as_, sq_tok[(n, h)])
            wait("vector", gs, fold_tok)
            wait("vector", ts, oT_tok.get(kp2, 0))
            E("vector", "scalar_tensor_tensor", otld[:, hp * 512:(hp + 1) * 512],
              g_tm[:, hp * 512:(hp + 1) * 512], rstdv[:, h:h + 1],
              ggb[:, h * 512:(h + 1) * 512], ALU.mult, ALU.mult, inc=(vs, 1))
            og_tok[(n, h)] = vs.n

        oT_grp = {}

        def emit_oT_pe(n, h):
            if h < 0:
                return
            hp = h % 2
            g2 = tr_group()
            oT_grp[(n, h)] = g2
            wait_t("tensor", trg_ev[g2])
            wait("tensor", vs, og_tok[(n, h)])
            for jj in range(4):
                icc = (ts, 1) if jj == 3 else None
                E("tensor", "transpose", p_tr[g2][:, jj * 128:(jj + 1) * 128],
                  otld[:, hp * 512 + jj * 128: hp * 512 + (jj + 1) * 128], idn[:, :], inc=icc)
            oT_tok[(n, h)] = ts.n

        def emit_ofm(n, h):
            if h < 0:
                return
            g2 = oT_grp[(n, h)]
            wait("scalar", ts, oT_tok[(n, h)])
            E("scalar", "copy", o_fm[:, h * 512:(h + 1) * 512], p_tr[g2][:, 0:512], inc=(as_, 1))
            ofm_tok_d[(n, h)] = as_.n
            trg_ev[g2] = (as_, as_.n)

        def emit_ln_stats(tile):
            nonlocal lnstats_tok
            par = tile % 2
            wait_t("vector", xtok[tile])
            wait("vector", osA, outdma_prev)
            E("vector", "tensor_scalar", out_sb[:, :], xb[par][:, :], 1.0, None,
              ALU.mult, ALU.add, accum_out=xsum[:, :])
            E("vector", "scalar_tensor_tensor", out_sb[:, :], xb[par][:, :], 1.0,
              xb[par][:, :], ALU.mult, ALU.mult, accum_out=xsq[:, :], inc=(vs, 1))
            lnstats_tok = vs.n

        def emit_ln_rstd():
            nonlocal rstdln_tok, lnvar_tok
            wait("scalar", vs, lnstats_tok)
            E("scalar", "activation", tmp1[:, :], xsum[:, :], AF.Square, bias=0.0,
              scale=1.0 / D, inc=(as_, 1))
            t_msq = as_.n
            wait("vector", as_, t_msq)
            E("vector", "tensor_scalar_mul", meanv[:, :], xsum[:, :], 1.0 / D)
            E("vector", "scalar_tensor_tensor", varv[:, :], xsq[:, :], 1.0 / D,
              tmp1[:, :], ALU.mult, ALU.subtract, inc=(vs, 1))
            lnvar_tok = vs.n
            wait("scalar", vs, lnvar_tok)
            E("scalar", "activation", rstdln[:, :], varv[:, :], AF.Ln, bias=cst[:, 0:1],
              scale=1.0, inc=(as_, 1))
            wait("scalar", as_, as_.n)
            E("scalar", "activation", rstdln[:, :], rstdln[:, :], AF.Exp, bias=0.0,
              scale=-0.5, inc=(as_, 1))
            rstdln_tok = as_.n

        def emit_ln_apply(tile):
            nonlocal xtm_tok
            par = tile % 2
            wait("vector", as_, rstdln_tok)
            E("vector", "tensor_scalar", xn_tm[:, :], xb[par][:, :], meanv[:, :],
              rstdln[:, :], ALU.subtract, ALU.mult, inc=(vs, 1))
            xtm_tok = vs.n

        emit_ln_stats(0)
        emit_ln_rstd()
        emit_ln_apply(0)

        for n in range(TILES):
            seq, r0 = n // TPS, (n % TPS) * C
            par = n % 2
            pn = n - 1

            if n == TPS:
                E("vector", "memset", Ssb[:, :], 0.0)

            if n + 1 < TILES:
                sq1, rr1 = (n + 1) // TPS, ((n + 1) % TPS) * C
                wait("sync", vs, outev_prev)
                xp = (n + 1) % 2
                E("sync", "dma_start", xb[xp][:, :], x_d[sq1, rr1:rr1 + C, :], inc=(xs[xp], 16))
                xtok[n + 1] = (xs[xp], xs[xp].n)

            g0 = tr_group()
            wait_t("tensor", trg_ev[g0])
            wait("tensor", vs, xtm_tok)
            for kk in range(8):
                icc = (ts, 1) if kk == 7 else None
                E("tensor", "transpose", p_tr[g0][:, kk * 128:(kk + 1) * 128],
                  xn_tm[:, kk * 128:(kk + 1) * 128], idn[:, :], inc=icc)
            xnT_tok = ts.n

            wait("scalar", ts, xnT_tok)
            wait("scalar", ss, stored_prev)
            E("scalar", "copy", xn_fm[:, :], p_tr[g0][:, :], inc=(as_, 1))
            xfm_tok = as_.n
            trg_ev[g0] = (as_, as_.n)

            wait("gpsimd", as_, xfm_tok)
            E("gpsimd", "dma_start", xns_d[n, :, :], xn_fm[:, :], inc=(ss, 16))
            stored_n = ss.n

            qk_banks = [p_sc, p_ds, p_pa, p_pb]
            qk_free = [shat_h3_prev, supd_h3_prev, supd_h3_prev, gev_pb_prev]
            evtoks = {}
            x1ev_tok = 0
            wait("tensor", as_, xfm_tok)
            if n == 0:
                wait("tensor", vs, wcast)
                tap("wq", wq[:, :], vs, wcast)
                tap("wk", wk[:, :], vs, wcast)
                tap("wv", wv[:, :], vs, wcast)
                tap("wg", wg[:, :], vs, wcast)
                tap("wo", wo[:, :], vs, wcast)
                tap("gnwbs", gnwb[:, :], vs, wcast)
            for gi, (name, wsb, dst, off, fcs) in enumerate((
                    ("qA", wq, q_fm, 0, range(0, 4)),
                    ("qB", wq, q_fm, 512, range(4, 8)),
                    ("kA", wk, k_fm, 0, range(0, 4)),
                    ("kB", wk, k_fm, 512, range(4, 8)))):
                bank = qk_banks[gi]
                wait("tensor", vs, qk_free[gi])
                for fi, fc in enumerate(fcs):
                    for dc in range(8):
                        icc = (ts, 1) if (fi == 3 and dc == 7) else None
                        E("tensor", "matmul", bank[:, (fc % 4) * 128:(fc % 4 + 1) * 128],
                          wsb[:, dc * 1024 + fc * 128: dc * 1024 + (fc + 1) * 128],
                          xn_fm[:, dc * 128:(dc + 1) * 128],
                          start=(dc == 0), stop=(dc == 7), inc=icc)
                wait("scalar", ts, ts.n)
                if gi == 0:
                    wait("scalar", ts, oh_tok.get((pn, 3), 0))
                E("scalar", "copy", dst[:, off:off + 512], bank[:, :], inc=(as_, 1))
                evtoks[name] = as_.n
                if gi == 0:
                    wait("tensor", vs, outev_prev)
                    for dc in range(8):
                        icc = (ts, 1) if dc == 7 else None
                        E("tensor", "matmul", p_o0[0:16, 0:128], wgk1[:, dc * 16:(dc + 1) * 16],
                          xn_fm[:, dc * 128:(dc + 1) * 128], start=(dc == 0), stop=(dc == 7), inc=icc)
                    x1_tok = ts.n
                    wait("scalar", ts, x1_tok)
                    wait("scalar", ts, zhi_prev)
                    E("scalar", "copy", x1_sb[:, :], p_o0[0:16, 0:128], inc=(as_, 1))
                    x1ev_tok = as_.n
                if gi == 1:
                    wait("tensor", as_, x1ev_tok)
                    for fc in range(8):
                        bank2 = p_o0 if fc < 4 else p_o1
                        icc = (ts, 1) if fc == 7 else None
                        E("tensor", "matmul", bank2[:, (fc % 4) * 128:(fc % 4 + 1) * 128],
                          wgk2[:, fc * 128:(fc + 1) * 128], x1_sb[:, :], start=True, stop=True, inc=icc)
                    z_tok = ts.n
                    zhi_prev = z_tok

            wait("scalar", ts, z_tok)
            wait("scalar", vs, qt_prev)
            for ch in range(8):
                bank = p_o0 if ch < 4 else p_o1
                E("scalar", "activation", EkS[:, ch * 128:(ch + 1) * 128],
                  bank[:, (ch % 4) * 128:(ch % 4 + 1) * 128], AF.Exp,
                  bias=negb[:, ch:ch + 1], scale=-1.0)
            E("scalar", "activation", D_bf[:, :], EkS[:, :], AF.Ln, bias=1.0, scale=1.0, inc=(as_, 1))
            gates_tok = as_.n

            vev_toks = [0] * 4
            gev_toks = [0] * 4
            for gi in range(8):
                wsb = wg if gi < 4 else wv
                grp = gi % 4
                bank = qk_banks[grp]
                if gi < 4:
                    wait("tensor", as_, evtoks[("qA", "qB", "kA", "kB")[gi]])
                else:
                    wait("tensor", vs, gev_toks[grp])
                for dc in range(8):
                    icc = (ts, 1) if dc == 7 else None
                    E("tensor", "matmul", bank[:, :], xn_fm[:, dc * 128:(dc + 1) * 128],
                      wsb[:, dc * 2048 + grp * 512: dc * 2048 + (grp + 1) * 512],
                      start=(dc == 0), stop=(dc == 7), inc=icc)
                vg_w = ts.n
                if gi >= 4:
                    wait("vector", ts, vg_w)
                    if gi == 4:
                        wait("vector", ts, ds_tok.get((pn, 3), 0))
                    E("vector", "tensor_copy", v_tm[:, grp * 512:(grp + 1) * 512], bank[:, :], inc=(vs, 1))
                    vev_toks[grp] = vs.n
                else:
                    wait("vector", ts, vg_w)
                    if gi == 0:
                        wait("vector", as_, sq_tok.get((pn, 3), 0))
                        wait("vector", gs, fold_half_prev[0])
                    if gi == 2:
                        wait("vector", gs, fold_half_prev[1])
                    E("vector", "tensor_copy", g_tm[:, grp * 512:(grp + 1) * 512], bank[:, :], inc=(vs, 1))
                    gev_toks[grp] = vs.n
                if gi in (2, 4):
                    hh = 0 if gi == 2 else 1
                    lo = hh * 1024
                    wait("scalar", vs, gev_toks[2 * hh + 1])
                    wait("scalar", gs, fold_half_prev[hh])
                    E("scalar", "activation", sgA[:, lo:lo + 1024],
                      g_tm[:, lo:lo + 1024], AF.Exp, bias=0.0, scale=-1.0)
                    E("scalar", "activation", sgA[:, lo:lo + 1024],
                      sgA[:, lo:lo + 1024], AF.Ln, bias=1.0, scale=1.0)
                    E("scalar", "activation", sgA[:, lo:lo + 1024],
                      sgA[:, lo:lo + 1024], AF.Exp, bias=0.0, scale=-1.0, inc=(as_, 1))
                    sig_half = as_.n
                    wait("gpsimd", vs, gev_toks[2 * hh + 1])
                    wait("gpsimd", vs, og_tok.get((pn, 2 * hh + 1), 0))
                    E("gpsimd", "tensor_tensor", ggb[:, lo:lo + 1024],
                      g_tm[:, lo:lo + 1024], gnwb[:, lo:lo + 1024], ALU.mult)
                    wait("gpsimd", as_, sig_half)
                    E("gpsimd", "tensor_tensor", ggb[:, lo:lo + 1024],
                      ggb[:, lo:lo + 1024], sgA[:, lo:lo + 1024], ALU.mult, inc=(gs, 1))
                    fold_toks[hh] = gs.n
                    fold_half_prev[hh] = gs.n
                if gi == 1:
                    wait("vector", as_, gates_tok)
                    wait("vector", as_, eqek_prev)
                    for ch in range(8):
                        icc = (vs, 1) if ch == 7 else None
                        E("vector", "tensor_tensor_scan", EqS[:, ch * 128:(ch + 1) * 128],
                          D_bf[:, ch * 128:(ch + 1) * 128], zbf[:, :], 0.0, ALU.add, ALU.add, inc=icc)
                    for ch in range(8):
                        icc = (vs, 1) if ch == 7 else None
                        E("vector", "tensor_copy", cendv[:, ch:ch + 1],
                          EqS[:, ch * 128 + 127: ch * 128 + 128], inc=icc)
                    scan_tok = vs.n
                    wait("scalar", vs, scan_tok)
                    E("scalar", "activation", lamv[:, :], cendv[:, :], AF.Exp, bias=0.0, scale=-1.0 / 16.0)
                    E("scalar", "activation", D_bf[:, :], EqS[:, :], AF.Exp, bias=cst[:, 1:2], scale=-1.0 / 16.0)
                    E("scalar", "activation", EkS[:, :], EqS[:, :], AF.Exp, bias=0.0, scale=1.0 / 16.0, inc=(as_, 1))
                    eqek_tok = as_.n
                    eqek_prev = eqek_tok
                if gi == 3:
                    wait("vector", ts, oh_tok.get((pn, 3), 0))
                    E("vector", "tensor_copy", sbf[:, 0:1024], Ssb[:, 0:1024])
                    E("vector", "tensor_copy", sbf[:, 1024:2048], Ssb[:, 1024:2048], inc=(vs, 1))
                    sbf_toks = {0: (vs, vs.n), 1: (vs, vs.n)}
                    wait("vector", as_, eqek_tok)
                    E("vector", "tensor_mul", q_fm[:, :], q_fm[:, :], D_bf[:, :])
                    E("vector", "tensor_mul", k_fm[:, :], k_fm[:, :], EkS[:, :])
                    for ch in range(8):
                        icc = (vs, 1) if ch == 7 else None
                        E("vector", "tensor_scalar_mul", D_bf[:, ch * 128:(ch + 1) * 128],
                          k_fm[:, ch * 128:(ch + 1) * 128], lamv[:, ch:ch + 1], inc=icc)
                    qt_tok = vs.n
                    if n == 0:
                        tap("lamv0", lamv[:, :], vs, qt_tok, blocker="scalar")
                        tap("cendv0", cendv[:, :], vs, qt_tok, blocker="scalar")
                        tap("kt0", k_fm[:, :], vs, qt_tok, blocker="tensor")
                    if n == 1:
                        tap("Ssb1", Ssb[:, :], vs, sbf_toks[0][1], blocker="vector")
                        tap("sbf1", sbf[:, :], vs, sbf_toks[0][1], blocker="vector")
                        tap("qt1", q_fm[:, :], vs, qt_tok, blocker="tensor")
                        tap("kt1", k_fm[:, :], vs, qt_tok, blocker="tensor")
                        tap("lamv1", lamv[:, :], vs, qt_tok, blocker="scalar")
                        tap("cum1", EqS[:, :], vs, qt_tok, blocker="scalar")
            sc_free = vev_toks[0]
            supd_last = supd_h3_prev
            khtm_tok = 0
            for h in range(HD):
                hp = h % 2
                c0, c1 = 2 * h, 2 * h + 1
                kprev = (n, h - 2) if h >= 2 else (pn, h + 2)

                wait("tensor", vs, qt_tok)
                wait("tensor", vs, sc_free)
                for ic in range(2):
                    cc_ = (c0, c1)[ic]
                    icc = (ts, 1) if ic == 1 else None
                    E("tensor", "matmul", p_sc[:, 0:128], k_fm[:, cc_ * 128:(cc_ + 1) * 128],
                      q_fm[:, cc_ * 128:(cc_ + 1) * 128], start=(ic == 0), stop=(ic == 1), inc=icc)
                sc_tok[(n, h)] = ts.n

                if h == 0:
                    g1 = tr_group()
                    wait_t("tensor", trg_ev[g1])
                    for cc_ in range(8):
                        icc = (ts, 1) if cc_ == 7 else None
                        E("tensor", "transpose", p_tr[g1][:, cc_ * 128:(cc_ + 1) * 128],
                          D_bf[:, cc_ * 128:(cc_ + 1) * 128], idn[:, :], inc=icc)
                    khT_tok = ts.n

                wait("vector", ts, sc_tok[(n, h)])
                wait("vector", ts, oh_tok.get(kprev, 0))
                E("vector", "tensor_tensor", shat[:, hp * 128:(hp + 1) * 128],
                  p_sc[:, 0:128], maskT[:, :], ALU.mult, inc=(vs, 1))
                shat_tok_d[(n, h)] = vs.n
                sc_free = vs.n
                if n == 1 and h == 0:
                    tap("shat1", shat[:, 0:128], vs, vs.n, blocker="tensor")
                if h == 0:
                    wait("vector", ts, khT_tok)
                    E("vector", "tensor_copy", khtm[:, :], p_tr[g1][:, :], inc=(vs, 1))
                    khtm_tok = vs.n
                    trg_ev[g1] = (vs, vs.n)
                    if n == 0:
                        tap("khtm0", khtm[:, :], vs, khtm_tok, blocker="tensor")

                wait_t("tensor", sbf_toks[hp])
                if h < 2:
                    wait("tensor", as_, gates_tok)
                wait("tensor", vs, ocp_tok.get(kprev, 0))
                for ic in range(2):
                    cc_ = (c0, c1)[ic]
                    E("tensor", "matmul", p_o[hp][:, :], q_fm[:, cc_ * 128:(cc_ + 1) * 128],
                      sbf[:, hp * 1024 + ic * 512: hp * 1024 + (ic + 1) * 512],
                      start=(ic == 0), stop=False)
                wait("tensor", vs, shat_tok_d[(n, h)])
                E("tensor", "matmul", p_o[hp][:, :], shat[:, hp * 128:(hp + 1) * 128],
                  v_tm[:, h * 512:(h + 1) * 512], start=False, stop=True, inc=(ts, 1))
                oh_tok[(n, h)] = ts.n

                wait("vector", ts, oh_tok[(n, h)])
                if h == 0:
                    wait("vector", gs, fold_toks[0])
                if h >= 2:
                    wait("vector", as_, sqr_tok[(n, h - 2)])
                E("vector", "tensor_copy", g_tm[:, hp * 512:(hp + 1) * 512],
                  p_o[hp][:, :], inc=(vs, 1))
                ocp_tok[(n, h)] = vs.n


                wait("scalar", vs, ocp_tok[(n, h)])
                E("scalar", "activation", sgB[:, :], g_tm[:, hp * 512:(hp + 1) * 512],
                  AF.Square, accum_out=ssumv[:, h:h + 1], inc=(as_, 1))
                sqr_tok[(n, h)] = as_.n
                wait("scalar", as_, as_.n)
                E("scalar", "activation", rstdv[:, h:h + 1], ssumv[:, h:h + 1],
                  AF.Ln, bias=cst[:, 0:1], scale=1.0 / DV, inc=(as_, 1))
                wait("scalar", as_, as_.n)
                E("scalar", "activation", rstdv[:, h:h + 1], rstdv[:, h:h + 1],
                  AF.Exp, bias=0.0, scale=-0.5, inc=(as_, 1))
                sq_tok[(n, h)] = as_.n

                if h < 2:
                    wait("scalar", ts, oh_tok[(n, h)])
                    E("scalar", "copy", sbf[:, hp * 1024:(hp + 1) * 1024],
                      Ssb[:, (h + 2) * 1024:(h + 3) * 1024], inc=(as_, 1))
                    sbf_toks[hp] = (as_, as_.n)
                    scast_tok[(n, h + 2)] = as_.n

                wait("tensor", vs, khtm_tok)
                wait("tensor", vs, supd_last)
                if h == 0:
                    wait("tensor", vs, vev_toks[1])
                    wait("tensor", vs, vev_toks[2])
                E("tensor", "matmul", p_ds[:, :], khtm[:, h * 256: h * 256 + 128],
                  v_tm[:, h * 512:(h + 1) * 512], start=True, stop=True)
                E("tensor", "matmul", p_pa[:, :], khtm[:, h * 256 + 128: h * 256 + 256],
                  v_tm[:, h * 512:(h + 1) * 512], start=True, stop=True, inc=(ts, 1))
                ds_tok[(n, h)] = ts.n

                wait("vector", ts, ds_tok[(n, h)])
                if h >= 2:
                    wait("vector", as_, scast_tok[(n, h)])
                E("vector", "scalar_tensor_tensor", Ssb[:, c0 * 512:(c0 + 1) * 512],
                  Ssb[:, c0 * 512:(c0 + 1) * 512], lamv[:, c0:c0 + 1], p_ds[:, :],
                  ALU.mult, ALU.add)
                E("vector", "scalar_tensor_tensor", Ssb[:, c1 * 512:(c1 + 1) * 512],
                  Ssb[:, c1 * 512:(c1 + 1) * 512], lamv[:, c1:c1 + 1], p_pa[:, :],
                  ALU.mult, ALU.add, inc=(vs, 1))
                supd_tok_d[(n, h)] = vs.n
                supd_last = vs.n

                emit_gate(n, h - 1, fold_toks[0 if h - 1 <= 1 else 1])
                emit_oT_pe(n, h - 2)
                emit_ofm(n, h - 2)

            emit_gate(n, 3, fold_toks[1])
            emit_oT_pe(n, 2)
            emit_ofm(n, 2)
            emit_oT_pe(n, 3)
            emit_ofm(n, 3)
            if n + 1 < TILES:
                emit_ln_stats(n + 1)
                emit_ln_rstd()
                wait("vector", ts, xnT_tok)
                emit_ln_apply(n + 1)

            wait("tensor", vs, ocp_tok[(n, 2)])
            for jc in range(16):
                if jc % 4 == 0:
                    wait("tensor", as_, ofm_tok_d[(n, jc // 4)])
                icc = (ts, 1) if jc == 15 else None
                E("tensor", "matmul", p_o0[:, :], o_fm[:, jc * 128:(jc + 1) * 128],
                  wo[:, jc * 1024 + 0: jc * 1024 + 512],
                  start=(jc == 0), stop=(jc == 15), inc=icc)
            wait("tensor", vs, ocp_tok[(n, 3)])
            for jc in range(16):
                icc = (ts, 1) if jc == 15 else None
                E("tensor", "matmul", p_o1[:, :], o_fm[:, jc * 128:(jc + 1) * 128],
                  wo[:, jc * 1024 + 512: jc * 1024 + 1024],
                  start=(jc == 0), stop=(jc == 15), inc=icc)
            ao_tok = ts.n

            wait("vector", ts, ao_tok)
            wait("vector", osA, outdma_prev)
            E("vector", "tensor_tensor", out_sb[:, 0:512], p_o0[:, :], xb[par][:, 0:512], ALU.add)
            E("vector", "tensor_tensor", out_sb[:, 512:1024], p_o1[:, :], xb[par][:, 512:1024],
              ALU.add, inc=(vs, 1))
            outev_tok = vs.n
            if n in (0, 1):
                tap("outA" if n == 0 else "outA1", out_sb[:, :], vs, outev_tok, blocker="gpsimd")

            wait("gpsimd", vs, outev_tok)
            E("gpsimd", "dma_start", out_d[seq, r0:r0 + C, :], out_sb[:, :], inc=(osA, 16))
            outdma_prev = osA.n

            stored_prev = stored_n
            outev_prev = outev_tok
            shat_h3_prev = shat_tok_d[(n, 3)]
            supd_h3_prev = supd_tok_d[(n, 3)]
            gev_pb_prev = vev_toks[3]
            qt_prev = qt_tok
            VEND, AEND, TEND = vs.n, as_.n, ts.n

        phA.close()
        phB = ExitStack()
        hm = sb(phB, "hm_s", [128, 8192], BF16)
        wfT = sb(phB, "wfT", [128, 8192], BF16)
        stgB = [sb(phB, "stgB0", [128, 1024], F32), sb(phB, "stgB1", [128, 1024], F32)]
        xnb = [sb(phB, f"xnb{i}", [128, D], BF16) for i in range(4)]
        outrd = [sb(phB, f"outrd{i}", [128, D], F32) for i in range(4)]
        xr_f = [sb(phB, "xr_f0", [128, D], F32), sb(phB, "xr_f1", [128, D], F32)]
        xq_tm = [sb(phB, "xq_tm0", [128, D], BF16), sb(phB, "xq_tm1", [128, D], BF16)]
        xq_fm = [sb(phB, "xq_fm0", [128, D], BF16), sb(phB, "xq_fm1", [128, D], BF16)]
        out_sb2 = sb(phB, "out_sb2", [128, D], F32)
        wrow = sb(phB, "wrow", [128, D], BF16)
        asum = sb(phB, "asum", [128, 1], F32)
        sv = sb(phB, "sv", [128, 1], F32)
        sinv = sb(phB, "sinv", [128, 1], F32)
        mx0 = sb(phB, "mx0", [128, 2], F32)
        mx1 = sb(phB, "mx1", [128, 2], F32)
        mxc = sb(phB, "mxc", [128, 2], F32)
        r7 = sb(phB, "r7", [128, 2], F32)
        s7 = sb(phB, "s7", [128, 2], F32)
        remB = nc.sbuf_bytes_remaining
        print("SBUF remB:", remB)

        wait("sync", ts, TEND); wait("sync", vs, VEND); wait("sync", as_, AEND)
        wait("sync", osA, osA.n); wait("sync", ss, ss.n)

        wfT_r = wfT[:, :].rearrange("p (a b) -> p a b", b=1024)
        ptr_r = p_tr0[:, :].rearrange("p (a b) -> p a b", b=128)
        wrow_toks = {}
        wfT_ev = 0
        for oc in range(8):
            if oc >= 2:
                wait("sync", vs, wrow_toks[oc - 2])
            E("sync", "dma_start", stgB[oc % 2][:, :], wffn_d[oc * 128:(oc + 1) * 128, :], inc=(wfp[oc % 2], 16))
            wait("vector", wfp[oc % 2], wfp[oc % 2].n)
            if oc == 0:
                wait("vector", ts, TEND)
                wait("vector", as_, AEND)
            E("vector", "tensor_reduce", asum[:, :], stgB[oc % 2][:, :], AX, ALU.add,
              apply_absolute_value=True, inc=(vs, 1))
            wait("vector", vs, vs.n)
            E("vector", "tensor_scalar", sv[:, :], asum[:, :], 1.0 / D, 1e-5,
              ALU.mult, ALU.max, inc=(vs, 1))
            wait("scalar", vs, vs.n)
            if oc == 0:
                wait("scalar", ts, TEND)
            E("scalar", "activation", sinv[:, :], sv[:, :], AF.Ln, bias=0.0, scale=1.0, inc=(as_, 1))
            wait("scalar", as_, as_.n)
            E("scalar", "activation", sinv[:, :], sinv[:, :], AF.Exp, bias=0.0, scale=-1.0, inc=(as_, 1))
            wait("vector", as_, as_.n)
            E("vector", "tensor_scalar", xr_f[0][:, :], stgB[oc % 2][:, :], sinv[:, :], MAGIC,
              ALU.mult, ALU.add)
            E("vector", "tensor_scalar", xr_f[0][:, :], xr_f[0][:, :], MAGIC, -1.0,
              ALU.subtract, ALU.max)
            E("vector", "tensor_scalar", wrow[:, :], xr_f[0][:, :], 1.0, sv[:, :],
              ALU.min, ALU.mult, inc=(vs, 1))
            wrow_toks[oc] = vs.n
            wait("tensor", vs, vs.n)
            if oc == 0:
                wait_t("tensor", trg_ev[0])
            else:
                wait("tensor", vs, wfT_ev)
            for ic in range(8):
                icc = (ts, 1) if ic == 7 else None
                E("tensor", "transpose", p_tr0[:, ic * 128:(ic + 1) * 128],
                  wrow[:, ic * 128:(ic + 1) * 128], idn[:, :], inc=icc)
            wait("vector", ts, ts.n)
            E("vector", "tensor_copy", wfT_r[:, :, oc * 128:(oc + 1) * 128], ptr_r[:, :, :], inc=(vs, 1))
            wfT_ev = vs.n
        for dc in range(8):
            E("sync", "dma_start", hm[:, dc * 1024:(dc + 1) * 1024],
              hmat_d[dc * 128:(dc + 1) * 128, :], inc=(hmS, 16))
        wprep_bs = hmS.n
        wprep_vs = vs.n
        tap("wfT", wfT[:, :], vs, wfT_ev, blocker="tensor")
        tap("hm", hm[:, :], hmS, wprep_bs, blocker="tensor")

        xh_tok = {}; ff_tok = {}; qz_tok = {}; outevB = {}
        xtokB = {}; otokB = {}
        outdmaB_par = [0, 0]
        xqfm_ev_par = [0, 0]
        out_sbB = [out_sb, out_sb2]

        def emit_b_tail(t):
            if t < 0:
                return
            par = t % 2
            seq, r0 = t // TPS, (t % TPS) * C
            trb = p_tr[par]
            wait("tensor", vs, qz_tok[t])
            if t >= 2:
                wait("tensor", vs, xqfm_ev_par[par])
            elif t == 0:
                wait("tensor", vs, wfT_ev)
            for ic in range(8):
                icc = (ts, 1) if ic == 7 else None
                E("tensor", "transpose", trb[:, ic * 128:(ic + 1) * 128],
                  xq_tm[par][:, ic * 128:(ic + 1) * 128], idn[:, :], inc=icc)
            xqT = ts.n
            wait("vector", ts, xqT)
            E("vector", "tensor_copy", xq_fm[par][:, :], trb[:, :], inc=(vs, 1))
            xqfm_ev_par[par] = vs.n
            wait("tensor", vs, xqfm_ev_par[par])
            if t >= 1:
                wait("tensor", vs, outevB[t - 1])
            for half, bank in ((0, p_o0), (1, p_o1)):
                for ic in range(8):
                    icc = (ts, 1) if (half == 1 and ic == 7) else None
                    E("tensor", "matmul", bank[:, :], xq_fm[par][:, ic * 128:(ic + 1) * 128],
                      wfT[:, ic * 1024 + half * 512: ic * 1024 + (half + 1) * 512],
                      start=(ic == 0), stop=(ic == 7), inc=icc)
            ff_tok[t] = ts.n
            wait("vector", ts, ff_tok[t])
            wait_t("vector", otokB[t])
            wait("vector", osB, outdmaB_par[par])
            if t == 0:
                wait("vector", osA, osA.n)
            E("vector", "tensor_tensor", out_sbB[par][:, 0:512], p_o0[:, :], outrd[t % 4][:, 0:512], ALU.add)
            E("vector", "tensor_tensor", out_sbB[par][:, 512:1024], p_o1[:, :], outrd[t % 4][:, 512:1024],
              ALU.add, inc=(vs, 1))
            outevB[t] = vs.n
            if t == 0:
                tap("outB", out_sbB[par][:, :], vs, outevB[t], blocker="gpsimd")
            wait("gpsimd", vs, outevB[t])
            E("gpsimd", "dma_start", out_d[seq, r0:r0 + C, :], out_sbB[par][:, :], inc=(osB, 16))
            outdmaB_par[par] = osB.n

        def emit_b_load(t):
            s4, sp = t % 4, t % 2
            seq2, r02 = t // TPS, (t % TPS) * C
            if t >= 2:
                wait("sync", bxn[sp], xtokB[t - 2][1])
                wait("sync", bor[sp], otokB[t - 2][1])
            if t >= 4:
                wait("sync", ts, xh_tok[t - 4])
                wait("sync", vs, outevB[t - 4])
            E("sync", "dma_start", xnb[s4][:, :], xns_d[t, :, :], inc=(bxn[sp], 16))
            xtokB[t] = (bxn[sp], bxn[sp].n)
            E("sync", "dma_start", outrd[s4][:, :], out_d[seq2, r02:r02 + C, :], inc=(bor[sp], 16))
            otokB[t] = (bor[sp], bor[sp].n)

        emit_b_load(0)
        emit_b_load(1)
        for t in range(TILES):
            seq, r0 = t // TPS, (t % TPS) * C
            par = t % 2
            banksA = [(p_pa, p_pb), (p_sc, p_ds)][par]
            if t + 2 < TILES:
                emit_b_load(t + 2)

            wait_t("tensor", xtokB[t])
            if t == 0:
                wait("tensor", hmS, wprep_bs)
                wait("tensor", vs, wprep_vs)
                wait("tensor", as_, AEND)
            elif t == 1:
                wait("tensor", vs, wprep_vs)
            else:
                wait("tensor", vs, qz_tok[t - 2])
            for half in (0, 1):
                bank = banksA[half]
                for dc in range(8):
                    icc = (ts, 1) if (half == 1 and dc == 7) else None
                    E("tensor", "matmul", bank[:, :], xnb[t % 4][:, dc * 128:(dc + 1) * 128],
                      hm[:, dc * 1024 + half * 512: dc * 1024 + (half + 1) * 512],
                      start=(dc == 0), stop=(dc == 7), inc=icc)
            xh_tok[t] = ts.n

            emit_b_tail(t - 1)

            wait("vector", ts, xh_tok[t])
            E("vector", "reduce_max", mx0[:, par:par + 1], banksA[0][:, :], AX, apply_absolute_value=True)
            E("vector", "reduce_max", mx1[:, par:par + 1], banksA[1][:, :], AX, apply_absolute_value=True, inc=(vs, 1))
            wait("vector", vs, vs.n)
            E("vector", "scalar_tensor_tensor", mxc[:, par:par + 1], mx0[:, par:par + 1], 1e-5,
              mx1[:, par:par + 1], ALU.max, ALU.max, inc=(vs, 1))
            wait("vector", vs, vs.n)
            E("vector", "reciprocal", r7[:, par:par + 1], mxc[:, par:par + 1], inc=(vs, 1))
            wait("vector", vs, vs.n)
            E("vector", "tensor_scalar_mul", s7[:, par:par + 1], mxc[:, par:par + 1], 1.0 / 7.0)
            E("vector", "tensor_scalar_mul", r7[:, par:par + 1], r7[:, par:par + 1], 7.0, inc=(vs, 1))
            wait("vector", vs, vs.n)
            E("vector", "tensor_scalar", xr_f[par][:, 0:512], banksA[0][:, :], r7[:, par:par + 1], MAGIC,
              ALU.mult, ALU.add)
            E("vector", "tensor_scalar", xr_f[par][:, 512:1024], banksA[1][:, :], r7[:, par:par + 1], MAGIC,
              ALU.mult, ALU.add)
            E("vector", "tensor_scalar", xq_tm[par][:, :], xr_f[par][:, :], MAGIC, s7[:, par:par + 1],
              ALU.subtract, ALU.mult, inc=(vs, 1))
            qz_tok[t] = vs.n

        emit_b_tail(TILES - 1)

        wait("gpsimd", osB, osB.n)
        phB.close()

        for s in (xs[0], xs[1], ws2[0], ws2[1], csem, ss, osA, osB,
                  bxn[0], bxn[1], bor[0], bor[1], wfp[0], wfp[1], hmS,
                  ts, vs, as_, gs, dbg_state["sem"]):
            nc.gpsimd.sem_clear(s.h)
        nc.all_engine_barrier()
        with nc.Block() as block:
            @block.sync
            def _(eng):
                for f in S_["sync"]:
                    f(eng)

            @block.tensor
            def _(eng):
                for f in S_["tensor"]:
                    f(eng)

            @block.vector
            def _(eng):
                for f in S_["vector"]:
                    f(eng)

            @block.scalar
            def _(eng):
                for f in S_["scalar"]:
                    f(eng)

            @block.gpsimd
            def _(eng):
                for f in S_["gpsimd"]:
                    f(eng)

    nc.reset()
    return nc


_CACHE = {}


def _hadamard_bf16():
    h = np.array([[1.0]], dtype=np.float64)
    while h.shape[0] < 1024:
        h = np.block([[h, h], [h, -h]])
    return np.ascontiguousarray((h / 32.0).astype(ml_dtypes.bfloat16))


def kernel(L0_attn_x, L0_attn_Wq, L0_attn_Wk, L0_attn_Wv, L0_attn_Wg,
           L0_attn_Wgk1, L0_attn_Wgk2, L0_attn_bgk2, L0_attn_Wo,
           L0_attn_g_norm_weight, L0_ffn_weight):
    x = np.ascontiguousarray(np.asarray(L0_attn_x, dtype=np.float32))
    common = {
        "wq": np.ascontiguousarray(np.asarray(L0_attn_Wq, np.float32)),
        "wk": np.ascontiguousarray(np.asarray(L0_attn_Wk, np.float32)),
        "wv": np.ascontiguousarray(np.asarray(L0_attn_Wv, np.float32)),
        "wg": np.ascontiguousarray(np.asarray(L0_attn_Wg, np.float32)),
        "wgk1": np.ascontiguousarray(np.asarray(L0_attn_Wgk1, np.float32)),
        "wgk2": np.ascontiguousarray(np.asarray(L0_attn_Wgk2, np.float32)),
        "wo": np.ascontiguousarray(np.asarray(L0_attn_Wo, np.float32)),
        "wffn": np.ascontiguousarray(np.asarray(L0_ffn_weight, np.float32)),
        "bgk2c": np.ascontiguousarray(np.asarray(L0_attn_bgk2, np.float32).reshape(8, 128).T),
        "gnwb": np.ascontiguousarray(np.broadcast_to(
            np.tile(np.asarray(L0_attn_g_norm_weight, np.float32), 4)[None, :], (128, 2048))),
        "hmat": _hadamard_bf16(),
        "idn": np.ascontiguousarray(np.eye(128, dtype=ml_dtypes.bfloat16)),
        "maskT": np.ascontiguousarray(
            np.triu(np.ones((128, 128), np.float32)).astype(ml_dtypes.bfloat16)),
    }
    if "nc" not in _CACHE:
        _CACHE["nc"] = build_program()
    nc = _CACHE["nc"]
    in_maps = []
    for c in range(8):
        m = dict(common)
        m["x"] = np.ascontiguousarray(x[2 * c:2 * c + 2])
        in_maps.append(m)
    res = run_bass_kernel_spmd(nc, in_maps, core_ids=list(range(8)))
    return np.concatenate([np.asarray(r["out"], dtype=np.float32) for r in res.results], axis=0)


# version 24 (final)
# speedup vs baseline: 1.5904x; 1.0635x over previous
import numpy as np
import ml_dtypes
import concourse.bass as bass
import concourse.mybir as mybir
from concourse.bass_utils import run_bass_kernel_spmd
from contextlib import ExitStack

F32 = mybir.dt.float32
BF16 = mybir.dt.bfloat16
AF = mybir.ActivationFunctionType
ALU = mybir.AluOpType
AX = mybir.AxisListType.X

MAGIC = 1.5 * 2.0 ** 23
LN16 = float(np.log(16.0))
LN7 = float(np.log(7.0))

SEQS = 2
N = 2048
D = 1024
C = 128
TPS = N // C
TILES = SEQS * TPS
HD = 4
DV = 512


def build_program(debug=False):
    nc = bass.Bass("TRN2", target_bir_lowering=False)

    x_d = nc.dram_tensor("x", [SEQS, N, D], F32, kind="ExternalInput")
    wq_d = nc.dram_tensor("wq", [D, D], F32, kind="ExternalInput")
    wk_d = nc.dram_tensor("wk", [D, D], F32, kind="ExternalInput")
    wv_d = nc.dram_tensor("wv", [D, 2048], F32, kind="ExternalInput")
    wg_d = nc.dram_tensor("wg", [D, 2048], F32, kind="ExternalInput")
    wgk1_d = nc.dram_tensor("wgk1", [D, 16], F32, kind="ExternalInput")
    wgk2_d = nc.dram_tensor("wgk2", [16, D], F32, kind="ExternalInput")
    wo_d = nc.dram_tensor("wo", [2048, D], F32, kind="ExternalInput")
    wffn_d = nc.dram_tensor("wffn", [D, D], F32, kind="ExternalInput")
    bgk2c_d = nc.dram_tensor("bgk2c", [128, 8], F32, kind="ExternalInput")
    gnwb_d = nc.dram_tensor("gnwb", [128, 2048], F32, kind="ExternalInput")
    hmat_d = nc.dram_tensor("hmat", [D, D], BF16, kind="ExternalInput")
    idn_d = nc.dram_tensor("idn", [128, 128], BF16, kind="ExternalInput")
    mask_d = nc.dram_tensor("maskT", [128, 128], BF16, kind="ExternalInput")
    out_d = nc.dram_tensor("out", [SEQS, N, D], F32, kind="ExternalOutput")
    xns_d = nc.dram_tensor("xnscr", [TILES, 128, D], BF16, kind="Internal")

    S_ = {"tensor": [], "vector": [], "scalar": [], "sync": [], "gpsimd": []}

    def E(eng, method, *args, inc=None, **kw):
        if inc is not None:
            sem, k = inc
            sem.n += k
            def f(e, m=method, a=args, kv=kw, sh=sem.h, kk=k):
                getattr(e, m)(*a, **kv).then_inc(sh, kk)
        else:
            def f(e, m=method, a=args, kv=kw):
                getattr(e, m)(*a, **kv)
        S_[eng].append(f)

    class Sem:
        def __init__(self, stack, name):
            self.h = stack.enter_context(nc.semaphore(name))
            self.n = 0

    def wait(eng, sem, val):
        if val and val > 0:
            S_[eng].append(lambda e, s=sem.h, v=val: e.wait_ge(s, v))

    def wait_t(eng, tok):
        if tok is not None:
            wait(eng, tok[0], tok[1])

    dbg_tensors = {}
    dbg_state = {"sem": None}

    def tap(name, ap, sem, val, blocker=None):
        if not debug:
            return
        sh = list(ap.shape) if hasattr(ap, "shape") else None
        t = nc.dram_tensor("dbg_" + name, sh, ap.dtype, kind="ExternalOutput")
        dbg_tensors[name] = t
        dsem = dbg_state["sem"]
        wait("gpsimd", sem, val)
        E("gpsimd", "dma_start", t[:, :] if len(sh) == 2 else t[:], ap, inc=(dsem, 16))
        if blocker:
            wait(blocker, dsem, dsem.n)

    with ExitStack() as top:
        xs = [Sem(top, "xs0"), Sem(top, "xs1")]
        ws2 = [Sem(top, "ws0"), Sem(top, "ws1")]
        csem = Sem(top, "csem")
        ss = Sem(top, "ss")
        osA = Sem(top, "osA"); osB = Sem(top, "osB")
        bxn = [Sem(top, "bxn0"), Sem(top, "bxn1")]
        bor = [Sem(top, "bor0"), Sem(top, "bor1")]
        wfp = [Sem(top, "wfp0"), Sem(top, "wfp1")]
        hmS = Sem(top, "hmS")
        ts = Sem(top, "ts"); vs = Sem(top, "vs"); as_ = Sem(top, "as")
        gs = Sem(top, "gs")
        dbg_state["sem"] = Sem(top, "dbg")

        p_tr0 = top.enter_context(nc.psum_tensor("p_tr0", [128, 1024], BF16))
        p_tr1 = top.enter_context(nc.psum_tensor("p_tr1", [128, 1024], BF16))
        p_pa = top.enter_context(nc.psum_tensor("p_pa", [128, 512], F32))
        p_pb = top.enter_context(nc.psum_tensor("p_pb", [128, 512], F32))
        p_sc = top.enter_context(nc.psum_tensor("p_sc", [128, 512], F32))
        p_o0 = top.enter_context(nc.psum_tensor("p_o0", [128, 512], F32))
        p_o1 = top.enter_context(nc.psum_tensor("p_o1", [128, 512], F32))
        p_ds = top.enter_context(nc.psum_tensor("p_ds", [128, 512], F32))
        p_tr = [p_tr0, p_tr1]
        p_o = [p_o0, p_o1]

        def sb(st, name, shape, dt):
            return st.enter_context(nc.sbuf_tensor(name, shape, dt))

        Ssb = sb(top, "S", [128, HD * 2 * DV], F32)
        gnwb = sb(top, "gnwb_s", [128, 2048], BF16)
        negb = sb(top, "negb", [128, 8], F32)
        idn = sb(top, "idn_s", [128, 128], BF16)
        maskT = sb(top, "mask_s", [128, 128], BF16)
        zbf = sb(top, "zbf", [128, 128], BF16)
        padb = sb(top, "padb", [128, 128], BF16)
        pada = sb(top, "pada", [128, 128], BF16)
        cst = sb(top, "cst", [128, 3], F32)
        xb = [sb(top, "xb0", [128, D], F32), sb(top, "xb1", [128, D], F32)]
        out_sb = sb(top, "out_sb", [128, D], F32)

        phA = ExitStack()
        wq = sb(phA, "wq_s", [128, 8192], BF16)
        wk = sb(phA, "wk_s", [128, 8192], BF16)
        wv = sb(phA, "wv_s", [128, 16384], BF16)
        wg = sb(phA, "wg_s", [128, 16384], BF16)
        wo = sb(phA, "wo_s", [128, 16384], BF16)
        wgk1 = sb(phA, "wgk1_s", [128, 128], BF16)
        wgk2 = sb(phA, "wgk2_s", [16, 1024], BF16)

        stg_ctx = ExitStack()
        stg = [sb(stg_ctx, "stg0", [128, 2048], F32), sb(stg_ctx, "stg1", [128, 2048], F32)]

        for dst, src in ((idn, idn_d), (maskT, mask_d), (negb, bgk2c_d)):
            E("sync", "dma_start", dst[:, :], src[:, :], inc=(csem, 16))
        const_tok = csem.n
        E("sync", "dma_start", xb[0][:, :], x_d[0, 0:C, :], inc=(xs[0], 16))
        xtok = {0: (xs[0], xs[0].n)}

        chunks = []
        for dc in range(8):
            chunks.append((wq_d[dc * 128:(dc + 1) * 128, :], wq[:, dc * 1024:(dc + 1) * 1024], 128, 1024))
        for dc in range(8):
            chunks.append((wk_d[dc * 128:(dc + 1) * 128, :], wk[:, dc * 1024:(dc + 1) * 1024], 128, 1024))
        for dc in range(8):
            chunks.append((wgk1_d[dc * 128:(dc + 1) * 128, :], wgk1[:, dc * 16:(dc + 1) * 16], 128, 16))
        chunks.append((wgk2_d[:, :], wgk2[:, :], 16, 1024))
        chunks.append((gnwb_d[:, :], gnwb[:, :], 128, 2048))
        for dc in range(8):
            chunks.append((wv_d[dc * 128:(dc + 1) * 128, :], wv[:, dc * 2048:(dc + 1) * 2048], 128, 2048))
        for dc in range(8):
            chunks.append((wg_d[dc * 128:(dc + 1) * 128, :], wg[:, dc * 2048:(dc + 1) * 2048], 128, 2048))
        for jc in range(16):
            chunks.append((wo_d[jc * 128:(jc + 1) * 128, :], wo[:, jc * 1024:(jc + 1) * 1024], 128, 1024))

        cast_tok = {}
        for i, (dap, sap, r, cc_) in enumerate(chunks):
            if i >= 2:
                wait("sync", vs, cast_tok[i - 2])
            E("sync", "dma_start", stg[i % 2][0:r, 0:cc_], dap, inc=(ws2[i % 2], 16))
            wait("vector", ws2[i % 2], ws2[i % 2].n)
            E("vector", "tensor_copy", sap, stg[i % 2][0:r, 0:cc_], inc=(vs, 1))
            cast_tok[i] = vs.n
        wcast = vs.n
        wait("vector", csem, const_tok)
        E("vector", "tensor_scalar_mul", negb[:, :], negb[:, :], -1.0)
        E("vector", "memset", zbf[:, :], 0.0)
        E("vector", "memset", cst[:, 0:1], 1e-5)
        E("vector", "memset", cst[:, 1:2], -LN16)
        E("vector", "memset", cst[:, 2:3], LN7)
        E("vector", "memset", Ssb[:, :], 0.0)

        stg_ctx.close()

        xn_tm = sb(phA, "xn_tm", [128, D], BF16)
        xn_fm = sb(phA, "xn_fm", [128, D], BF16)
        q_fm = sb(phA, "q_fm", [128, D], BF16)
        k_fm = sb(phA, "k_fm", [128, D], BF16)
        x1_sb = sb(phA, "x1_sb", [16, 128], BF16)
        D_bf = sb(phA, "D_bf", [128, D], BF16)
        EqS = sb(phA, "EqS", [128, D], BF16)
        EkS = sb(phA, "EkS", [128, D], BF16)
        v_tm = sb(phA, "v_tm", [128, 2048], BF16)
        g_tm = sb(phA, "g_tm", [128, 2048], BF16)
        sgA = sb(phA, "sgA", [128, 2048], BF16)
        sgB = sb(phA, "sgB", [128, 512], BF16)
        sbf = sb(phA, "sbf", [128, 2048], BF16)
        shat = sb(phA, "shat", [128, 256], BF16)
        khtm = sb(phA, "khtm", [128, 1024], BF16)
        ggb = sb(phA, "ggb", [128, 2048], BF16)
        otld = sb(phA, "otld", [128, 1024], BF16)
        o_fm = sb(phA, "o_fm", [128, 2048], BF16)
        xsum = sb(phA, "xsum", [128, 1], F32)
        xsq = sb(phA, "xsq", [128, 1], F32)
        meanv = sb(phA, "meanv", [128, 1], F32)
        varv = sb(phA, "varv", [128, 1], F32)
        rstdln = sb(phA, "rstdln", [128, 1], F32)
        tmp1 = sb(phA, "tmp1", [128, 1], F32)
        cendv = sb(phA, "cendv", [128, 8], F32)
        lamv = sb(phA, "lamv", [128, 8], F32)
        ssumv = sb(phA, "ssumv", [128, 4], F32)
        rstdv = sb(phA, "rstdv", [128, 4], F32)
        remA = nc.sbuf_bytes_remaining
        print("SBUF remA:", remA)

        tr_i = 0
        trg_ev = [None, None]
        VEND = AEND = TEND = 0
        stored_prev = 0
        outdma_prev = 0
        xtm_tok = 0
        zhi_prev = 0
        eqek_prev = 0
        outev_prev = 0
        shat_h3_prev = 0
        supd_h3_prev = 0
        gev_pb_prev = 0
        qt_prev = 0
        fold_half_prev = [0, 0]
        fold_toks = [0, 0]
        lnstats_tok = 0
        rstdln_tok = 0
        lnvar_tok = 0
        sc_tok = {}; shat_tok_d = {}; supd_tok_d = {}; ofm_tok_d = {}
        oh_tok = {}; ds_tok = {}; sq_tok = {}; og_tok = {}; oT_tok = {}
        ocp_tok = {}; sqr_tok = {}; scast_tok = {}

        def tr_group():
            nonlocal tr_i
            g = tr_i % 2
            tr_i += 1
            return g

        def emit_gate(n, h, fold_tok):
            if h < 0:
                return
            hp = h % 2
            kp2 = (n, h - 2) if h >= 2 else (n - 1, h + 2)
            wait("vector", as_, sq_tok[(n, h)])
            wait("vector", gs, fold_tok)
            wait("vector", ts, oT_tok.get(kp2, 0))
            E("vector", "scalar_tensor_tensor", otld[:, hp * 512:(hp + 1) * 512],
              g_tm[:, hp * 512:(hp + 1) * 512], rstdv[:, h:h + 1],
              ggb[:, h * 512:(h + 1) * 512], ALU.mult, ALU.mult, inc=(vs, 1))
            og_tok[(n, h)] = vs.n

        oT_grp = {}

        def emit_oT_pe(n, h):
            if h < 0:
                return
            hp = h % 2
            g2 = tr_group()
            oT_grp[(n, h)] = g2
            wait_t("tensor", trg_ev[g2])
            wait("tensor", vs, og_tok[(n, h)])
            for jj in range(4):
                icc = (ts, 1) if jj == 3 else None
                E("tensor", "transpose", p_tr[g2][:, jj * 128:(jj + 1) * 128],
                  otld[:, hp * 512 + jj * 128: hp * 512 + (jj + 1) * 128], idn[:, :], inc=icc)
            oT_tok[(n, h)] = ts.n

        def emit_ofm(n, h):
            if h < 0:
                return
            g2 = oT_grp[(n, h)]
            wait("scalar", ts, oT_tok[(n, h)])
            E("scalar", "copy", o_fm[:, h * 512:(h + 1) * 512], p_tr[g2][:, 0:512], inc=(as_, 1))
            ofm_tok_d[(n, h)] = as_.n
            trg_ev[g2] = (as_, as_.n)

        def emit_ln_stats(tile):
            nonlocal lnstats_tok
            par = tile % 2
            wait_t("vector", xtok[tile])
            wait("vector", osA, outdma_prev)
            E("vector", "tensor_scalar", out_sb[:, :], xb[par][:, :], 1.0, None,
              ALU.mult, ALU.add, accum_out=xsum[:, :])
            E("vector", "scalar_tensor_tensor", out_sb[:, :], xb[par][:, :], 1.0,
              xb[par][:, :], ALU.mult, ALU.mult, accum_out=xsq[:, :], inc=(vs, 1))
            lnstats_tok = vs.n

        def emit_ln_rstd():
            nonlocal rstdln_tok, lnvar_tok
            wait("scalar", vs, lnstats_tok)
            E("scalar", "activation", tmp1[:, :], xsum[:, :], AF.Square, bias=0.0,
              scale=1.0 / D, inc=(as_, 1))
            t_msq = as_.n
            wait("vector", as_, t_msq)
            E("vector", "tensor_scalar_mul", meanv[:, :], xsum[:, :], 1.0 / D)
            E("vector", "scalar_tensor_tensor", varv[:, :], xsq[:, :], 1.0 / D,
              tmp1[:, :], ALU.mult, ALU.subtract, inc=(vs, 1))
            lnvar_tok = vs.n
            wait("scalar", vs, lnvar_tok)
            E("scalar", "activation", rstdln[:, :], varv[:, :], AF.Ln, bias=cst[:, 0:1],
              scale=1.0, inc=(as_, 1))
            wait("scalar", as_, as_.n)
            E("scalar", "activation", rstdln[:, :], rstdln[:, :], AF.Exp, bias=0.0,
              scale=-0.5, inc=(as_, 1))
            rstdln_tok = as_.n

        def emit_ln_apply(tile):
            nonlocal xtm_tok
            par = tile % 2
            wait("vector", as_, rstdln_tok)
            E("vector", "tensor_scalar", xn_tm[:, :], xb[par][:, :], meanv[:, :],
              rstdln[:, :], ALU.subtract, ALU.mult, inc=(vs, 1))
            xtm_tok = vs.n

        emit_ln_stats(0)
        emit_ln_rstd()
        emit_ln_apply(0)

        for n in range(TILES):
            seq, r0 = n // TPS, (n % TPS) * C
            par = n % 2
            pn = n - 1

            if n == TPS:
                E("vector", "memset", Ssb[:, :], 0.0)

            if n + 1 < TILES:
                sq1, rr1 = (n + 1) // TPS, ((n + 1) % TPS) * C
                wait("sync", vs, outev_prev)
                xp = (n + 1) % 2
                E("sync", "dma_start", xb[xp][:, :], x_d[sq1, rr1:rr1 + C, :], inc=(xs[xp], 16))
                xtok[n + 1] = (xs[xp], xs[xp].n)

            g0 = tr_group()
            wait_t("tensor", trg_ev[g0])
            wait("tensor", vs, xtm_tok)
            for kk in range(8):
                icc = (ts, 1) if kk == 7 else None
                E("tensor", "transpose", p_tr[g0][:, kk * 128:(kk + 1) * 128],
                  xn_tm[:, kk * 128:(kk + 1) * 128], idn[:, :], inc=icc)
            xnT_tok = ts.n

            wait("scalar", ts, xnT_tok)
            wait("scalar", ss, stored_prev)
            E("scalar", "copy", xn_fm[:, :], p_tr[g0][:, :], inc=(as_, 1))
            xfm_tok = as_.n
            trg_ev[g0] = (as_, as_.n)

            wait("gpsimd", as_, xfm_tok)
            E("gpsimd", "dma_start", xns_d[n, :, :], xn_fm[:, :], inc=(ss, 16))
            stored_n = ss.n

            qk_banks = [p_sc, p_ds, p_pa, p_pb]
            qk_free = [shat_h3_prev, supd_h3_prev, supd_h3_prev, gev_pb_prev]
            evtoks = {}
            x1ev_tok = 0
            wait("tensor", as_, xfm_tok)
            if n == 0:
                wait("tensor", vs, wcast)
                tap("wq", wq[:, :], vs, wcast)
                tap("wk", wk[:, :], vs, wcast)
                tap("wv", wv[:, :], vs, wcast)
                tap("wg", wg[:, :], vs, wcast)
                tap("wo", wo[:, :], vs, wcast)
                tap("gnwbs", gnwb[:, :], vs, wcast)
            for gi, (name, wsb, dst, off, fcs) in enumerate((
                    ("qA", wq, q_fm, 0, range(0, 4)),
                    ("qB", wq, q_fm, 512, range(4, 8)),
                    ("kA", wk, k_fm, 0, range(0, 4)),
                    ("kB", wk, k_fm, 512, range(4, 8)))):
                bank = qk_banks[gi]
                wait("tensor", vs, qk_free[gi])
                for fi, fc in enumerate(fcs):
                    for dc in range(8):
                        icc = (ts, 1) if (fi == 3 and dc == 7) else None
                        E("tensor", "matmul", bank[:, (fc % 4) * 128:(fc % 4 + 1) * 128],
                          wsb[:, dc * 1024 + fc * 128: dc * 1024 + (fc + 1) * 128],
                          xn_fm[:, dc * 128:(dc + 1) * 128],
                          start=(dc == 0), stop=(dc == 7), inc=icc)
                wait("scalar", ts, ts.n)
                if gi == 0:
                    wait("scalar", ts, oh_tok.get((pn, 3), 0))
                E("scalar", "copy", dst[:, off:off + 512], bank[:, :], inc=(as_, 1))
                evtoks[name] = as_.n
                if gi == 0:
                    wait("tensor", vs, outev_prev)
                    for dc in range(8):
                        icc = (ts, 1) if dc == 7 else None
                        E("tensor", "matmul", p_o0[0:16, 0:128], wgk1[:, dc * 16:(dc + 1) * 16],
                          xn_fm[:, dc * 128:(dc + 1) * 128], start=(dc == 0), stop=(dc == 7), inc=icc)
                    x1_tok = ts.n
                    wait("scalar", ts, x1_tok)
                    wait("scalar", ts, zhi_prev)
                    E("scalar", "copy", x1_sb[:, :], p_o0[0:16, 0:128], inc=(as_, 1))
                    x1ev_tok = as_.n
                if gi == 1:
                    wait("tensor", as_, x1ev_tok)
                    for fc in range(8):
                        bank2 = p_o0 if fc < 4 else p_o1
                        icc = (ts, 1) if fc == 7 else None
                        E("tensor", "matmul", bank2[:, (fc % 4) * 128:(fc % 4 + 1) * 128],
                          wgk2[:, fc * 128:(fc + 1) * 128], x1_sb[:, :], start=True, stop=True, inc=icc)
                    z_tok = ts.n
                    zhi_prev = z_tok

            wait("scalar", ts, z_tok)
            wait("scalar", vs, qt_prev)
            for ch in range(8):
                bank = p_o0 if ch < 4 else p_o1
                E("scalar", "activation", EkS[:, ch * 128:(ch + 1) * 128],
                  bank[:, (ch % 4) * 128:(ch % 4 + 1) * 128], AF.Exp,
                  bias=negb[:, ch:ch + 1], scale=-1.0)
            E("scalar", "activation", D_bf[:, :], EkS[:, :], AF.Ln, bias=1.0, scale=1.0, inc=(as_, 1))
            gates_tok = as_.n

            vev_toks = [0] * 4
            gev_toks = [0] * 4
            for gi in range(8):
                wsb = wg if gi < 4 else wv
                grp = gi % 4
                bank = qk_banks[grp]
                if gi < 4:
                    wait("tensor", as_, evtoks[("qA", "qB", "kA", "kB")[gi]])
                else:
                    wait("tensor", vs, gev_toks[grp])
                for dc in range(8):
                    icc = (ts, 1) if dc == 7 else None
                    E("tensor", "matmul", bank[:, :], xn_fm[:, dc * 128:(dc + 1) * 128],
                      wsb[:, dc * 2048 + grp * 512: dc * 2048 + (grp + 1) * 512],
                      start=(dc == 0), stop=(dc == 7), inc=icc)
                vg_w = ts.n
                if gi >= 4:
                    wait("vector", ts, vg_w)
                    if gi == 4:
                        wait("vector", ts, ds_tok.get((pn, 3), 0))
                    E("vector", "tensor_copy", v_tm[:, grp * 512:(grp + 1) * 512], bank[:, :], inc=(vs, 1))
                    vev_toks[grp] = vs.n
                else:
                    wait("vector", ts, vg_w)
                    if gi == 0:
                        wait("vector", as_, sq_tok.get((pn, 3), 0))
                        wait("vector", gs, fold_half_prev[0])
                    if gi == 2:
                        wait("vector", gs, fold_half_prev[1])
                    E("vector", "tensor_copy", g_tm[:, grp * 512:(grp + 1) * 512], bank[:, :], inc=(vs, 1))
                    gev_toks[grp] = vs.n
                if gi in (2, 4):
                    hh = 0 if gi == 2 else 1
                    lo = hh * 1024
                    wait("scalar", vs, gev_toks[2 * hh + 1])
                    wait("scalar", gs, fold_half_prev[hh])
                    E("scalar", "activation", sgA[:, lo:lo + 1024],
                      g_tm[:, lo:lo + 1024], AF.Exp, bias=0.0, scale=-1.0)
                    E("scalar", "activation", sgA[:, lo:lo + 1024],
                      sgA[:, lo:lo + 1024], AF.Ln, bias=1.0, scale=1.0)
                    E("scalar", "activation", sgA[:, lo:lo + 1024],
                      sgA[:, lo:lo + 1024], AF.Exp, bias=0.0, scale=-1.0, inc=(as_, 1))
                    sig_half = as_.n
                    wait("gpsimd", vs, gev_toks[2 * hh + 1])
                    wait("gpsimd", vs, og_tok.get((pn, 2 * hh + 1), 0))
                    E("gpsimd", "tensor_tensor", ggb[:, lo:lo + 1024],
                      g_tm[:, lo:lo + 1024], gnwb[:, lo:lo + 1024], ALU.mult)
                    wait("gpsimd", as_, sig_half)
                    E("gpsimd", "tensor_tensor", ggb[:, lo:lo + 1024],
                      ggb[:, lo:lo + 1024], sgA[:, lo:lo + 1024], ALU.mult, inc=(gs, 1))
                    fold_toks[hh] = gs.n
                    fold_half_prev[hh] = gs.n
                if gi == 1:
                    wait("vector", as_, gates_tok)
                    wait("vector", as_, eqek_prev)
                    for ch in range(8):
                        icc = (vs, 1) if ch == 7 else None
                        E("vector", "tensor_tensor_scan", EqS[:, ch * 128:(ch + 1) * 128],
                          D_bf[:, ch * 128:(ch + 1) * 128], zbf[:, :], 0.0, ALU.add, ALU.add, inc=icc)
                    for ch in range(8):
                        icc = (vs, 1) if ch == 7 else None
                        E("vector", "tensor_copy", cendv[:, ch:ch + 1],
                          EqS[:, ch * 128 + 127: ch * 128 + 128], inc=icc)
                    scan_tok = vs.n
                    wait("scalar", vs, scan_tok)
                    E("scalar", "activation", lamv[:, :], cendv[:, :], AF.Exp, bias=0.0, scale=-1.0 / 16.0)
                    E("scalar", "activation", D_bf[:, :], EqS[:, :], AF.Exp, bias=cst[:, 1:2], scale=-1.0 / 16.0)
                    E("scalar", "activation", EkS[:, :], EqS[:, :], AF.Exp, bias=0.0, scale=1.0 / 16.0, inc=(as_, 1))
                    eqek_tok = as_.n
                    eqek_prev = eqek_tok
                if gi == 3:
                    wait("vector", ts, oh_tok.get((pn, 3), 0))
                    E("vector", "tensor_copy", sbf[:, 0:1024], Ssb[:, 0:1024])
                    E("vector", "tensor_copy", sbf[:, 1024:2048], Ssb[:, 1024:2048], inc=(vs, 1))
                    sbf_toks = {0: (vs, vs.n), 1: (vs, vs.n)}
                    wait("vector", as_, eqek_tok)
                    E("vector", "tensor_mul", q_fm[:, :], q_fm[:, :], D_bf[:, :])
                    E("vector", "tensor_mul", k_fm[:, :], k_fm[:, :], EkS[:, :])
                    for ch in range(8):
                        icc = (vs, 1) if ch == 7 else None
                        E("vector", "tensor_scalar_mul", D_bf[:, ch * 128:(ch + 1) * 128],
                          k_fm[:, ch * 128:(ch + 1) * 128], lamv[:, ch:ch + 1], inc=icc)
                    qt_tok = vs.n
                    if n == 0:
                        tap("lamv0", lamv[:, :], vs, qt_tok, blocker="scalar")
                        tap("cendv0", cendv[:, :], vs, qt_tok, blocker="scalar")
                        tap("kt0", k_fm[:, :], vs, qt_tok, blocker="tensor")
                    if n == 1:
                        tap("Ssb1", Ssb[:, :], vs, sbf_toks[0][1], blocker="vector")
                        tap("sbf1", sbf[:, :], vs, sbf_toks[0][1], blocker="vector")
                        tap("qt1", q_fm[:, :], vs, qt_tok, blocker="tensor")
                        tap("kt1", k_fm[:, :], vs, qt_tok, blocker="tensor")
                        tap("lamv1", lamv[:, :], vs, qt_tok, blocker="scalar")
                        tap("cum1", EqS[:, :], vs, qt_tok, blocker="scalar")
            sc_free = vev_toks[0]
            supd_last = supd_h3_prev
            khtm_tok = 0
            for h in range(HD):
                hp = h % 2
                c0, c1 = 2 * h, 2 * h + 1
                kprev = (n, h - 2) if h >= 2 else (pn, h + 2)

                wait("tensor", vs, qt_tok)
                wait("tensor", vs, sc_free)
                for ic in range(2):
                    cc_ = (c0, c1)[ic]
                    icc = (ts, 1) if ic == 1 else None
                    E("tensor", "matmul", p_sc[:, 0:128], k_fm[:, cc_ * 128:(cc_ + 1) * 128],
                      q_fm[:, cc_ * 128:(cc_ + 1) * 128], start=(ic == 0), stop=(ic == 1), inc=icc)
                sc_tok[(n, h)] = ts.n

                if h == 0:
                    g1 = tr_group()
                    wait_t("tensor", trg_ev[g1])
                    for cc_ in range(8):
                        icc = (ts, 1) if cc_ == 7 else None
                        E("tensor", "transpose", p_tr[g1][:, cc_ * 128:(cc_ + 1) * 128],
                          D_bf[:, cc_ * 128:(cc_ + 1) * 128], idn[:, :], inc=icc)
                    khT_tok = ts.n

                wait("vector", ts, sc_tok[(n, h)])
                wait("vector", ts, oh_tok.get(kprev, 0))
                E("vector", "tensor_tensor", shat[:, hp * 128:(hp + 1) * 128],
                  p_sc[:, 0:128], maskT[:, :], ALU.mult, inc=(vs, 1))
                shat_tok_d[(n, h)] = vs.n
                sc_free = vs.n
                if n == 1 and h == 0:
                    tap("shat1", shat[:, 0:128], vs, vs.n, blocker="tensor")
                if h == 0:
                    wait("vector", ts, khT_tok)
                    E("vector", "tensor_copy", khtm[:, :], p_tr[g1][:, :], inc=(vs, 1))
                    khtm_tok = vs.n
                    trg_ev[g1] = (vs, vs.n)
                    if n == 0:
                        tap("khtm0", khtm[:, :], vs, khtm_tok, blocker="tensor")

                wait_t("tensor", sbf_toks[hp])
                if h < 2:
                    wait("tensor", as_, gates_tok)
                wait("tensor", vs, ocp_tok.get(kprev, 0))
                for ic in range(2):
                    cc_ = (c0, c1)[ic]
                    E("tensor", "matmul", p_o[hp][:, :], q_fm[:, cc_ * 128:(cc_ + 1) * 128],
                      sbf[:, hp * 1024 + ic * 512: hp * 1024 + (ic + 1) * 512],
                      start=(ic == 0), stop=False)
                wait("tensor", vs, shat_tok_d[(n, h)])
                E("tensor", "matmul", p_o[hp][:, :], shat[:, hp * 128:(hp + 1) * 128],
                  v_tm[:, h * 512:(h + 1) * 512], start=False, stop=True, inc=(ts, 1))
                oh_tok[(n, h)] = ts.n

                wait("vector", ts, oh_tok[(n, h)])
                if h == 0:
                    wait("vector", gs, fold_toks[0])
                if h >= 2:
                    wait("vector", as_, sqr_tok[(n, h - 2)])
                E("vector", "tensor_copy", g_tm[:, hp * 512:(hp + 1) * 512],
                  p_o[hp][:, :], inc=(vs, 1))
                ocp_tok[(n, h)] = vs.n


                wait("scalar", vs, ocp_tok[(n, h)])
                E("scalar", "activation", sgB[:, :], g_tm[:, hp * 512:(hp + 1) * 512],
                  AF.Square, accum_out=ssumv[:, h:h + 1], inc=(as_, 1))
                sqr_tok[(n, h)] = as_.n
                wait("scalar", as_, as_.n)
                E("scalar", "activation", rstdv[:, h:h + 1], ssumv[:, h:h + 1],
                  AF.Ln, bias=cst[:, 0:1], scale=1.0 / DV, inc=(as_, 1))
                wait("scalar", as_, as_.n)
                E("scalar", "activation", rstdv[:, h:h + 1], rstdv[:, h:h + 1],
                  AF.Exp, bias=0.0, scale=-0.5, inc=(as_, 1))
                sq_tok[(n, h)] = as_.n

                if h < 2:
                    wait("scalar", ts, oh_tok[(n, h)])
                    E("scalar", "copy", sbf[:, hp * 1024:(hp + 1) * 1024],
                      Ssb[:, (h + 2) * 1024:(h + 3) * 1024], inc=(as_, 1))
                    sbf_toks[hp] = (as_, as_.n)
                    scast_tok[(n, h + 2)] = as_.n

                wait("tensor", vs, khtm_tok)
                wait("tensor", vs, supd_last)
                if h == 0:
                    wait("tensor", vs, vev_toks[1])
                    wait("tensor", vs, vev_toks[2])
                E("tensor", "matmul", p_ds[:, :], khtm[:, h * 256: h * 256 + 128],
                  v_tm[:, h * 512:(h + 1) * 512], start=True, stop=True)
                E("tensor", "matmul", p_pa[:, :], khtm[:, h * 256 + 128: h * 256 + 256],
                  v_tm[:, h * 512:(h + 1) * 512], start=True, stop=True, inc=(ts, 1))
                ds_tok[(n, h)] = ts.n

                wait("vector", ts, ds_tok[(n, h)])
                if h >= 2:
                    wait("vector", as_, scast_tok[(n, h)])
                E("vector", "scalar_tensor_tensor", Ssb[:, c0 * 512:(c0 + 1) * 512],
                  Ssb[:, c0 * 512:(c0 + 1) * 512], lamv[:, c0:c0 + 1], p_ds[:, :],
                  ALU.mult, ALU.add)
                E("vector", "scalar_tensor_tensor", Ssb[:, c1 * 512:(c1 + 1) * 512],
                  Ssb[:, c1 * 512:(c1 + 1) * 512], lamv[:, c1:c1 + 1], p_pa[:, :],
                  ALU.mult, ALU.add, inc=(vs, 1))
                supd_tok_d[(n, h)] = vs.n
                supd_last = vs.n

                emit_gate(n, h - 1, fold_toks[0 if h - 1 <= 1 else 1])
                emit_oT_pe(n, h - 2)
                emit_ofm(n, h - 2)

            emit_gate(n, 3, fold_toks[1])
            emit_oT_pe(n, 2)
            emit_ofm(n, 2)
            emit_oT_pe(n, 3)
            emit_ofm(n, 3)
            if n + 1 < TILES:
                emit_ln_stats(n + 1)
                emit_ln_rstd()
                wait("vector", ts, xnT_tok)
                emit_ln_apply(n + 1)

            wait("tensor", vs, ocp_tok[(n, 2)])
            for jc in range(16):
                if jc % 4 == 0:
                    wait("tensor", as_, ofm_tok_d[(n, jc // 4)])
                icc = (ts, 1) if jc == 15 else None
                E("tensor", "matmul", p_o0[:, :], o_fm[:, jc * 128:(jc + 1) * 128],
                  wo[:, jc * 1024 + 0: jc * 1024 + 512],
                  start=(jc == 0), stop=(jc == 15), inc=icc)
            wait("tensor", vs, ocp_tok[(n, 3)])
            for jc in range(16):
                icc = (ts, 1) if jc == 15 else None
                E("tensor", "matmul", p_o1[:, :], o_fm[:, jc * 128:(jc + 1) * 128],
                  wo[:, jc * 1024 + 512: jc * 1024 + 1024],
                  start=(jc == 0), stop=(jc == 15), inc=icc)
            ao_tok = ts.n

            wait("vector", ts, ao_tok)
            wait("vector", osA, outdma_prev)
            E("vector", "tensor_tensor", out_sb[:, 0:512], p_o0[:, :], xb[par][:, 0:512], ALU.add)
            E("vector", "tensor_tensor", out_sb[:, 512:1024], p_o1[:, :], xb[par][:, 512:1024],
              ALU.add, inc=(vs, 1))
            outev_tok = vs.n
            if n in (0, 1):
                tap("outA" if n == 0 else "outA1", out_sb[:, :], vs, outev_tok, blocker="gpsimd")

            wait("gpsimd", vs, outev_tok)
            E("gpsimd", "dma_start", out_d[seq, r0:r0 + C, :], out_sb[:, :], inc=(osA, 16))
            outdma_prev = osA.n

            stored_prev = stored_n
            outev_prev = outev_tok
            shat_h3_prev = shat_tok_d[(n, 3)]
            supd_h3_prev = supd_tok_d[(n, 3)]
            gev_pb_prev = vev_toks[3]
            qt_prev = qt_tok
            VEND, AEND, TEND = vs.n, as_.n, ts.n

        phA.close()
        phB = ExitStack()
        hm = sb(phB, "hm_s", [128, 8192], BF16)
        wfT = sb(phB, "wfT", [128, 8192], BF16)
        stgB = [sb(phB, "stgB0", [128, 1024], F32), sb(phB, "stgB1", [128, 1024], F32)]
        xnb = [sb(phB, f"xnb{i}", [128, D], BF16) for i in range(4)]
        outrd = [sb(phB, f"outrd{i}", [128, D], F32) for i in range(4)]
        xr_f = [sb(phB, "xr_f0", [128, D], F32), sb(phB, "xr_f1", [128, D], F32)]
        xq_tm = [sb(phB, "xq_tm0", [128, D], BF16), sb(phB, "xq_tm1", [128, D], BF16)]
        xq_fm = [sb(phB, "xq_fm0", [128, D], BF16), sb(phB, "xq_fm1", [128, D], BF16)]
        out_sb2 = sb(phB, "out_sb2", [128, D], F32)
        wrow = sb(phB, "wrow", [128, D], BF16)
        asum = sb(phB, "asum", [128, 1], F32)
        sv = sb(phB, "sv", [128, 1], F32)
        sinv = sb(phB, "sinv", [128, 1], F32)
        mx0 = sb(phB, "mx0", [128, 2], F32)
        mx1 = sb(phB, "mx1", [128, 2], F32)
        mxc = sb(phB, "mxc", [128, 2], F32)
        r7 = sb(phB, "r7", [128, 2], F32)
        s7 = sb(phB, "s7", [128, 2], F32)
        remB = nc.sbuf_bytes_remaining
        print("SBUF remB:", remB)

        wait("sync", ts, TEND); wait("sync", vs, VEND); wait("sync", as_, AEND)
        wait("sync", osA, osA.n); wait("sync", ss, ss.n)

        wfT_r = wfT[:, :].rearrange("p (a b) -> p a b", b=1024)
        ptr_r = p_tr0[:, :].rearrange("p (a b) -> p a b", b=128)
        wrow_toks = {}
        wfT_ev = 0
        for oc in range(8):
            if oc >= 2:
                wait("sync", vs, wrow_toks[oc - 2])
            E("sync", "dma_start", stgB[oc % 2][:, :], wffn_d[oc * 128:(oc + 1) * 128, :], inc=(wfp[oc % 2], 16))
            wait("vector", wfp[oc % 2], wfp[oc % 2].n)
            if oc == 0:
                wait("vector", ts, TEND)
                wait("vector", as_, AEND)
            E("vector", "tensor_reduce", asum[:, :], stgB[oc % 2][:, :], AX, ALU.add,
              apply_absolute_value=True, inc=(vs, 1))
            wait("vector", vs, vs.n)
            E("vector", "tensor_scalar", sv[:, :], asum[:, :], 1.0 / D, 1e-5,
              ALU.mult, ALU.max, inc=(vs, 1))
            wait("scalar", vs, vs.n)
            if oc == 0:
                wait("scalar", ts, TEND)
            E("scalar", "activation", sinv[:, :], sv[:, :], AF.Ln, bias=0.0, scale=1.0, inc=(as_, 1))
            wait("scalar", as_, as_.n)
            E("scalar", "activation", sinv[:, :], sinv[:, :], AF.Exp, bias=0.0, scale=-1.0, inc=(as_, 1))
            wait("vector", as_, as_.n)
            E("vector", "tensor_scalar", xr_f[0][:, :], stgB[oc % 2][:, :], sinv[:, :], MAGIC,
              ALU.mult, ALU.add)
            E("vector", "tensor_scalar", xr_f[0][:, :], xr_f[0][:, :], MAGIC, -1.0,
              ALU.subtract, ALU.max)
            E("vector", "tensor_scalar", wrow[:, :], xr_f[0][:, :], 1.0, sv[:, :],
              ALU.min, ALU.mult, inc=(vs, 1))
            wrow_toks[oc] = vs.n
            wait("tensor", vs, vs.n)
            if oc == 0:
                wait_t("tensor", trg_ev[0])
            else:
                wait("tensor", vs, wfT_ev)
            for ic in range(8):
                icc = (ts, 1) if ic == 7 else None
                E("tensor", "transpose", p_tr0[:, ic * 128:(ic + 1) * 128],
                  wrow[:, ic * 128:(ic + 1) * 128], idn[:, :], inc=icc)
            wait("vector", ts, ts.n)
            E("vector", "tensor_copy", wfT_r[:, :, oc * 128:(oc + 1) * 128], ptr_r[:, :, :], inc=(vs, 1))
            wfT_ev = vs.n
        for dc in range(8):
            E("sync", "dma_start", hm[:, dc * 1024:(dc + 1) * 1024],
              hmat_d[dc * 128:(dc + 1) * 128, :], inc=(hmS, 16))
        wprep_bs = hmS.n
        wprep_vs = vs.n
        tap("wfT", wfT[:, :], vs, wfT_ev, blocker="tensor")
        tap("hm", hm[:, :], hmS, wprep_bs, blocker="tensor")

        xh_tok = {}; ff_tok = {}; qz_tok = {}; outevB = {}
        xtokB = {}; otokB = {}
        outdmaB_par = [0, 0]
        xqfm_ev_par = [0, 0]
        out_sbB = [out_sb, out_sb2]

        def emit_b_tail(t):
            if t < 0:
                return
            par = t % 2
            seq, r0 = t // TPS, (t % TPS) * C
            trb = p_tr[par]
            wait("tensor", vs, qz_tok[t])
            if t >= 2:
                wait("tensor", vs, xqfm_ev_par[par])
            elif t == 0:
                wait("tensor", vs, wfT_ev)
            for ic in range(8):
                icc = (ts, 1) if ic == 7 else None
                E("tensor", "transpose", trb[:, ic * 128:(ic + 1) * 128],
                  xq_tm[par][:, ic * 128:(ic + 1) * 128], idn[:, :], inc=icc)
            xqT = ts.n
            wait("vector", ts, xqT)
            E("vector", "tensor_copy", xq_fm[par][:, :], trb[:, :], inc=(vs, 1))
            xqfm_ev_par[par] = vs.n
            wait("tensor", vs, xqfm_ev_par[par])
            if t >= 1:
                wait("tensor", vs, outevB[t - 1])
            for half, bank in ((0, p_o0), (1, p_o1)):
                for ic in range(8):
                    icc = (ts, 1) if (half == 1 and ic == 7) else None
                    E("tensor", "matmul", bank[:, :], xq_fm[par][:, ic * 128:(ic + 1) * 128],
                      wfT[:, ic * 1024 + half * 512: ic * 1024 + (half + 1) * 512],
                      start=(ic == 0), stop=(ic == 7), inc=icc)
            ff_tok[t] = ts.n

        def emit_b_out(t):
            if t < 0:
                return
            par = t % 2
            seq, r0 = t // TPS, (t % TPS) * C
            wait("vector", ts, ff_tok[t])
            wait_t("vector", otokB[t])
            wait("vector", osB, outdmaB_par[par])
            if t == 0:
                wait("vector", osA, osA.n)
            E("vector", "tensor_tensor", out_sbB[par][:, 0:512], p_o0[:, :], outrd[t % 4][:, 0:512], ALU.add)
            E("vector", "tensor_tensor", out_sbB[par][:, 512:1024], p_o1[:, :], outrd[t % 4][:, 512:1024],
              ALU.add, inc=(vs, 1))
            outevB[t] = vs.n
            if t == 0:
                tap("outB", out_sbB[par][:, :], vs, outevB[t], blocker="gpsimd")
            wait("gpsimd", vs, outevB[t])
            E("gpsimd", "dma_start", out_d[seq, r0:r0 + C, :], out_sbB[par][:, :], inc=(osB, 16))
            outdmaB_par[par] = osB.n

        def emit_b_load(t):
            s4, sp = t % 4, t % 2
            seq2, r02 = t // TPS, (t % TPS) * C
            if t >= 2:
                wait("sync", bxn[sp], xtokB[t - 2][1])
                wait("sync", bor[sp], otokB[t - 2][1])
            if t >= 4:
                wait("sync", ts, xh_tok[t - 4])
                wait("sync", vs, outevB[t - 4])
            E("sync", "dma_start", xnb[s4][:, :], xns_d[t, :, :], inc=(bxn[sp], 16))
            xtokB[t] = (bxn[sp], bxn[sp].n)
            E("sync", "dma_start", outrd[s4][:, :], out_d[seq2, r02:r02 + C, :], inc=(bor[sp], 16))
            otokB[t] = (bor[sp], bor[sp].n)

        emit_b_load(0)
        emit_b_load(1)
        for t in range(TILES):
            seq, r0 = t // TPS, (t % TPS) * C
            par = t % 2
            banksA = [(p_pa, p_pb), (p_sc, p_ds)][par]
            if t + 2 < TILES:
                emit_b_load(t + 2)

            wait_t("tensor", xtokB[t])
            if t == 0:
                wait("tensor", hmS, wprep_bs)
                wait("tensor", vs, wprep_vs)
                wait("tensor", as_, AEND)
            elif t == 1:
                wait("tensor", vs, wprep_vs)
            else:
                wait("tensor", vs, qz_tok[t - 2])
            for half in (0, 1):
                bank = banksA[half]
                for dc in range(8):
                    icc = (ts, 1) if (half == 1 and dc == 7) else None
                    E("tensor", "matmul", bank[:, :], xnb[t % 4][:, dc * 128:(dc + 1) * 128],
                      hm[:, dc * 1024 + half * 512: dc * 1024 + (half + 1) * 512],
                      start=(dc == 0), stop=(dc == 7), inc=icc)
            xh_tok[t] = ts.n

            emit_b_tail(t - 1)

            wait("vector", ts, xh_tok[t])
            E("vector", "reduce_max", mx0[:, par:par + 1], banksA[0][:, :], AX, apply_absolute_value=True)
            E("vector", "reduce_max", mx1[:, par:par + 1], banksA[1][:, :], AX, apply_absolute_value=True, inc=(vs, 1))
            wait("vector", vs, vs.n)
            E("vector", "scalar_tensor_tensor", mxc[:, par:par + 1], mx0[:, par:par + 1], 1e-5,
              mx1[:, par:par + 1], ALU.max, ALU.max, inc=(vs, 1))
            wait("vector", vs, vs.n)
            E("vector", "reciprocal", r7[:, par:par + 1], mxc[:, par:par + 1], inc=(vs, 1))
            wait("vector", vs, vs.n)
            E("vector", "tensor_scalar_mul", s7[:, par:par + 1], mxc[:, par:par + 1], 1.0 / 7.0)
            E("vector", "tensor_scalar_mul", r7[:, par:par + 1], r7[:, par:par + 1], 7.0, inc=(vs, 1))
            wait("vector", vs, vs.n)
            E("vector", "tensor_scalar", xr_f[par][:, 0:512], banksA[0][:, :], r7[:, par:par + 1], MAGIC,
              ALU.mult, ALU.add)
            E("vector", "tensor_scalar", xr_f[par][:, 512:1024], banksA[1][:, :], r7[:, par:par + 1], MAGIC,
              ALU.mult, ALU.add)
            E("vector", "tensor_scalar", xq_tm[par][:, :], xr_f[par][:, :], MAGIC, s7[:, par:par + 1],
              ALU.subtract, ALU.mult, inc=(vs, 1))
            qz_tok[t] = vs.n

            emit_b_out(t - 1)

        emit_b_tail(TILES - 1)
        emit_b_out(TILES - 1)

        wait("gpsimd", osB, osB.n)
        phB.close()

        for s in (xs[0], xs[1], ws2[0], ws2[1], csem, ss, osA, osB,
                  bxn[0], bxn[1], bor[0], bor[1], wfp[0], wfp[1], hmS,
                  ts, vs, as_, gs, dbg_state["sem"]):
            nc.gpsimd.sem_clear(s.h)
        nc.all_engine_barrier()
        with nc.Block() as block:
            @block.sync
            def _(eng):
                for f in S_["sync"]:
                    f(eng)

            @block.tensor
            def _(eng):
                for f in S_["tensor"]:
                    f(eng)

            @block.vector
            def _(eng):
                for f in S_["vector"]:
                    f(eng)

            @block.scalar
            def _(eng):
                for f in S_["scalar"]:
                    f(eng)

            @block.gpsimd
            def _(eng):
                for f in S_["gpsimd"]:
                    f(eng)

    nc.reset()
    return nc


_CACHE = {}


def _hadamard_bf16():
    h = np.array([[1.0]], dtype=np.float64)
    while h.shape[0] < 1024:
        h = np.block([[h, h], [h, -h]])
    return np.ascontiguousarray((h / 32.0).astype(ml_dtypes.bfloat16))


def kernel(L0_attn_x, L0_attn_Wq, L0_attn_Wk, L0_attn_Wv, L0_attn_Wg,
           L0_attn_Wgk1, L0_attn_Wgk2, L0_attn_bgk2, L0_attn_Wo,
           L0_attn_g_norm_weight, L0_ffn_weight):
    x = np.ascontiguousarray(np.asarray(L0_attn_x, dtype=np.float32))
    common = {
        "wq": np.ascontiguousarray(np.asarray(L0_attn_Wq, np.float32)),
        "wk": np.ascontiguousarray(np.asarray(L0_attn_Wk, np.float32)),
        "wv": np.ascontiguousarray(np.asarray(L0_attn_Wv, np.float32)),
        "wg": np.ascontiguousarray(np.asarray(L0_attn_Wg, np.float32)),
        "wgk1": np.ascontiguousarray(np.asarray(L0_attn_Wgk1, np.float32)),
        "wgk2": np.ascontiguousarray(np.asarray(L0_attn_Wgk2, np.float32)),
        "wo": np.ascontiguousarray(np.asarray(L0_attn_Wo, np.float32)),
        "wffn": np.ascontiguousarray(np.asarray(L0_ffn_weight, np.float32)),
        "bgk2c": np.ascontiguousarray(np.asarray(L0_attn_bgk2, np.float32).reshape(8, 128).T),
        "gnwb": np.ascontiguousarray(np.broadcast_to(
            np.tile(np.asarray(L0_attn_g_norm_weight, np.float32), 4)[None, :], (128, 2048))),
        "hmat": _hadamard_bf16(),
        "idn": np.ascontiguousarray(np.eye(128, dtype=ml_dtypes.bfloat16)),
        "maskT": np.ascontiguousarray(
            np.triu(np.ones((128, 128), np.float32)).astype(ml_dtypes.bfloat16)),
    }
    if "nc" not in _CACHE:
        _CACHE["nc"] = build_program()
    nc = _CACHE["nc"]
    in_maps = []
    for c in range(8):
        m = dict(common)
        m["x"] = np.ascontiguousarray(x[2 * c:2 * c + 2])
        in_maps.append(m)
    res = run_bass_kernel_spmd(nc, in_maps, core_ids=list(range(8)))
    return np.concatenate([np.asarray(r["out"], dtype=np.float32) for r in res.results], axis=0)
